# version 01
# baseline (speedup 1.0000x reference)
import numpy as np

import concourse.bass as bass
import concourse.mybir as mybir
from concourse.bass_utils import run_bass_kernel_spmd

F32 = mybir.dt.float32
AX = mybir.AxisListType
ALU = mybir.AluOpType
ACT = mybir.ActivationFunctionType

_DEBUG_OUT = [False]
T = 4096
D = 1024
DI = 2048
NS = 16
NG = 8
GT = T // NG
NSUB = GT // 128
EPS = 1e-5
TS_SCAN = 32
NCHUNK = T // TS_SCAN


class Sem:
    def __init__(self, h):
        self.h = h
        self.v = 0


class Build:

    def __init__(self, nc, n_sublayers=6):
        self.nc = nc
        self.n_sublayers = n_sublayers
        self.phase_idx = 0

    def inc(self, inst, sem, n=1):
        inst.then_inc(sem.h, n)
        sem.v += n

    def raw_inc(self, eng, sem, n=1):
        eng.drain()
        eng.sem_inc(sem.h, n)
        sem.v += n

    def red(self, vector, out, in_, op, absval=False):
        vector.drain()
        vector.tensor_reduce(out=out, in_=in_, axis=AX.X, op=op,
                             apply_absolute_value=absval)
        vector.drain()

    def recip(self, vector, out, in_):
        vector.drain()
        vector.reciprocal(out=out, in_=in_)
        vector.drain()

    def ln_tile(self, vector, scalar_ops, x, out, sq, sc):
        F = x.shape[1]
        s_sum = sc[:, 0:1]
        s_ssq = sc[:, 1:2]
        s_mu = sc[:, 2:3]
        s_var = sc[:, 3:4]
        s_inv = sc[:, 4:5]
        s_nms = sc[:, 5:6]
        self.red(vector, s_sum, x, ALU.add)
        self.red(vector, s_ssq, sq, ALU.add)
        vector.tensor_single_scalar(out=s_mu, in_=s_sum, scalar=1.0 / F, op=ALU.mult)
        vector.tensor_single_scalar(out=s_var, in_=s_ssq, scalar=1.0 / F, op=ALU.mult)
        vector.drain()
        vector.tensor_tensor(out=s_nms, in0=s_mu, in1=s_mu, op=ALU.mult)
        vector.drain()
        vector.tensor_tensor(out=s_var, in0=s_var, in1=s_nms, op=ALU.subtract)
        vector.drain()
        return s_var, s_inv, s_mu, s_nms

    def ln_apply(self, vector, x, out, s_mu, s_inv, s_nms):
        vector.tensor_tensor(out=s_nms, in0=s_mu, in1=s_inv, op=ALU.mult)
        vector.drain()
        vector.tensor_single_scalar(out=s_nms, in_=s_nms, scalar=-1.0, op=ALU.mult)
        vector.drain()
        vector.tensor_scalar(out=out, in0=x, scalar1=s_inv, scalar2=s_nms,
                             op0=ALU.mult, op1=ALU.add)

    def quant_pre(self, vector, src, v, qsc, F):
        nb = F // 32
        amax = qsc[:, 0:nb]
        scale = qsc[:, nb:2 * nb]
        iscl = qsc[:, 2 * nb:3 * nb]
        src3 = src.rearrange("p (b c) -> p b c", c=32)
        self.red(vector, amax, src3, ALU.max, absval=True)
        vector.tensor_scalar(out=scale, in0=amax, scalar1=1.0 / 7.5,
                             scalar2=1e-30, op0=ALU.mult, op1=ALU.max)
        self.recip(vector, iscl, scale)
        isb = iscl.unsqueeze(2).broadcast_to([128, nb, 32])
        vector.tensor_tensor(out=v.rearrange("p (b c) -> p b c", c=32),
                             in0=src3, in1=isb, op=ALU.mult)

    def quant_post(self, vector, v, sq, c1, qsc, F, dst):
        nb = F // 32
        scale = qsc[:, nb:2 * nb]
        vector.tensor_single_scalar(out=c1, in_=sq, scalar=4.0, op=ALU.is_ge)
        vector.tensor_single_scalar(out=sq, in_=sq, scalar=16.0, op=ALU.is_ge)
        vector.scalar_tensor_tensor(out=sq, in0=sq, scalar=0.5, in1=c1,
                                    op0=ALU.mult, op1=ALU.add)
        vector.tensor_scalar(out=c1, in0=sq, scalar1=-4.0, scalar2=8.0,
                             op0=ALU.mult, op1=ALU.add)
        vector.tensor_tensor(out=v, in0=v, in1=c1, op=ALU.mult)
        vector.tensor_single_scalar(out=c1, in_=sq, scalar=1.5, op=ALU.is_ge)
        vector.tensor_single_scalar(out=sq, in_=sq, scalar=1.0, op=ALU.is_ge)
        vector.scalar_tensor_tensor(out=sq, in0=c1, scalar=2.0, in1=sq,
                                    op0=ALU.mult, op1=ALU.add)
        vector.tensor_scalar(out=c1, in0=sq, scalar1=0.125, scalar2=0.125,
                             op0=ALU.mult, op1=ALU.add)
        vector.tensor_single_scalar(out=sq, in_=v, scalar=16.5, op=ALU.add)
        for dlev in (32.0, 16.0, 8.0, 4.0, 2.0, 1.0):
            vector.tensor_scalar(out=dst, in0=sq, scalar1=dlev, scalar2=dlev,
                                 op0=ALU.is_gt, op1=ALU.mult)
            vector.tensor_tensor(out=sq, in0=sq, in1=dst, op=ALU.subtract)
        vector.scalar_tensor_tensor(out=sq, in0=sq, scalar=-1.0, in1=v,
                                    op0=ALU.mult, op1=ALU.add)
        vector.tensor_single_scalar(out=sq, in_=sq, scalar=0.5, op=ALU.add)
        vector.tensor_tensor(out=sq, in0=sq, in1=c1, op=ALU.mult)
        scb = scale.unsqueeze(2).broadcast_to([128, nb, 32])
        vector.tensor_tensor(out=dst.rearrange("p (b c) -> p b c", c=32),
                             in0=sq.rearrange("p (b c) -> p b c", c=32),
                             in1=scb, op=ALU.mult)


def build_program(n_sublayers=6):
    nc = bass.Bass("TRN2", target_bir_lowering=False, debug=False,
                   detect_race_conditions=False)
    B = Build(nc, n_sublayers)

    inp = {}

    def I(name, shape):
        inp[name] = nc.dram_tensor(name, list(shape), F32, kind="ExternalInput").ap()
        return inp[name]

    x_in = I("x_in", (T, D))
    w_in_t = I("w_in_t", (D, 2 * DI))
    conv_w_l = I("conv_w_l", (128, NS, 4))
    conv_b_l = I("conv_b_l", (128, NS))
    w_x_t = I("w_x_t", (DI, 33))
    wdt_l = I("wdt_l", (128, NS))
    bdt_l = I("bdt_l", (128, NS))
    alog_c = I("alog_c", (128, 1))
    d_col = I("d_col", (128, NS))
    lng_col = I("lng_col", (128, NS))
    lnb_col = I("lnb_col", (128, NS))
    w_out_t = I("w_out_t", (DI, D))
    ffn_w = I("ffn_w", (D, D))
    ffn_b_col = I("ffn_b_col", (128, 8))
    wq_i = I("wq", (D, D))
    wk_i = I("wk", (D, D))
    wv_i = I("wv", (D, D))
    wo_i = I("wo", (D, D))
    rw_t = I("rw_t", (D, 16))
    cb_rep = I("cb_rep", (128, 16))
    gate_w = I("gate_w", (16, D, 256))
    up_w = I("up_w", (16, D, 256))
    down_w = I("down_w", (16, 256, D))
    sgw_t = I("sgw_t", (D, DI))
    suw_t = I("suw_t", (D, DI))
    sdw_t = I("sdw_t", (DI, D))
    ident = I("ident", (128, 128))
    ones_row = I("ones_row", (1, 128))
    ones_col = I("ones_col", (128, 1))
    win_mask = I("win_mask", (3, 128, 384))

    out_x = nc.dram_tensor("out_x", [T, D], F32, kind="ExternalOutput").ap()

    _dk = "ExternalOutput" if _DEBUG_OUT[0] else "Internal"
    xres = nc.dram_tensor("xres", [T, D], F32, kind=_dk).ap()
    u_fm = nc.dram_tensor("u_fm", [DI, T], F32, kind="Internal").ap()
    dt_fm = nc.dram_tensor("dt_fm", [DI, T], F32, kind="Internal").ap()
    du_fm = nc.dram_tensor("du_fm", [DI, T], F32, kind="Internal").ap()
    sz_fm = nc.dram_tensor("sz_fm", [DI, T], F32, kind="Internal").ap()
    y_fm = nc.dram_tensor("y_fm", [DI, T], F32, kind="Internal").ap()
    bc_fm = nc.dram_tensor("bc_fm", [32, T], F32, kind="Internal").ap()
    wdq_t = nc.dram_tensor("wdq_t", [D, D], F32, kind=_dk).ap()
    dbg = nc.dram_tensor("dbg", [128, 4 * D], F32, kind=_dk).ap()

    q_fm = nc.dram_tensor("q_fm", [D, T], F32, kind="Internal").ap()
    k_fm = nc.dram_tensor("k_fm", [D, T], F32, kind="Internal").ap()
    v_tok = nc.dram_tensor("v_tok", [T, D], F32, kind="Internal").ap()
    o_tok = nc.dram_tensor("o_tok", [T, D], F32, kind="Internal").ap()
    comb_fm = nc.dram_tensor("comb_fm", [16, T], F32, kind="Internal").ap()

    c_eps = nc.alloc_sbuf_tensor("c_eps", [128, 1], F32).ap()
    nc.const_aps.register(F32, EPS, c_eps)
    ids_g = nc.alloc_sbuf_tensor("ids_g", [128, 128], F32).ap()
    onesr_g = nc.alloc_sbuf_tensor("onesr_g", [1, 128], F32).ap()
    onesc_g = nc.alloc_sbuf_tensor("onesc_g", [128, 1], F32).ap()
    acol_g = nc.alloc_sbuf_tensor("acol_g", [128, 1], F32).ap()
    arep_g = nc.alloc_sbuf_tensor("arep_g", [128, 16], F32).ap()
    B.dram = dict(dbg=dbg, x_in=x_in, xres=xres, out_x=out_x, u_fm=u_fm, dt_fm=dt_fm,
                  du_fm=du_fm, sz_fm=sz_fm, y_fm=y_fm, bc_fm=bc_fm, wdq_t=wdq_t,
                  q_fm=q_fm, k_fm=k_fm, v_tok=v_tok, o_tok=o_tok, comb_fm=comb_fm)
    B.inp = inp
    B.ids = ids_g
    B.c_eps = c_eps
    B.onesr = onesr_g
    B.onesc = onesc_g
    B.acol = acol_g
    B.arep = arep_g

    with (
        nc.semaphore("bar") as h_bar,
        nc.semaphore("s_ld") as h_ld,
        nc.semaphore("s_st") as h_st,
        nc.semaphore("s_t") as h_t,
        nc.semaphore("s_v") as h_v,
        nc.semaphore("s_s") as h_s,
        nc.semaphore("s_g") as h_g,
    ):
        B.bar = Sem(h_bar)
        B.ld = Sem(h_ld)
        B.st = Sem(h_st)
        B.tv = Sem(h_t)
        B.vv = Sem(h_v)
        B.sv = Sem(h_s)
        B.gv = Sem(h_g)

        phase_a(B)
        phase_ffn(B, src=x_in, dst=xres)
        if n_sublayers >= 2:
            phase_mamba_in(B)
            phase_scan(B)
            phase_mamba_out(B)
        if n_sublayers >= 3:
            phase_ffn(B, src=xres, dst=xres)
        if n_sublayers >= 4:
            phase_moe(B, src=xres, dst=xres)
        if n_sublayers >= 5:
            phase_attn(B)
        if n_sublayers >= 6:
            phase_moe(B, src=xres, dst=out_x)
            copy_done = True
        else:
            phase_copy_out(B)

    return nc, inp


def barrier_wrap(B, block, used):
    k = B.phase_idx
    bar = B.bar
    emitted = set()

    def mk(name, attr):
        def deco(f):
            def wrapped(eng):
                eng.wait_ge(bar.h, 5 * k)
                f(eng)
                eng.drain()
                eng.sem_inc(bar.h, 1)
            attr(wrapped)
            emitted.add(name)
        return deco

    decos = {n: mk(n, getattr(block, n))
             for n in ("sync", "tensor", "vector", "scalar", "gpsimd")}

    def finish():
        for n in ("sync", "tensor", "vector", "scalar", "gpsimd"):
            if n not in emitted:
                def idle(eng):
                    pass
                decos[n](idle)
        B.bar.v += 5

    return decos, finish


def phase_a(B):
    nc = B.nc
    ld, st, tv, vv, sv = B.ld, B.st, B.tv, B.vv, B.sv
    inp = B.inp
    ld0, sv0, vv0, tv0, st0 = ld.v, sv.v, vv.v, tv.v, st.v
    with (
        nc.sbuf_tensor("pa_w", [128, D], F32) as wtile,
        nc.sbuf_tensor("pa_wq", [128, D], F32) as wq_s,
        nc.sbuf_tensor("pa_sc", [128, 96], F32) as qsc,
        nc.sbuf_tensor("pa_tmp", [128, 3 * D], F32) as qtmp,
        nc.sbuf_tensor("pa_al", [128, 1], F32) as al_s,
        nc.sbuf_tensor("pa_ar", [1, 16], F32) as arow,
        nc.sbuf_tensor("pa_tr", [128, 2 * 128], F32) as tr_s,
        nc.psum_tensor("pa_ps", [128, 128], F32) as ps_tr,
        nc.psum_tensor("pa_ps2", [128, 16], F32) as ps_rep,
        nc.Block() as block,
    ):
        wtile, wq_s, qsc, qtmp, al_s, arow, tr_s, ps_tr, ps_rep = [
            t.ap() for t in (wtile, wq_s, qsc, qtmp, al_s, arow, tr_s,
                             ps_tr, ps_rep)]
        decos, finish = barrier_wrap(B, block, None)

        @decos["sync"]
        def _(sync):
            sync.dma_start(out=B.ids, in_=inp["ident"]).then_inc(ld.h, 16)
            sync.dma_start(out=B.onesr, in_=inp["ones_row"]).then_inc(ld.h, 16)
            sync.dma_start(out=B.onesc, in_=inp["ones_col"]).then_inc(ld.h, 16)
            sync.dma_start(out=al_s, in_=inp["alog_c"]).then_inc(ld.h, 16)
            ld.v += 64
            for r in range(8):
                if r >= 1:
                    sync.wait_ge(vv.h, vv0 + 1 + 2 * r)
                sync.dma_start(out=wtile, in_=inp["ffn_w"][r * 128:(r + 1) * 128, :]
                               ).then_inc(ld.h, 16)
                ld.v += 16

        def sq_val_v(r):
            return 2 if r == 0 else 3 + 9 * (r - 1)

        @decos["vector"]
        def _(vector):
            vector.wait_ge(ld.h, ld0 + 80)
            vector.memset(B.c_eps, EPS)
            vector.tensor_single_scalar(out=B.acol, in_=al_s, scalar=5.0, op=ALU.min)
            B.raw_inc(vector, vv, 1)
            for r in range(8):
                vector.wait_ge(ld.h, ld0 + 80 + 16 * r)
                B.quant_pre(vector, wtile, qtmp[:, 0:D], qsc, D)
                B.raw_inc(vector, vv, 1)
                vector.wait_ge(sv.h, sv0 + sq_val_v(r))
                if r >= 1:
                    vector.wait_ge(tv.h, tv0 + 8 * r)
                B.quant_post(vector, qtmp[:, 0:D], qtmp[:, D:2 * D],
                             qtmp[:, 2 * D:3 * D], qsc, D, wq_s)
                B.raw_inc(vector, vv, 1)

        def sq_val(r):
            return 2 if r == 0 else 3 + 9 * (r - 1)

        def evac_val(j):
            rr, kk = j // 8, j % 8
            return (4 + 9 * rr + kk) if rr <= 6 else (66 + kk)

        def emit_evacs(scalar, rr):
            for kk in range(8):
                j = rr * 8 + kk
                scalar.wait_ge(tv.h, tv0 + j + 1)
                if j >= 2:
                    scalar.wait_ge(st.h, st0 + 16 * (j - 1))
                slot = tr_s[:, (j % 2) * 128:(j % 2 + 1) * 128]
                inst = scalar.activation(out=slot, in_=ps_tr, func=ACT.Copy)
                B.inc(inst, sv, 1)

        @decos["scalar"]
        def _(scalar):
            scalar.wait_ge(vv.h, vv0 + 1)
            scalar.activation(out=B.acol, in_=B.acol, func=ACT.Exp)
            scalar.mul(out=B.acol, in_=B.acol, mul=-1.0)
            B.raw_inc(scalar, sv, 1)
            for r in range(8):
                scalar.wait_ge(vv.h, vv0 + 1 + 2 * r + 1)
                inst = scalar.activation(out=qtmp[:, D:2 * D], in_=qtmp[:, 0:D],
                                         func=ACT.Square)
                B.inc(inst, sv, 1)
                if r >= 1:
                    emit_evacs(scalar, r - 1)
            emit_evacs(scalar, 7)
            scalar.wait_ge(tv.h, tv0 + 64 + 1)
            inst = scalar.activation(out=arow, in_=ps_tr[0:1, 0:16], func=ACT.Copy)
            B.inc(inst, sv, 1)
            scalar.wait_ge(tv.h, tv0 + 64 + 2)
            inst = scalar.activation(out=B.arep, in_=ps_rep, func=ACT.Copy)
            B.inc(inst, sv, 1)

        @decos["tensor"]
        def _(tensor):
            tensor.wait_ge(ld.h, ld0 + 80)
            for r in range(8):
                tensor.wait_ge(vv.h, vv0 + 1 + 2 * r + 2)
                for k in range(8):
                    idx = r * 8 + k
                    if idx >= 1:
                        j = idx - 1
                        ev = (4 + 9 * (j // 8) + j % 8) if j // 8 <= 6 else (66 + j % 8)
                        tensor.wait_ge(sv.h, sv0 + ev)
                    inst = tensor.transpose(ps_tr, wq_s[:, k * 128:(k + 1) * 128],
                                            B.ids)
                    B.inc(inst, tv, 1)
            tensor.wait_ge(sv.h, sv0 + 1)
            tensor.wait_ge(sv.h, sv0 + 73)
            inst = tensor.transpose(ps_tr[0:1, 0:16], B.acol[0:16, :],
                                    B.ids[0:16, 0:16])
            B.inc(inst, tv, 1)
            tensor.wait_ge(sv.h, sv0 + 74)
            inst = tensor.matmul(ps_rep, B.onesr, arow, start=True, stop=True)
            B.inc(inst, tv, 1)

        @decos["gpsimd"]
        def _(gpsimd):
            for idx in range(64):
                r, k = idx // 8, idx % 8
                ev = (4 + 9 * (idx // 8) + idx % 8) if idx // 8 <= 6 else (66 + idx % 8)
                gpsimd.wait_ge(sv.h, sv0 + ev)
                slot = tr_s[:, (idx % 2) * 128:(idx % 2 + 1) * 128]
                inst = gpsimd.dma_start(
                    out=B.dram["wdq_t"][k * 128:(k + 1) * 128,
                                        r * 128:(r + 1) * 128], in_=slot)
                B.inc(inst, st, 16)
                gpsimd.wait_ge(st.h, st.v)

        finish()
    B.phase_idx += 1


def phase_ffn(B, src, dst):
    nc = B.nc
    ld, st, tv, vv, sv = B.ld, B.st, B.tv, B.vv, B.sv
    inp = B.inp
    ld0, sv0, vv0, tv0, st0 = ld.v, sv.v, vv.v, tv.v, st.v
    VPG = 3 * NSUB + 8 + NSUB
    SPG = 3 * NSUB + 32 + 32
    TPG = 32 + 64 + 32
    sfx = f"_{B.phase_idx}"
    with (
        nc.sbuf_tensor("pf_x" + sfx, [128, NSUB * D], F32) as xg,
        nc.sbuf_tensor("pf_ln" + sfx, [128, D], F32) as lnx,
        nc.sbuf_tensor("pf_xq" + sfx, [128, NSUB * D], F32) as xq,
        nc.sbuf_tensor("pf_xqT" + sfx, [128, 8 * GT], F32) as xqT,
        nc.sbuf_tensor("pf_w" + sfx, [128, 8 * D], F32) as wdq,
        nc.sbuf_tensor("pf_yf" + sfx, [128, 8 * GT], F32) as yf,
        nc.sbuf_tensor("pf_yt" + sfx, [128, NSUB * D], F32) as yt,
        nc.sbuf_tensor("pf_sc" + sfx, [128, 96], F32) as qsc,
        nc.sbuf_tensor("pf_tmp" + sfx, [128, 3 * D], F32) as qtmp,
        nc.sbuf_tensor("pf_st" + sfx, [128, 8], F32) as lnsc,
        nc.sbuf_tensor("pf_b" + sfx, [128, 8], F32) as ffnb,
        nc.psum_tensor("pf_ptr0" + sfx, [128, 128], F32) as ps_tr0,
        nc.psum_tensor("pf_ptr1" + sfx, [128, 128], F32) as ps_tr1,
        nc.psum_tensor("pf_mm0" + sfx, [128, GT], F32) as ps_mm0,
        nc.psum_tensor("pf_mm1" + sfx, [128, GT], F32) as ps_mm1,
        nc.Block() as block,
    ):
        (xg, lnx, xq, xqT, wdq, yf, yt, qsc, qtmp, lnsc, ffnb, ps_tr0,
         ps_tr1, ps_mm0, ps_mm1) = [
            t.ap() for t in (xg, lnx, xq, xqT, wdq, yf, yt, qsc, qtmp, lnsc,
                             ffnb, ps_tr0, ps_tr1, ps_mm0, ps_mm1)]
        ps_tr_s = [ps_tr0, ps_tr1]
        ps_mm_s = [ps_mm0, ps_mm1]
        decos, finish = barrier_wrap(B, block, None)

        @decos["sync"]
        def _(sync):
            sync.dma_start(out=ffnb, in_=inp["ffn_b_col"]).then_inc(ld.h, 16)
            for kk in range(8):
                sync.dma_start(out=wdq[:, kk * D:(kk + 1) * D],
                               in_=B.dram["wdq_t"][kk * 128:(kk + 1) * 128, :]
                               ).then_inc(ld.h, 16)
            ld.v += 16 * 9
            for g in range(NG):
                if g >= 1:
                    sync.wait_ge(vv.h, vv0 + g * VPG)
                for sb in range(NSUB):
                    r0 = g * GT + sb * 128
                    sync.dma_start(out=xg[:, sb * D:(sb + 1) * D],
                                   in_=src[r0:r0 + 128, :]).then_inc(ld.h, 16)
                    ld.v += 16

        @decos["vector"]
        def _(vector):
            for g in range(NG):
                for sb in range(NSUB):
                    vector.wait_ge(ld.h, ld0 + 144 + 64 * (g + 1))
                    xs_ = xg[:, sb * D:(sb + 1) * D]
                    vector.wait_ge(sv.h, sv0 + g * SPG + 3 * sb + 1)
                    if _SKIP_LN[0]:
                        vector.engine_nop()
                        B.raw_inc(vector, vv, 1)
                        vector.wait_ge(sv.h, sv0 + g * SPG + 3 * sb + 2)
                        vector.tensor_copy(out=lnx, in_=xs_)
                    else:
                        B.ln_tile(vector, None, xs_, lnx, qtmp[:, D:2 * D], lnsc)
                        B.raw_inc(vector, vv, 1)
                        vector.wait_ge(sv.h, sv0 + g * SPG + 3 * sb + 2)
                        B.ln_apply(vector, xs_, lnx, lnsc[:, 2:3], lnsc[:, 4:5],
                                   lnsc[:, 5:6])
                    if _SKIP_QUANT[0]:
                        vector.tensor_copy(out=qtmp[:, 0:D], in_=lnx)
                        B.raw_inc(vector, vv, 1)
                        vector.wait_ge(sv.h, sv0 + g * SPG + 3 * sb + 3)
                        vector.tensor_copy(out=xq[:, sb * D:(sb + 1) * D], in_=lnx)
                        B.raw_inc(vector, vv, 1)
                    else:
                        B.quant_pre(vector, lnx, qtmp[:, 0:D], qsc, D)
                        B.raw_inc(vector, vv, 1)
                        vector.wait_ge(sv.h, sv0 + g * SPG + 3 * sb + 3)
                        B.quant_post(vector, qtmp[:, 0:D], qtmp[:, D:2 * D],
                                     qtmp[:, 2 * D:3 * D], qsc, D,
                                     xq[:, sb * D:(sb + 1) * D])
                        B.raw_inc(vector, vv, 1)
                for oc in range(8):
                    vector.wait_ge(tv.h, tv0 + g * TPG + 32 + (oc + 1) * 8)
                    pslot = ps_mm_s[oc % 2]
                    vector.tensor_scalar(out=yf[:, oc * GT:(oc + 1) * GT],
                                         in0=pslot, scalar1=ffnb[:, oc:oc + 1],
                                         scalar2=0.5, op0=ALU.add, op1=ALU.mult)
                    B.raw_inc(vector, vv, 1)
                for sb in range(NSUB):
                    vector.wait_ge(sv.h,
                                   sv0 + g * SPG + 3 * NSUB + 32 + (sb + 1) * 8)
                    vector.tensor_tensor(out=yt[:, sb * D:(sb + 1) * D],
                                         in0=yt[:, sb * D:(sb + 1) * D],
                                         in1=xg[:, sb * D:(sb + 1) * D], op=ALU.add)
                    B.raw_inc(vector, vv, 1)

        @decos["scalar"]
        def _(scalar):
            for g in range(NG):
                for sb in range(NSUB):
                    scalar.wait_ge(ld.h, ld0 + 144 + 64 * (g + 1))
                    if g * NSUB + sb >= 1:
                        scalar.wait_ge(vv.h, vv0 + g * VPG + 3 * sb)
                    inst = scalar.activation(out=qtmp[:, D:2 * D],
                                             in_=xg[:, sb * D:(sb + 1) * D],
                                             func=ACT.Square)
                    B.inc(inst, sv, 1)
                    scalar.wait_ge(vv.h, vv0 + g * VPG + 3 * sb + 1)
                    scalar.activation(out=lnsc[:, 4:5], in_=lnsc[:, 3:4],
                                      func=ACT.Ln, bias=EPS)
                    scalar.drain()
                    inst = scalar.activation(out=lnsc[:, 4:5], in_=lnsc[:, 4:5],
                                             func=ACT.Exp, scale=-0.5)
                    scalar.drain()
                    scalar.sem_inc(sv.h, 1)
                    sv.v += 1
                    scalar.wait_ge(vv.h, vv0 + g * VPG + 3 * sb + 2)
                    inst = scalar.activation(out=qtmp[:, D:2 * D],
                                             in_=qtmp[:, 0:D], func=ACT.Square)
                    B.inc(inst, sv, 1)
                for idx in range(32):
                    sb, k = idx // 8, idx % 8
                    scalar.wait_ge(tv.h, tv0 + g * TPG + idx + 1)
                    slot = ps_tr_s[idx % 2]
                    inst = scalar.activation(
                        out=xqT[:, k * GT + sb * 128:k * GT + (sb + 1) * 128],
                        in_=slot, func=ACT.Copy)
                    B.inc(inst, sv, 1)
                for idx in range(32):
                    sb, oc = idx // 8, idx % 8
                    scalar.wait_ge(tv.h, tv0 + g * TPG + 32 + 64 + idx + 1)
                    if idx == 0 and g >= 1:
                        scalar.wait_ge(st.h, st0 + 64 * g)
                    slot = ps_tr_s[idx % 2]
                    inst = scalar.activation(
                        out=yt[:, sb * D + oc * 128:sb * D + (oc + 1) * 128],
                        in_=slot, func=ACT.Copy)
                    B.inc(inst, sv, 1)

        @decos["tensor"]
        def _(tensor):
            tensor.wait_ge(ld.h, ld0 + 144 + 64)
            for g in range(NG):
                for idx in range(32):
                    sb, k = idx // 8, idx % 8
                    if k == 0:
                        tensor.wait_ge(vv.h, vv0 + g * VPG + 3 * sb + 3)
                    if idx >= 2:
                        tensor.wait_ge(sv.h, sv0 + g * SPG + 3 * NSUB + idx - 1)
                    elif g >= 1:
                        tensor.wait_ge(sv.h, sv0 + g * SPG)
                    slot = ps_tr_s[idx % 2]
                    inst = tensor.transpose(
                        slot, xq[:, sb * D + k * 128:sb * D + (k + 1) * 128], B.ids)
                    B.inc(inst, tv, 1)
                for oc in range(8):
                    if oc == 0:
                        tensor.wait_ge(sv.h, sv0 + g * SPG + 3 * NSUB + 32)
                    if oc >= 2:
                        tensor.wait_ge(vv.h, vv0 + g * VPG + 3 * NSUB + oc - 1)
                    elif g >= 1:
                        tensor.wait_ge(vv.h, vv0 + (g - 1) * VPG + 3 * NSUB + 8)
                    pslot = ps_mm_s[oc % 2]
                    for k in range(8):
                        inst = tensor.matmul(
                            pslot,
                            wdq[:, k * D + oc * 128:k * D + (oc + 1) * 128],
                            xqT[:, k * GT:(k + 1) * GT],
                            start=(k == 0), stop=(k == 7))
                        B.inc(inst, tv, 1)
                for idx in range(32):
                    sb, oc = idx // 8, idx % 8
                    if sb == 0:
                        tensor.wait_ge(vv.h, vv0 + g * VPG + 3 * NSUB + oc + 1)
                    if idx >= 2:
                        tensor.wait_ge(sv.h,
                                       sv0 + g * SPG + 3 * NSUB + 32 + idx - 1)
                    slot = ps_tr_s[idx % 2]
                    inst = tensor.transpose(
                        slot, yf[:, oc * GT + sb * 128:oc * GT + (sb + 1) * 128],
                        B.ids)
                    B.inc(inst, tv, 1)

        @decos["gpsimd"]
        def _(gpsimd):
            for g in range(NG):
                for sb in range(NSUB):
                    gpsimd.wait_ge(vv.h,
                                   vv0 + g * VPG + 3 * NSUB + 8 + sb + 1)
                    r0 = g * GT + sb * 128
                    inst = gpsimd.dma_start(out=dst[r0:r0 + 128, :],
                                            in_=yt[:, sb * D:(sb + 1) * D])
                    B.inc(inst, st, 16)
            if _DEBUG_OUT[0]:
                inst = gpsimd.dma_start(out=B.dram["dbg"][:, 0:D], in_=xq[:, 0:D])
                B.inc(inst, st, 16)
                inst = gpsimd.dma_start(out=B.dram["dbg"][:, D:D+GT], in_=xqT[:, 0:GT])
                B.inc(inst, st, 16)
                inst = gpsimd.dma_start(out=B.dram["dbg"][:, 2*D:2*D+GT], in_=yf[:, 0:GT])
                B.inc(inst, st, 16)
                gpsimd.wait_ge(st.h, st.v)
            gpsimd.wait_ge(st.h, st.v)

        finish()
    B.phase_idx += 1


def phase_mamba_in(B):
    nc = B.nc
    ld, st, tv, vv, sv = B.ld, B.st, B.tv, B.vv, B.sv
    inp = B.inp
    ld0, sv0, vv0, tv0, st0 = ld.v, sv.v, vv.v, tv.v, st.v
    VPGm = 44
    SPGm = 74
    TPGm = 305
    STG = 464

    import contextlib
    with contextlib.ExitStack() as _stk:
        def A(*args):
            return _stk.enter_context(nc.sbuf_tensor(*args)).ap()

        def PP(*args):
            return _stk.enter_context(nc.psum_tensor(*args)).ap()

        xg = A("pm_x", [128, NSUB * D], F32)
        lnx = A("pm_ln", [128, D], F32)
        lnT = A("pm_lnT", [128, 8 * GT], F32)
        wbuf = A("pm_w", [128, 2 * 8 * 128], F32)
        xc_sb = A("pm_xc", [128, NS * (GT + 3)], F32)
        u_sb = A("pm_u", [128, NS * GT], F32)
        dt_sb = A("pm_dt", [128, NS * GT], F32)
        du_sb = A("pm_du", [128, NS * GT], F32)
        szb = A("pm_sz", [128, GT], F32)
        wx_sb = A("pm_wx", [128, NS * 33], F32)
        cw_sb = A("pm_cw", [128, NS * 4], F32)
        cb_sb = A("pm_cb", [128, NS], F32)
        wdt_sb = A("pm_wdt", [128, NS], F32)
        bdt_sb = A("pm_bdt", [128, NS], F32)
        xdbl_sb = A("pm_xdbl", [33, GT], F32)
        dtr_sb = A("pm_dtr", [128, GT], F32)
        lnsc = A("pm_sc", [128, 8], F32)
        qtmp = A("pm_q", [128, 2 * D], F32)
        ps_tr0 = PP("pm_tr0", [128, 128], F32)
        ps_tr1 = PP("pm_tr1", [128, 128], F32)
        ps_mm0 = PP("pm_mm0", [128, GT], F32)
        ps_mm1 = PP("pm_mm1", [128, GT], F32)
        ps_xd = PP("pm_xd", [33, GT], F32)
        ps_rep = PP("pm_rep", [128, GT], F32)
        block = _stk.enter_context(nc.Block())
        ps_tr_s = [ps_tr0, ps_tr1]
        ps_mm_s = [ps_mm0, ps_mm1]
        decos, finish = barrier_wrap(B, block, None)

        def xw_val(g):
            return 176 if g == 0 else 80 + (g * 36 + 4) * 16

        @decos["sync"]
        def _(sync):
            sync.dma_start(out=wx_sb.rearrange("p (c e) -> p c e", c=NS),
                           in_=inp["w_x_t"].rearrange("(c p) e -> p c e", p=128)
                           ).then_inc(ld.h, 16)
            sync.dma_start(out=cw_sb,
                           in_=inp["conv_w_l"].rearrange("p c k -> p (c k)")
                           ).then_inc(ld.h, 16)
            sync.dma_start(out=cb_sb, in_=inp["conv_b_l"]).then_inc(ld.h, 16)
            sync.dma_start(out=wdt_sb, in_=inp["wdt_l"]).then_inc(ld.h, 16)
            sync.dma_start(out=bdt_sb, in_=inp["bdt_l"]).then_inc(ld.h, 16)
            ld.v += 80
            for g in range(NG):
                if g >= 1:
                    sync.wait_ge(vv.h, vv0 + g * VPGm)
                for sb in range(NSUB):
                    r0 = g * GT + sb * 128
                    sync.dma_start(out=xg[:, sb * D:(sb + 1) * D],
                                   in_=B.dram["xres"][r0:r0 + 128, :]
                                   ).then_inc(ld.h, 16)
                    ld.v += 16
                for ec in range(32):
                    if g * 32 + ec >= 2:
                        sync.wait_ge(tv.h, tv0 + g * TPGm + 32 + (ec - 1) * 8)
                    src = inp["w_in_t"][:, ec * 128:(ec + 1) * 128]
                    src = src.rearrange("(k p) c -> p k c", p=128)
                    wdst = wbuf[:, (ec % 2) * 1024:(ec % 2 + 1) * 1024]
                    sync.dma_start(out=wdst.rearrange("p (k c) -> p k c", k=8),
                                   in_=src).then_inc(ld.h, 16)
                    ld.v += 16

        @decos["vector"]
        def _(vector):
            xc3 = xc_sb.rearrange("p (c t) -> p c t", c=NS)
            vector.memset(xc3[:, :, 0:3], 0.0)
            vector.drain()
            for g in range(NG):
                vb = vv0 + g * VPGm
                sbb = sv0 + g * SPGm
                tb = tv0 + g * TPGm
                vector.wait_ge(ld.h, ld0 + xw_val(g))
                for sb in range(NSUB):
                    xs_ = xg[:, sb * D:(sb + 1) * D]
                    sqi = (1, 3, 13, 23)[sb]
                    vector.wait_ge(sv.h, sbb + sqi)
                    B.ln_tile(vector, None, xs_, lnx, qtmp[:, D:2 * D], lnsc)
                    B.raw_inc(vector, vv, 1)
                    vector.wait_ge(sv.h, sbb + sqi + 1)
                    if sb >= 1:
                        vector.wait_ge(tv.h, tb + 8 * sb)
                    elif g >= 1:
                        vector.wait_ge(tv.h, tv0 + (g - 1) * TPGm + 32)
                    B.ln_apply(vector, xs_, lnx, lnsc[:, 2:3], lnsc[:, 4:5],
                               lnsc[:, 5:6])
                    B.raw_inc(vector, vv, 1)
                for ec in range(16):
                    if ec == 0 and g >= 1:
                        vector.wait_ge(st.h, st0 + g * STG)
                    vector.wait_ge(tv.h, tb + 32 + (ec + 1) * 8)
                    pslot = ps_mm_s[ec % 2]
                    vector.tensor_copy(out=xc_sb[:, ec * (GT + 3) + 3:
                                                 (ec + 1) * (GT + 3)],
                                       in_=pslot)
                    B.raw_inc(vector, vv, 1)
                    base = ec * (GT + 3)
                    cacc = du_sb[:, ec * GT:(ec + 1) * GT]
                    vector.tensor_scalar(out=cacc,
                                         in0=xc_sb[:, base:base + GT],
                                         scalar1=cw_sb[:, ec * 4:ec * 4 + 1],
                                         scalar2=cb_sb[:, ec:ec + 1],
                                         op0=ALU.mult, op1=ALU.add)
                    for kk in range(1, 4):
                        vector.scalar_tensor_tensor(
                            out=cacc, in0=xc_sb[:, base + kk:base + kk + GT],
                            scalar=cw_sb[:, ec * 4 + kk:ec * 4 + kk + 1],
                            in1=cacc, op0=ALU.mult, op1=ALU.add)
                    B.raw_inc(vector, vv, 1)
                vector.wait_ge(tv.h, tb + 32 + 256 + 16)
                vector.tensor_copy(out=xdbl_sb, in_=ps_xd)
                B.raw_inc(vector, vv, 1)
                vector.wait_ge(tv.h, tb + 32 + 256 + 16 + 1)
                vector.tensor_copy(out=dtr_sb, in_=ps_rep)
                B.raw_inc(vector, vv, 1)
                vector.wait_ge(sv.h, sbb + 8 + 32 + 16 + 16)
                dp3 = du_sb.rearrange("p (c t) -> p c t", c=NS)
                dtr3 = dtr_sb.unsqueeze(1).broadcast_to([128, NS, GT])
                wdt3 = wdt_sb.unsqueeze(2).broadcast_to([128, NS, GT])
                bdt3 = bdt_sb.unsqueeze(2).broadcast_to([128, NS, GT])
                vector.tensor_tensor(out=dp3, in0=dtr3, in1=wdt3, op=ALU.mult)
                vector.tensor_tensor(out=dp3, in0=dp3, in1=bdt3, op=ALU.add)
                B.raw_inc(vector, vv, 1)
                vector.wait_ge(sv.h, sbb + SPGm)
                vector.tensor_scalar(out=dt_sb, in0=du_sb, scalar1=1e-4,
                                     scalar2=10.0, op0=ALU.max, op1=ALU.min)
                vector.tensor_tensor(out=du_sb, in0=dt_sb, in1=u_sb,
                                     op=ALU.mult)
                vector.tensor_copy(out=xc3[:, :, 0:3], in_=xc3[:, :, GT:GT + 3])
                B.raw_inc(vector, vv, 1)

        @decos["scalar"]
        def _(scalar):
            def emit_evacs(scalar, tb, lo, hi):
                for idx in range(lo, hi):
                    sb, k = idx // 8, idx % 8
                    scalar.wait_ge(tv.h, tb + idx + 1)
                    slot = ps_tr_s[idx % 2]
                    inst = scalar.activation(
                        out=lnT[:, k * GT + sb * 128:k * GT + (sb + 1) * 128],
                        in_=slot, func=ACT.Copy)
                    B.inc(inst, sv, 1)

            for g in range(NG):
                vb = vv0 + g * VPGm
                sbb = sv0 + g * SPGm
                tb = tv0 + g * TPGm
                scalar.wait_ge(ld.h, ld0 + xw_val(g))
                for sb in range(NSUB):
                    if g * NSUB + sb >= 1:
                        scalar.wait_ge(vv.h, vb + 2 * sb)
                    inst = scalar.activation(out=qtmp[:, D:2 * D],
                                             in_=xg[:, sb * D:(sb + 1) * D],
                                             func=ACT.Square)
                    B.inc(inst, sv, 1)
                    scalar.wait_ge(vv.h, vb + 2 * sb + 1)
                    scalar.activation(out=lnsc[:, 4:5], in_=lnsc[:, 3:4],
                                      func=ACT.Ln, bias=EPS)
                    scalar.drain()
                    scalar.activation(out=lnsc[:, 4:5], in_=lnsc[:, 4:5],
                                      func=ACT.Exp, scale=-0.5)
                    scalar.drain()
                    scalar.sem_inc(sv.h, 1)
                    sv.v += 1
                    if sb >= 1:
                        emit_evacs(scalar, tb, (sb - 1) * 8, sb * 8)
                emit_evacs(scalar, tb, 24, 32)
                for zc in range(16):
                    ec = 16 + zc
                    scalar.wait_ge(tv.h, tb + 32 + (ec + 1) * 8)
                    if zc >= 1:
                        scalar.wait_ge(st.h, st0 + g * STG + zc * 16)
                    pslot = ps_mm_s[ec % 2]
                    inst = scalar.activation(out=szb, in_=pslot, func=ACT.Silu)
                    B.inc(inst, sv, 1)
                for ec in range(16):
                    if ec == 0 and g >= 1:
                        scalar.wait_ge(st.h, st0 + (g - 1) * STG + 320)
                    scalar.wait_ge(vv.h, vb + 8 + 2 * ec + 2)
                    inst = scalar.activation(
                        out=u_sb[:, ec * GT:(ec + 1) * GT],
                        in_=du_sb[:, ec * GT:(ec + 1) * GT], func=ACT.Silu)
                    B.inc(inst, sv, 1)
                scalar.wait_ge(vv.h, vb + 43)
                inst = scalar.activation(out=dt_sb, in_=du_sb, func=ACT.Exp)
                B.inc(inst, sv, 1)
                scalar.drain()
                inst = scalar.activation(out=du_sb, in_=dt_sb, func=ACT.Ln,
                                         bias=1.0)
                B.inc(inst, sv, 1)

        @decos["tensor"]
        def _(tensor):
            tensor.wait_ge(ld.h, ld0 + 176)
            for g in range(NG):
                vb = vv0 + g * VPGm
                sbb = sv0 + g * SPGm
                tb = tv0 + g * TPGm
                for sb in range(NSUB):
                    tensor.wait_ge(vv.h, vb + 2 * sb + 2)
                    for k in range(8):
                        idx = sb * 8 + k
                        if idx >= 2:
                            j = idx - 2
                            ei = 5 + j if j < 8 else (j + 7 if j < 16 else j + 9)
                            tensor.wait_ge(sv.h, sbb + ei)
                        elif g >= 1:
                            tensor.wait_ge(sv.h, sv0 + g * SPGm)
                        slot = ps_tr_s[idx % 2]
                        inst = tensor.transpose(
                            slot, lnx[:, k * 128:(k + 1) * 128], B.ids)
                        B.inc(inst, tv, 1)
                for ec in range(32):
                    if ec == 0:
                        tensor.wait_ge(sv.h, sbb + 8 + 32)
                    wv = 80 + 16 * (g * 36 + 4 + ec + 1)
                    if g == 0 and ec < 2:
                        wv = 176
                    tensor.wait_ge(ld.h, ld0 + wv)
                    if ec >= 2:
                        if ec - 2 < 16:
                            tensor.wait_ge(vv.h, vb + 8 + 2 * (ec - 2) + 1)
                        else:
                            tensor.wait_ge(sv.h,
                                           sbb + 8 + 32 + (ec - 2 - 16) + 1)
                    pslot = ps_mm_s[ec % 2]
                    wslot = wbuf[:, (ec % 2) * 1024:(ec % 2 + 1) * 1024]
                    for k in range(8):
                        inst = tensor.matmul(pslot,
                                             wslot[:, k * 128:(k + 1) * 128],
                                             lnT[:, k * GT:(k + 1) * GT],
                                             start=(k == 0), stop=(k == 7))
                        B.inc(inst, tv, 1)
                tensor.wait_ge(sv.h, sbb + 8 + 32 + 16 + 16)
                for dch in range(NS):
                    inst = tensor.matmul(ps_xd,
                                         wx_sb[:, dch * 33:(dch + 1) * 33],
                                         u_sb[:, dch * GT:(dch + 1) * GT],
                                         start=(dch == 0), stop=(dch == NS - 1))
                    B.inc(inst, tv, 1)
                tensor.wait_ge(vv.h, vb + 41)
                inst = tensor.matmul(ps_rep, B.onesr, xdbl_sb[0:1, :],
                                     start=True, stop=True)
                B.inc(inst, tv, 1)

        @decos["gpsimd"]
        def _(gpsimd):
            for g in range(NG):
                vb = vv0 + g * VPGm
                sbb = sv0 + g * SPGm
                t0c = g * GT
                for zc in range(16):
                    gpsimd.wait_ge(sv.h, sbb + 8 + 32 + zc + 1)
                    inst = gpsimd.dma_start(
                        out=B.dram["sz_fm"][zc * 128:(zc + 1) * 128,
                                            t0c:t0c + GT], in_=szb)
                    B.inc(inst, st, 16)
                    gpsimd.wait_ge(st.h, st.v)
                gpsimd.wait_ge(sv.h, sbb + 8 + 32 + 16 + 16)
                u3 = u_sb.rearrange("p (c t) -> p c t", c=NS)
                dst_u = B.dram["u_fm"].rearrange("(c p) t -> p c t", p=128)
                for q in range(4):
                    inst = gpsimd.dma_start(
                        out=dst_u[:, q * 4:(q + 1) * 4, t0c:t0c + GT],
                        in_=u3[:, q * 4:(q + 1) * 4, :])
                    B.inc(inst, st, 16)
                gpsimd.wait_ge(vv.h, vb + 41)
                inst = gpsimd.dma_start(out=B.dram["bc_fm"][:, t0c:t0c + GT],
                                        in_=xdbl_sb[1:33, :])
                B.inc(inst, st, 16)
                gpsimd.wait_ge(vv.h, vb + VPGm)
                dt3 = dt_sb.rearrange("p (c t) -> p c t", c=NS)
                dst_dt = B.dram["dt_fm"].rearrange("(c p) t -> p c t", p=128)
                du3 = du_sb.rearrange("p (c t) -> p c t", c=NS)
                dst_du = B.dram["du_fm"].rearrange("(c p) t -> p c t", p=128)
                for q in range(4):
                    inst = gpsimd.dma_start(
                        out=dst_dt[:, q * 4:(q + 1) * 4, t0c:t0c + GT],
                        in_=dt3[:, q * 4:(q + 1) * 4, :])
                    B.inc(inst, st, 16)
                for q in range(4):
                    inst = gpsimd.dma_start(
                        out=dst_du[:, q * 4:(q + 1) * 4, t0c:t0c + GT],
                        in_=du3[:, q * 4:(q + 1) * 4, :])
                    B.inc(inst, st, 16)
                gpsimd.wait_ge(st.h, st.v)

        finish()
    B.phase_idx += 1


def phase_scan(B):
    nc = B.nc
    ld, st, tv, vv, sv, gv = B.ld, B.st, B.tv, B.vv, B.sv, B.gv
    ld0, sv0, vv0, tv0, st0, gv0 = ld.v, sv.v, vv.v, tv.v, st.v, gv.v
    TS = TS_SCAN
    FB = TS * 256
    with (
        nc.sbuf_tensor("sc_dt", [128, 2 * NS * TS], F32) as dt_t,
        nc.sbuf_tensor("sc_du", [128, 2 * NS * TS], F32) as du_t,
        nc.sbuf_tensor("sc_b", [128, 2 * NS * TS], F32) as b_t,
        nc.sbuf_tensor("sc_c", [128, 2 * NS * TS], F32) as c_t,
        nc.sbuf_tensor("sc_z", [128, FB], F32) as zbuf,
        nc.sbuf_tensor("sc_e", [128, FB], F32) as ebuf,
        nc.sbuf_tensor("sc_p", [128, FB], F32) as pbuf,
        nc.sbuf_tensor("sc_h", [128, FB], F32) as htraj,
        nc.sbuf_tensor("sc_hc", [128, 256], F32) as hc,
        nc.sbuf_tensor("sc_t1", [128, 256], F32) as t1,
        nc.sbuf_tensor("sc_y", [128, 2 * TS * NS], F32) as ybuf,
        nc.sbuf_tensor("sc_y2", [128, 2 * TS * NS], F32) as ybuf2,
        nc.Block() as block,
    ):
        (dt_t, du_t, b_t, c_t, zbuf, ebuf, pbuf, htraj, hc, t1, ybuf,
         ybuf2) = [t.ap() for t in (dt_t, du_t, b_t, c_t, zbuf, ebuf, pbuf,
                                    htraj, hc, t1, ybuf, ybuf2)]
        decos, finish = barrier_wrap(B, block, None)

        VPC, GPC, SPC = 3, 2, 1

        @decos["sync"]
        def _(sync):
            for c in range(NCHUNK):
                pb = c % 2
                if c >= 2:
                    sync.wait_ge(vv.h, vv0 + (c - 1) * VPC)
                t0c = c * TS
                dt3 = dt_t[:, pb * NS * TS:(pb + 1) * NS * TS].rearrange(
                    "p (c t) -> p c t", c=NS)
                du3 = du_t[:, pb * NS * TS:(pb + 1) * NS * TS].rearrange(
                    "p (c t) -> p c t", c=NS)
                for q in range(4):
                    src = B.dram["dt_fm"].rearrange("(c p) t -> p c t", p=128)
                    sync.dma_start(out=dt3[:, q * 4:(q + 1) * 4, :],
                                   in_=src[:, q * 4:(q + 1) * 4, t0c:t0c + TS]
                                   ).then_inc(ld.h, 16)
                    src2 = B.dram["du_fm"].rearrange("(c p) t -> p c t", p=128)
                    sync.dma_start(out=du3[:, q * 4:(q + 1) * 4, :],
                                   in_=src2[:, q * 4:(q + 1) * 4, t0c:t0c + TS]
                                   ).then_inc(ld.h, 16)
                    ld.v += 32
                bsrc = B.dram["bc_fm"][0:16, t0c:t0c + TS]
                bsrc = bsrc.unsqueeze(0).broadcast_to([128, 16, TS])
                sync.dma_start(out=b_t[:, pb * NS * TS:(pb + 1) * NS * TS]
                               .rearrange("p (n t) -> p n t", n=NS),
                               in_=bsrc).then_inc(ld.h, 16)
                csrc = B.dram["bc_fm"][16:32, t0c:t0c + TS]
                csrc = csrc.unsqueeze(0).broadcast_to([128, 16, TS])
                sync.dma_start(out=c_t[:, pb * NS * TS:(pb + 1) * NS * TS]
                               .rearrange("p (n t) -> p n t", n=NS),
                               in_=csrc).then_inc(ld.h, 16)
                ld.v += 32

        @decos["vector"]
        def _(vector):
            for c in range(NCHUNK):
                pb = c % 2
                vector.wait_ge(ld.h, ld0 + max(c + 1, 2) * 160)
                dtv = dt_t[:, pb * NS * TS:(pb + 1) * NS * TS]
                duv = du_t[:, pb * NS * TS:(pb + 1) * NS * TS]
                bv = b_t[:, pb * NS * TS:(pb + 1) * NS * TS]
                cv_ = c_t[:, pb * NS * TS:(pb + 1) * NS * TS]
                z3 = zbuf.rearrange("p (t c n) -> p t c n", t=TS, c=NS)
                dt4 = dtv.rearrange("p (c t) -> p c t", c=NS)
                dt4 = dt4.transpose([0, 2, 1]).unsqueeze(3)
                dt4 = dt4.broadcast_to([128, TS, NS, NS])
                a4 = B.arep.unsqueeze(1).unsqueeze(1)
                a4 = a4.broadcast_to([128, TS, NS, NS])
                if c >= 1:
                    vector.wait_ge(gv.h, gv0 + (c - 1) * GPC + 1)
                vector.tensor_tensor(out=z3, in0=dt4, in1=a4, op=ALU.mult)
                if c >= 1:
                    vector.wait_ge(gv.h, gv0 + c * GPC)
                p3 = pbuf.rearrange("p (t c n) -> p t c n", t=TS, c=NS)
                du4 = duv.rearrange("p (c t) -> p c t", c=NS)
                du4 = du4.transpose([0, 2, 1]).unsqueeze(3)
                du4 = du4.broadcast_to([128, TS, NS, NS])
                b4 = bv.rearrange("p (n t) -> p n t", n=NS)
                b4 = b4.transpose([0, 2, 1]).unsqueeze(2)
                b4 = b4.broadcast_to([128, TS, NS, NS])
                vector.tensor_tensor(out=p3, in0=du4, in1=b4, op=ALU.mult)
                B.raw_inc(vector, vv, 1)
                vector.wait_ge(sv.h, sv0 + (c + 1) * SPC)
                vector.wait_ge(gv.h, gv0 + c * GPC + 2)
                if c == 0:
                    vector.memset(hc, 0.0)
                    vector.drain()
                for j in range(TS):
                    hsrc = hc if j == 0 else htraj[:, (j - 1) * 256:j * 256]
                    vector.tensor_tensor(out=t1, in0=hsrc,
                                         in1=zbuf[:, j * 256:(j + 1) * 256],
                                         op=ALU.mult)
                    vector.tensor_tensor(out=t1, in0=t1,
                                         in1=ebuf[:, j * 256:(j + 1) * 256],
                                         op=ALU.add)
                    vector.tensor_scalar(out=htraj[:, j * 256:(j + 1) * 256],
                                         in0=t1, scalar1=-100.0, scalar2=100.0,
                                         op0=ALU.max, op1=ALU.min)
                vector.tensor_copy(out=hc, in_=htraj[:, (TS - 1) * 256:TS * 256])
                B.raw_inc(vector, vv, 1)
                h4 = htraj.rearrange("p (t c n) -> p t c n", t=TS, c=NS)
                c4 = cv_.rearrange("p (n t) -> p n t", n=NS)
                c4 = c4.transpose([0, 2, 1]).unsqueeze(2)
                c4 = c4.broadcast_to([128, TS, NS, NS])
                vector.tensor_tensor(out=h4, in0=h4, in1=c4, op=ALU.mult)
                ypb = ybuf2[:, pb * TS * NS:(pb + 1) * TS * NS]
                B.red(vector, ypb.rearrange("p (t c) -> p t c", t=TS), h4,
                      ALU.add)
                yo = ybuf[:, pb * TS * NS:(pb + 1) * TS * NS]
                if c >= 2:
                    vector.wait_ge(st.h, st0 + (c - 1) * 64)
                vector.tensor_copy(
                    out=yo.rearrange("p (c t) -> p c t", c=NS),
                    in_=ypb.rearrange("p (t c) -> p c t", c=NS))
                B.raw_inc(vector, vv, 1)
            vv_final = vv.v

        @decos["gpsimd"]
        def _(gpsimd):
            for c in range(NCHUNK):
                pb = c % 2
                gpsimd.wait_ge(vv.h, vv0 + c * VPC + 1)
                gpsimd.tensor_single_scalar(out=ebuf, in_=zbuf, scalar=-20.0,
                                            op=ALU.max)
                gpsimd.drain()
                gpsimd.sem_inc(gv.h, 1)
                gv.v += 1
                gpsimd.wait_ge(sv.h, sv0 + (c + 1) * SPC)
                gpsimd.tensor_scalar(out=ebuf, in0=pbuf, scalar1=-10.0,
                                     scalar2=10.0, op0=ALU.max, op1=ALU.min)
                gpsimd.drain()
                gpsimd.sem_inc(gv.h, 1)
                gv.v += 1
                gpsimd.wait_ge(vv.h, vv0 + (c + 1) * VPC)
                yo = ybuf[:, pb * TS * NS:(pb + 1) * TS * NS].rearrange(
                    "p (c t) -> p c t", c=NS)
                t0c = c * TS
                dst = B.dram["y_fm"].rearrange("(c p) t -> p c t", p=128)
                for q in range(4):
                    inst = gpsimd.dma_start(
                        out=dst[:, q * 4:(q + 1) * 4, t0c:t0c + TS],
                        in_=yo[:, q * 4:(q + 1) * 4, :])
                    B.inc(inst, st, 16)

        @decos["scalar"]
        def _(scalar):
            for c in range(NCHUNK):
                scalar.wait_ge(gv.h, gv0 + c * GPC + 1)
                inst = scalar.activation(out=zbuf, in_=ebuf, func=ACT.Exp)
                B.inc(inst, sv, 1)

        finish()
    B.phase_idx += 1


def phase_mamba_out(B):
    nc = B.nc
    ld, st, tv, vv, sv = B.ld, B.st, B.tv, B.vv, B.sv
    inp = B.inp
    ld0, sv0, vv0, tv0, st0 = ld.v, sv.v, vv.v, tv.v, st.v
    G2 = 256
    NG2 = T // G2
    NSUB2 = G2 // 128
    with (
        nc.sbuf_tensor("po_y", [128, NS * G2], F32) as yg,
        nc.sbuf_tensor("po_u", [128, NS * G2], F32) as ug,
        nc.sbuf_tensor("po_sz", [128, NS * G2], F32) as szg,
        nc.sbuf_tensor("po_x", [128, NSUB2 * D], F32) as xg,
        nc.sbuf_tensor("po_w", [128, NS * D], F32) as wout,
        nc.sbuf_tensor("po_sq", [128, 2 * G2], F32) as sqb,
        nc.sbuf_tensor("po_stat", [1, 6 * G2], F32) as stats,
        nc.sbuf_tensor("po_rep", [128, 2 * G2], F32) as reps,
        nc.sbuf_tensor("po_g", [128, NS], F32) as lngc,
        nc.sbuf_tensor("po_b", [128, NS], F32) as lnbc,
        nc.sbuf_tensor("po_d", [128, NS], F32) as dcolc,
        nc.sbuf_tensor("po_out", [128, NSUB2 * D], F32) as outt,
        nc.psum_tensor("po_s1", [1, G2], F32) as ps_s1,
        nc.psum_tensor("po_s2", [1, G2], F32) as ps_s2,
        nc.psum_tensor("po_rep0", [128, G2], F32) as ps_rep0,
        nc.psum_tensor("po_rep1", [128, G2], F32) as ps_rep1,
        nc.psum_tensor("po_o0", [128, 512], F32) as ps_o0,
        nc.psum_tensor("po_o1", [128, 512], F32) as ps_o1,
        nc.Block() as block,
    ):
        (yg, ug, szg, xg, wout, sqb, stats, reps, lngc, lnbc, dcolc, outt,
         ps_s1, ps_s2, ps_rep0, ps_rep1, ps_o0, ps_o1) = [t.ap() for t in (
             yg, ug, szg, xg, wout, sqb, stats, reps, lngc, lnbc, dcolc, outt,
             ps_s1, ps_s2, ps_rep0, ps_rep1, ps_o0, ps_o1)]
        decos, finish = barrier_wrap(B, block, None)
        VPGo = 5
        SPGo = 19
        TPGo = 98
        LDG = 14 * 16

        @decos["sync"]
        def _(sync):
            for k in range(NS):
                sync.dma_start(out=wout[:, k * D:(k + 1) * D],
                               in_=inp["w_out_t"][k * 128:(k + 1) * 128, :]
                               ).then_inc(ld.h, 16)
            sync.dma_start(out=lngc, in_=inp["lng_col"]).then_inc(ld.h, 16)
            sync.dma_start(out=lnbc, in_=inp["lnb_col"]).then_inc(ld.h, 16)
            sync.dma_start(out=dcolc, in_=inp["d_col"]).then_inc(ld.h, 16)
            ld.v += 304
            for g in range(NG2):
                t0c = g * G2
                if g >= 1:
                    sync.wait_ge(vv.h, vv0 + g * VPGo)
                for nm, buf in (("y_fm", yg), ("u_fm", ug), ("sz_fm", szg)):
                    src = B.dram[nm].rearrange("(c p) t -> p c t", p=128)
                    b3 = buf.rearrange("p (c t) -> p c t", c=NS)
                    for q in range(4):
                        sync.dma_start(out=b3[:, q * 4:(q + 1) * 4, :],
                                       in_=src[:, q * 4:(q + 1) * 4,
                                               t0c:t0c + G2]).then_inc(ld.h, 16)
                        ld.v += 16
                for sb in range(NSUB2):
                    r0 = t0c + sb * 128
                    sync.dma_start(out=xg[:, sb * D:(sb + 1) * D],
                                   in_=B.dram["xres"][r0:r0 + 128, :]
                                   ).then_inc(ld.h, 16)
                    ld.v += 16

        @decos["scalar"]
        def _(scalar):
            for g in range(NG2):
                vb = vv0 + g * VPGo
                sbb = sv0 + g * SPGo
                tb = tv0 + g * TPGo
                scalar.wait_ge(ld.h, ld0 + 304 + (g + 1) * LDG)
                for dch in range(NS):
                    if dch >= 2:
                        scalar.wait_ge(tv.h, tb + 16 + dch - 1)
                    if g >= 1 and dch < 2:
                        scalar.wait_ge(tv.h, tv0 + g * TPGo)
                    inst = scalar.activation(
                        out=sqb[:, (dch % 2) * G2:(dch % 2 + 1) * G2],
                        in_=yg[:, dch * G2:(dch + 1) * G2], func=ACT.Square)
                    B.inc(inst, sv, 1)
                scalar.wait_ge(vv.h, vb + 1)
                scalar.activation(out=stats[:, 4 * G2:5 * G2],
                                  in_=stats[:, 3 * G2:4 * G2], func=ACT.Ln,
                                  bias=EPS)
                scalar.drain()
                scalar.activation(out=stats[:, 4 * G2:5 * G2],
                                  in_=stats[:, 4 * G2:5 * G2], func=ACT.Exp,
                                  scale=-0.5)
                scalar.drain()
                scalar.sem_inc(sv.h, 1)
                sv.v += 1
                scalar.wait_ge(tv.h, tb + 32 + 1)
                inst = scalar.activation(out=reps[:, 0:G2], in_=ps_rep0,
                                         func=ACT.Copy)
                B.inc(inst, sv, 1)
                scalar.wait_ge(tv.h, tb + 32 + 2)
                inst = scalar.activation(out=reps[:, G2:2 * G2], in_=ps_rep1,
                                         func=ACT.Copy)
                B.inc(inst, sv, 1)

        @decos["vector"]
        def _(vector):
            for g in range(NG2):
                vb = vv0 + g * VPGo
                sbb = sv0 + g * SPGo
                tb = tv0 + g * TPGo
                vector.wait_ge(tv.h, tb + 32)
                s1 = stats[0:1, 0:G2]
                s2 = stats[0:1, G2:2 * G2]
                mu = stats[0:1, 2 * G2:3 * G2]
                var = stats[0:1, 3 * G2:4 * G2]
                inv = stats[0:1, 4 * G2:5 * G2]
                nms = stats[0:1, 5 * G2:6 * G2]
                vector.tensor_copy(out=s1, in_=ps_s1)
                vector.tensor_copy(out=s2, in_=ps_s2)
                vector.drain()
                vector.tensor_single_scalar(out=mu, in_=s1, scalar=1.0 / DI,
                                            op=ALU.mult)
                vector.tensor_single_scalar(out=var, in_=s2, scalar=1.0 / DI,
                                            op=ALU.mult)
                vector.drain()
                vector.tensor_tensor(out=nms, in0=mu, in1=mu, op=ALU.mult)
                vector.drain()
                vector.tensor_tensor(out=var, in0=var, in1=nms, op=ALU.subtract)
                B.raw_inc(vector, vv, 1)
                vector.wait_ge(sv.h, sbb + 16 + 1)
                vector.tensor_tensor(out=nms, in0=mu, in1=inv, op=ALU.mult)
                vector.drain()
                vector.tensor_single_scalar(out=nms, in_=nms, scalar=-1.0,
                                            op=ALU.mult)
                B.raw_inc(vector, vv, 1)
                vector.wait_ge(sv.h, sbb + 16 + 1 + 2)
                y4 = yg.rearrange("p (c t) -> p c t", c=NS)
                inv_r = reps[:, 0:G2].unsqueeze(1).broadcast_to([128, NS, G2])
                nms_r = reps[:, G2:2 * G2].unsqueeze(1).broadcast_to(
                    [128, NS, G2])
                vector.tensor_tensor(out=y4, in0=y4, in1=inv_r, op=ALU.mult)
                vector.tensor_tensor(out=y4, in0=y4, in1=nms_r, op=ALU.add)
                lng3 = lngc.unsqueeze(2).broadcast_to([128, NS, G2])
                lnb3 = lnbc.unsqueeze(2).broadcast_to([128, NS, G2])
                vector.tensor_tensor(out=y4, in0=y4, in1=lng3, op=ALU.mult)
                vector.tensor_tensor(out=y4, in0=y4, in1=lnb3, op=ALU.add)
                u4 = ug.rearrange("p (c t) -> p c t", c=NS)
                d3 = dcolc.unsqueeze(2).broadcast_to([128, NS, G2])
                vector.tensor_tensor(out=u4, in0=u4, in1=d3, op=ALU.mult)
                vector.tensor_tensor(out=y4, in0=y4, in1=u4, op=ALU.add)
                sz4 = szg.rearrange("p (c t) -> p c t", c=NS)
                vector.tensor_tensor(out=y4, in0=y4, in1=sz4, op=ALU.mult)
                B.raw_inc(vector, vv, 1)
                if g >= 1:
                    vector.wait_ge(st.h, st0 + g * 32)
                for sb in range(NSUB2):
                    vector.wait_ge(tv.h, tb + 34 + (sb + 1) * 32)
                    vector.tensor_copy(out=outt[:, sb * D:sb * D + 512],
                                       in_=ps_o0)
                    vector.tensor_copy(out=outt[:, sb * D + 512:(sb + 1) * D],
                                       in_=ps_o1)
                    vector.drain()
                    vector.tensor_tensor(out=outt[:, sb * D:(sb + 1) * D],
                                         in0=outt[:, sb * D:(sb + 1) * D],
                                         in1=xg[:, sb * D:(sb + 1) * D],
                                         op=ALU.add)
                    B.raw_inc(vector, vv, 1)

        @decos["tensor"]
        def _(tensor):
            for g in range(NG2):
                vb = vv0 + g * VPGo
                sbb = sv0 + g * SPGo
                tb = tv0 + g * TPGo
                tensor.wait_ge(ld.h, ld0 + 304 + (g + 1) * LDG)
                for dch in range(NS):
                    inst = tensor.matmul(ps_s1, B.onesc,
                                         yg[:, dch * G2:(dch + 1) * G2],
                                         start=(dch == 0), stop=(dch == NS - 1))
                    B.inc(inst, tv, 1)
                for dch in range(NS):
                    tensor.wait_ge(sv.h, sbb + dch + 1)
                    inst = tensor.matmul(
                        ps_s2, B.onesc,
                        sqb[:, (dch % 2) * G2:(dch % 2 + 1) * G2],
                        start=(dch == 0), stop=(dch == NS - 1))
                    B.inc(inst, tv, 1)
                tensor.wait_ge(vv.h, vb + 2)
                inst = tensor.matmul(ps_rep0, B.onesr,
                                     stats[0:1, 4 * G2:5 * G2],
                                     start=True, stop=True)
                B.inc(inst, tv, 1)
                inst = tensor.matmul(ps_rep1, B.onesr,
                                     stats[0:1, 5 * G2:6 * G2],
                                     start=True, stop=True)
                B.inc(inst, tv, 1)
                tensor.wait_ge(vv.h, vb + 3)
                for sb in range(NSUB2):
                    if g + sb > 0:
                        prev = vb + 3 + sb if sb >= 1 else vb - VPGo + 5
                        tensor.wait_ge(vv.h, prev)
                    for k in range(NS):
                        stat = yg[:, k * G2 + sb * 128:k * G2 + (sb + 1) * 128]
                        inst = tensor.matmul(ps_o0, stat,
                                             wout[:, k * D:k * D + 512],
                                             start=(k == 0), stop=(k == NS - 1))
                        B.inc(inst, tv, 1)
                        inst = tensor.matmul(ps_o1, stat,
                                             wout[:, k * D + 512:(k + 1) * D],
                                             start=(k == 0), stop=(k == NS - 1))
                        B.inc(inst, tv, 1)

        @decos["gpsimd"]
        def _(gpsimd):
            for g in range(NG2):
                vb = vv0 + g * VPGo
                for sb in range(NSUB2):
                    gpsimd.wait_ge(vv.h, vb + 3 + sb + 1)
                    r0 = g * G2 + sb * 128
                    inst = gpsimd.dma_start(out=B.dram["xres"][r0:r0 + 128, :],
                                            in_=outt[:, sb * D:(sb + 1) * D])
                    B.inc(inst, st, 16)
                gpsimd.wait_ge(st.h, st.v)

        finish()
    B.phase_idx += 1


MOE_Q = 1024
MOE_NB = MOE_Q // 128
MOE_NE = 24


def _moe_stage1(B, src, q, tl):
    nc = B.nc
    ld, st, tv, vv, sv = B.ld, B.st, B.tv, B.vv, B.sv
    inp = B.inp
    ld0, sv0, vv0, tv0, st0 = ld.v, sv.v, vv.v, tv.v, st.v
    lnT, xg, lnx, qtmp, lnsc, rw_sb, cbs, lg_sb, sc_sb, rt, comb_sb, combT = (
        tl["lnT"], tl["xg"], tl["lnx"], tl["qtmp"], tl["lnsc"], tl["rw"],
        tl["cb"], tl["lg"], tl["sc"], tl["rt"], tl["comb"], tl["combT"])
    NB = MOE_NB
    first = (q == 0 and tl["load_consts"])

    import contextlib
    with contextlib.ExitStack() as _stk:
        def PP(nm, shp):
            return _stk.enter_context(
                nc.psum_tensor(nm + tl["sfx"] + f"_{q}", shp, F32)).ap()

        ps_tr0 = PP("m1tr0", [128, 128])
        ps_tr1 = PP("m1tr1", [128, 128])
        ps_lg = PP("m1lg", [128, 16])
        ps_ct = PP("m1ct", [16, 128])
        ps_tr_s = [ps_tr0, ps_tr1]
        block = _stk.enter_context(nc.Block())
        decos, finish = barrier_wrap(B, block, None)

        cst = 32 if first else 0

        def xw(blk):
            if blk < 2:
                return ld0 + cst + 32
            return ld0 + cst + 16 * (blk + 1)

        @decos["sync"]
        def _(sync):
            if first:
                sync.dma_start(out=rw_sb.rearrange("p (k e) -> p k e", k=8),
                               in_=inp["rw_t"].rearrange("(k p) e -> p k e",
                                                         p=128)
                               ).then_inc(ld.h, 16)
                sync.dma_start(out=cbs, in_=inp["cb_rep"]).then_inc(ld.h, 16)
                ld.v += 32
            for blk in range(NB):
                if blk >= 2:
                    sync.wait_ge(vv.h, vv0 + (blk - 2) * 4 + 2)
                r0 = q * MOE_Q + blk * 128
                sync.dma_start(out=xg[:, (blk % 2) * D:(blk % 2 + 1) * D],
                               in_=src[r0:r0 + 128, :]).then_inc(ld.h, 16)
                ld.v += 16

        @decos["scalar"]
        def _(scalar):
            for blk in range(NB):
                sbb = sv0 + blk * 12
                vb = vv0 + blk * 4
                tb = tv0 + blk * 17
                scalar.wait_ge(ld.h, xw(blk))
                if blk >= 2:
                    scalar.wait_ge(vv.h, vv0 + (blk - 2) * 4 + 1)
                inst = scalar.activation(
                    out=qtmp[:, (blk % 2) * D:(blk % 2 + 1) * D],
                    in_=xg[:, (blk % 2) * D:(blk % 2 + 1) * D],
                    func=ACT.Square)
                B.inc(inst, sv, 1)
                scalar.wait_ge(vv.h, vb + 1)
                scalar.activation(out=lnsc[:, 4:5], in_=lnsc[:, 3:4],
                                  func=ACT.Ln, bias=EPS)
                scalar.drain()
                scalar.activation(out=lnsc[:, 4:5], in_=lnsc[:, 4:5],
                                  func=ACT.Exp, scale=-0.5)
                scalar.drain()
                scalar.sem_inc(sv.h, 1)
                sv.v += 1
                for k in range(8):
                    scalar.wait_ge(tv.h, tb + k + 1)
                    inst = scalar.activation(
                        out=lnT[:, k * MOE_Q + blk * 128:
                                k * MOE_Q + (blk + 1) * 128],
                        in_=ps_tr_s[k % 2], func=ACT.Copy)
                    B.inc(inst, sv, 1)
                scalar.wait_ge(vv.h, vb + 3)
                inst = scalar.activation(out=sc_sb, in_=lg_sb,
                                         func=ACT.Sigmoid)
                B.inc(inst, sv, 1)
                scalar.wait_ge(tv.h, tb + 17)
                inst = scalar.activation(
                    out=combT[:, blk * 128:(blk + 1) * 128],
                    in_=ps_ct, func=ACT.Copy)
                B.inc(inst, sv, 1)

        @decos["vector"]
        def _(vector):
            for blk in range(NB):
                sbb = sv0 + blk * 12
                vb = vv0 + blk * 4
                tb = tv0 + blk * 17
                xs_ = xg[:, (blk % 2) * D:(blk % 2 + 1) * D]
                sq_ = qtmp[:, (blk % 2) * D:(blk % 2 + 1) * D]
                vector.wait_ge(ld.h, xw(blk))
                vector.wait_ge(sv.h, sbb + 1)
                B.ln_tile(vector, None, xs_, None, sq_, lnsc)
                B.raw_inc(vector, vv, 1)
                vector.wait_ge(sv.h, sbb + 2)
                if blk >= 2:
                    vector.wait_ge(tv.h, tv0 + (blk - 2) * 17 + 8)
                B.ln_apply(vector, xs_, lnx[:, (blk % 2) * D:(blk % 2 + 1) * D],
                           lnsc[:, 2:3], lnsc[:, 4:5], lnsc[:, 5:6])
                B.raw_inc(vector, vv, 1)
                vector.wait_ge(tv.h, tb + 16)
                if blk >= 1:
                    vector.wait_ge(sv.h, sv0 + (blk - 1) * 12 + 11)
                vector.tensor_copy(out=lg_sb, in_=ps_lg)
                B.raw_inc(vector, vv, 1)
                vector.wait_ge(sv.h, sbb + 11)
                if blk >= 1:
                    vector.wait_ge(tv.h, tv0 + (blk - 1) * 17 + 17)
                sbv = rt[:, 0:16]
                ptmp = rt[:, 16:40]
                gsc = rt[:, 40:44]
                g2 = rt[:, 44:48]
                g3 = rt[:, 48:52]
                mtmp = rt[:, 52:58]
                thr2 = rt[:, 58:59]
                gmb = rt[:, 59:63]
                sbm = rt[:, 64:80]
                cnt = rt[:, 80:96]
                sel = rt[:, 96:112]
                wsum = rt[:, 112:113]
                wrcp = rt[:, 113:114]
                vector.tensor_tensor(out=sbv, in0=sc_sb, in1=cbs, op=ALU.add)
                vector.drain()
                sb3 = sbv.rearrange("p (g e) -> p g e", g=4)
                pairs = [(0, 1), (0, 2), (0, 3), (1, 2), (1, 3), (2, 3)]
                for ix, (i, j) in enumerate(pairs):
                    vector.tensor_tensor(out=ptmp[:, ix * 4:(ix + 1) * 4],
                                         in0=sb3[:, :, i], in1=sb3[:, :, j],
                                         op=ALU.add)
                vector.drain()
                vector.tensor_tensor(out=gsc, in0=ptmp[:, 0:4],
                                     in1=ptmp[:, 4:8], op=ALU.max)
                vector.tensor_tensor(out=g2, in0=ptmp[:, 8:12],
                                     in1=ptmp[:, 12:16], op=ALU.max)
                vector.tensor_tensor(out=g3, in0=ptmp[:, 16:20],
                                     in1=ptmp[:, 20:24], op=ALU.max)
                vector.drain()
                vector.tensor_tensor(out=gsc, in0=gsc, in1=g2, op=ALU.max)
                vector.drain()
                vector.tensor_tensor(out=gsc, in0=gsc, in1=g3, op=ALU.max)
                vector.drain()
                for ix, (i, j) in enumerate(pairs):
                    vector.tensor_tensor(out=mtmp[:, ix:ix + 1],
                                         in0=gsc[:, i:i + 1],
                                         in1=gsc[:, j:j + 1], op=ALU.min)
                B.red(vector, thr2, mtmp, ALU.max)
                vector.tensor_scalar(out=gmb, in0=gsc, scalar1=thr2,
                                     scalar2=1.0, op0=ALU.is_ge,
                                     op1=ALU.subtract)
                vector.drain()
                vector.tensor_single_scalar(out=gmb, in_=gmb, scalar=1e30,
                                            op=ALU.mult)
                vector.drain()
                gm4 = gmb.unsqueeze(2).broadcast_to([128, 4, 4])
                vector.tensor_tensor(out=sbm.rearrange("p (g e) -> p g e", g=4),
                                     in0=sb3, in1=gm4, op=ALU.add)
                vector.drain()
                vector.tensor_scalar(out=cnt, in0=sbm, scalar1=sbm[:, 0:1],
                                     scalar2=None, op0=ALU.is_lt)
                vector.drain()
                for j in range(1, 16):
                    vector.scalar_tensor_tensor(out=cnt, in0=sbm,
                                                scalar=sbm[:, j:j + 1],
                                                in1=cnt, op0=ALU.is_lt,
                                                op1=ALU.add)
                    vector.drain()
                vector.tensor_single_scalar(out=sel, in_=cnt, scalar=3.5,
                                            op=ALU.is_le)
                vector.drain()
                vector.tensor_tensor(out=sel, in0=sel, in1=sc_sb, op=ALU.mult)
                B.red(vector, wsum, sel, ALU.add)
                vector.tensor_single_scalar(out=wsum, in_=wsum, scalar=1e-20,
                                            op=ALU.add)
                B.recip(vector, wrcp, wsum)
                vector.tensor_scalar(out=comb_sb, in0=sel, scalar1=wrcp,
                                     scalar2=None, op0=ALU.mult)
                B.raw_inc(vector, vv, 1)

        @decos["tensor"]
        def _(tensor):
            for blk in range(NB):
                sbb = sv0 + blk * 12
                vb = vv0 + blk * 4
                tb = tv0 + blk * 17
                tensor.wait_ge(vv.h, vb + 2)
                for k in range(8):
                    if k >= 2:
                        tensor.wait_ge(sv.h, sbb + 3 + (k - 2))
                    elif blk >= 1:
                        tensor.wait_ge(sv.h, sv0 + (blk - 1) * 12 + 10)
                    inst = tensor.transpose(
                        ps_tr_s[k % 2],
                        lnx[:, (blk % 2) * D + k * 128:
                            (blk % 2) * D + (k + 1) * 128], B.ids)
                    B.inc(inst, tv, 1)
                if blk >= 1:
                    tensor.wait_ge(vv.h, vv0 + (blk - 1) * 4 + 3)
                if first and blk == 0:
                    tensor.wait_ge(ld.h, ld0 + 64)
                for k in range(8):
                    tensor.wait_ge(sv.h, sbb + 3 + k)
                    inst = tensor.matmul(
                        ps_lg,
                        lnT[:, k * MOE_Q + blk * 128:
                            k * MOE_Q + (blk + 1) * 128],
                        rw_sb[:, k * 16:(k + 1) * 16],
                        start=(k == 0), stop=(k == 7))
                    B.inc(inst, tv, 1)
                tensor.wait_ge(vv.h, vb + 4)
                if blk >= 1:
                    tensor.wait_ge(sv.h, sv0 + (blk - 1) * 12 + 12)
                inst = tensor.transpose(ps_ct, comb_sb, B.ids)
                B.inc(inst, tv, 1)

        @decos["gpsimd"]
        def _(gpsimd):
            gpsimd.wait_ge(sv.h, sv0 + (NB - 1) * 12 + 12)
            inst = gpsimd.dma_start(
                out=B.dram["comb_fm"][:, q * MOE_Q:(q + 1) * MOE_Q],
                in_=combT)
            B.inc(inst, st, 16)
            gpsimd.wait_ge(st.h, st.v)

        finish()
    B.phase_idx += 1


def _moe_stage2(B, src, dst, q, tl):
    nc = B.nc
    ld, st, tv, vv, sv = B.ld, B.st, B.tv, B.vv, B.sv
    inp = B.inp
    ld0, sv0, vv0, tv0, st0 = ld.v, sv.v, vv.v, tv.v, st.v
    lnT, acc, gsb, usb, dsb, cfm, hfm, sil, htmp, xg2, outf = (
        tl["lnT"], tl["acc"], tl["gsb"], tl["usb"], tl["dsb"], tl["cfm"],
        tl["hfm"], tl["sil"], tl["htmp"], tl["xg2"], tl["outf"])
    NE = MOE_NE
    NB = MOE_NB

    t_gate, t_up, t_down = {}, {}, {}
    tp = tv0
    for e in range(NE):
        for tg in range(2):
            for hh in range(2):
                tp += 8
                t_gate[(e, tg, hh)] = tp
                tp += 8
                t_up[(e, tg, hh)] = tp
            for ts in range(4):
                tp += 4
                t_down[(e, tg, ts)] = tp
    s_silu = {}
    sp = sv0
    for e in range(NE):
        for tg in range(2):
            for hh in range(2):
                sp += 1
                s_silu[(e, tg, hh)] = sp
    v_h, v_acc, v_fin = {}, {}, {}
    vp = vv0
    for e in range(NE):
        for tg in range(2):
            for hh in range(2):
                vp += 1
                v_h[(e, tg, hh)] = vp
            for ts in range(4):
                vp += 1
                v_acc[(e, tg, ts)] = vp
    for blk in range(NB):
        vp += 1
        v_fin[blk] = vp
    ld_e, ld_x = {}, {}
    lp = ld0
    for e in range(NE):
        lp += 48 + (16 if e < 16 else 0)
        ld_e[e] = lp
    for blk in range(NB):
        lp += 16
        ld_x[blk] = lp
    st_fin = {}
    stp = st0
    for blk in range(NB):
        stp += 16
        st_fin[blk] = stp

    def ldw_e(e):
        return ld_e[1] if e < 2 else ld_e[e]

    def ldw_x(blk):
        return ld_x[1] if blk < 2 else ld_x[blk]

    import contextlib
    with contextlib.ExitStack() as _stk:
        def PP(nm, shp):
            return _stk.enter_context(
                nc.psum_tensor(nm + tl["sfx"] + f"_{q}", shp, F32)).ap()

        ps_g = [PP("m2g0", [128, 512]), PP("m2g1", [128, 512])]
        ps_u = [PP("m2u0", [128, 512]), PP("m2u1", [128, 512])]
        ps_o0 = PP("m2o0", [128, 512])
        ps_o1 = PP("m2o1", [128, 512])
        block = _stk.enter_context(nc.Block())
        decos, finish = barrier_wrap(B, block, None)
        gsb3 = gsb.rearrange("p (s k h) -> p s k h", s=2, k=8)
        usb3 = usb.rearrange("p (s k h) -> p s k h", s=2, k=8)
        dsb3 = dsb.rearrange("p (s c o) -> p s c o", s=2, c=2)
        hfm3 = hfm.rearrange("p (s h t) -> p s h t", s=2, h=2)

        @decos["sync"]
        def _(sync):
            for e in range(NE):
                if e >= 2:
                    sync.wait_ge(tv.h, t_down[(e - 2, 1, 3)])
                sl = e % 2
                if e < 16:
                    gs = inp["gate_w"][e].rearrange("(k p) h -> p k h", p=128)
                    us = inp["up_w"][e].rearrange("(k p) h -> p k h", p=128)
                    ds = inp["down_w"][e].rearrange("(c p) o -> p c o", p=128)
                else:
                    es = e - 16
                    gs = inp["sgw_t"][:, es * 256:(es + 1) * 256].rearrange(
                        "(k p) h -> p k h", p=128)
                    us = inp["suw_t"][:, es * 256:(es + 1) * 256].rearrange(
                        "(k p) h -> p k h", p=128)
                    ds = inp["sdw_t"][es * 256:(es + 1) * 256, :].rearrange(
                        "(c p) o -> p c o", p=128)
                sync.dma_start(out=gsb3[:, sl], in_=gs).then_inc(ld.h, 16)
                sync.dma_start(out=usb3[:, sl], in_=us).then_inc(ld.h, 16)
                sync.dma_start(out=dsb3[:, sl], in_=ds).then_inc(ld.h, 16)
                ld.v += 48
                if e < 16:
                    if e >= 2:
                        sync.wait_ge(vv.h, v_h[(e - 2, 1, 1)])
                    csrc = B.dram["comb_fm"][e:e + 1, q * MOE_Q:(q + 1) * MOE_Q]
                    csrc = csrc.broadcast_to([128, MOE_Q])
                    sync.dma_start(out=cfm[:, sl * MOE_Q:(sl + 1) * MOE_Q],
                                   in_=csrc).then_inc(ld.h, 16)
                    ld.v += 16
            for blk in range(NB):
                if blk >= 2:
                    sync.wait_ge(vv.h, v_fin[blk - 2])
                r0 = q * MOE_Q + blk * 128
                sync.dma_start(out=xg2[:, (blk % 2) * D:(blk % 2 + 1) * D],
                               in_=src[r0:r0 + 128, :]).then_inc(ld.h, 16)
                ld.v += 16

        @decos["tensor"]
        def _(tensor):
            for e in range(NE):
                tensor.wait_ge(ld.h, ldw_e(e))
                sl = e % 2
                for tg in range(2):
                    prev = (e, tg - 1) if tg >= 1 else \
                        ((e - 1, 1) if e >= 1 else None)
                    for hh in range(2):
                        if prev is not None:
                            tensor.wait_ge(sv.h,
                                           s_silu[(prev[0], prev[1], hh)])
                        for k in range(8):
                            inst = tensor.matmul(
                                ps_g[hh],
                                gsb3[:, sl, k, hh * 128:(hh + 1) * 128],
                                lnT[:, k * MOE_Q + tg * 512:
                                    k * MOE_Q + (tg + 1) * 512],
                                start=(k == 0), stop=(k == 7))
                            B.inc(inst, tv, 1)
                        if prev is not None:
                            tensor.wait_ge(vv.h,
                                           v_h[(prev[0], prev[1], hh)])
                        for k in range(8):
                            inst = tensor.matmul(
                                ps_u[hh],
                                usb3[:, sl, k, hh * 128:(hh + 1) * 128],
                                lnT[:, k * MOE_Q + tg * 512:
                                    k * MOE_Q + (tg + 1) * 512],
                                start=(k == 0), stop=(k == 7))
                            B.inc(inst, tv, 1)
                    tensor.wait_ge(vv.h, v_h[(e, tg, 1)])
                    for ts in range(4):
                        pts = (e, tg, ts - 1) if ts >= 1 else \
                            ((e, tg - 1, 3) if tg >= 1 else
                             ((e - 1, 1, 3) if e >= 1 else None))
                        if pts is not None:
                            tensor.wait_ge(vv.h, v_acc[pts])
                        for hh in range(2):
                            inst = tensor.matmul(
                                ps_o0, hfm3[:, tg, hh, ts * 128:(ts + 1) * 128],
                                dsb3[:, sl, hh, 0:512],
                                start=(hh == 0), stop=(hh == 1))
                            B.inc(inst, tv, 1)
                            inst = tensor.matmul(
                                ps_o1, hfm3[:, tg, hh, ts * 128:(ts + 1) * 128],
                                dsb3[:, sl, hh, 512:1024],
                                start=(hh == 0), stop=(hh == 1))
                            B.inc(inst, tv, 1)

        @decos["scalar"]
        def _(scalar):
            for e in range(NE):
                for tg in range(2):
                    prev = (e, tg - 1) if tg >= 1 else \
                        ((e - 1, 1) if e >= 1 else None)
                    for hh in range(2):
                        scalar.wait_ge(tv.h, t_gate[(e, tg, hh)])
                        if prev is not None:
                            scalar.wait_ge(vv.h,
                                           v_h[(prev[0], prev[1], hh)])
                        inst = scalar.activation(
                            out=sil[:, hh * 512:(hh + 1) * 512],
                            in_=ps_g[hh], func=ACT.Silu)
                        B.inc(inst, sv, 1)

        @decos["vector"]
        def _(vector):
            for e in range(NE):
                sl = e % 2
                for tg in range(2):
                    for hh in range(2):
                        vector.wait_ge(sv.h, s_silu[(e, tg, hh)])
                        vector.wait_ge(tv.h, t_up[(e, tg, hh)])
                        if hh == 0 and e >= 1:
                            vector.wait_ge(tv.h, t_down[(e - 1, tg, 3)])
                        dstc = hfm3[:, tg, hh, :]
                        if e < 16:
                            vector.wait_ge(ld.h, ldw_e(e))
                            vector.tensor_tensor(
                                out=htmp, in0=sil[:, hh * 512:(hh + 1) * 512],
                                in1=ps_u[hh], op=ALU.mult)
                            vector.tensor_tensor(
                                out=dstc, in0=htmp,
                                in1=cfm[:, sl * MOE_Q + tg * 512:
                                        sl * MOE_Q + (tg + 1) * 512],
                                op=ALU.mult)
                        else:
                            vector.tensor_tensor(
                                out=dstc, in0=sil[:, hh * 512:(hh + 1) * 512],
                                in1=ps_u[hh], op=ALU.mult)
                        B.raw_inc(vector, vv, 1)
                    for ts in range(4):
                        vector.wait_ge(tv.h, t_down[(e, tg, ts)])
                        col = (tg * 4 + ts) * D
                        if e == 0:
                            vector.tensor_copy(out=acc[:, col:col + 512],
                                               in_=ps_o0)
                            vector.tensor_copy(out=acc[:, col + 512:col + 1024],
                                               in_=ps_o1)
                        else:
                            vector.tensor_tensor(out=acc[:, col:col + 512],
                                                 in0=acc[:, col:col + 512],
                                                 in1=ps_o0, op=ALU.add)
                            vector.tensor_tensor(
                                out=acc[:, col + 512:col + 1024],
                                in0=acc[:, col + 512:col + 1024],
                                in1=ps_o1, op=ALU.add)
                        B.raw_inc(vector, vv, 1)
            for blk in range(NB):
                vector.wait_ge(ld.h, ldw_x(blk))
                if blk >= 2:
                    vector.wait_ge(st.h, st_fin[blk - 2])
                vector.scalar_tensor_tensor(
                    out=outf[:, (blk % 2) * D:(blk % 2 + 1) * D],
                    in0=acc[:, blk * D:(blk + 1) * D], scalar=0.5,
                    in1=xg2[:, (blk % 2) * D:(blk % 2 + 1) * D],
                    op0=ALU.mult, op1=ALU.add)
                B.raw_inc(vector, vv, 1)

        @decos["gpsimd"]
        def _(gpsimd):
            for blk in range(NB):
                gpsimd.wait_ge(vv.h, v_fin[blk])
                r0 = q * MOE_Q + blk * 128
                inst = gpsimd.dma_start(
                    out=dst[r0:r0 + 128, :],
                    in_=outf[:, (blk % 2) * D:(blk % 2 + 1) * D])
                B.inc(inst, st, 16)
                gpsimd.wait_ge(st.h, st.v)

        finish()
    B.phase_idx += 1


def phase_moe(B, src, dst):
    nc = B.nc
    sfx = f"_{B.phase_idx}"
    import contextlib
    with contextlib.ExitStack() as _stk:
        def A(nm, shp):
            return _stk.enter_context(nc.sbuf_tensor(nm + sfx, shp, F32)).ap()

        tl = {
            "sfx": sfx,
            "load_consts": True,
            "lnT": A("mo_lnT", [128, 8 * MOE_Q]),
            "acc": A("mo_acc", [128, MOE_NB * D]),
            "xg": A("mo_x", [128, 2 * D]),
            "lnx": A("mo_ln", [128, 2 * D]),
            "qtmp": A("mo_sq", [128, 2 * D]),
            "lnsc": A("mo_st", [128, 8]),
            "rw": A("mo_rw", [128, 128]),
            "cb": A("mo_cb", [128, 16]),
            "lg": A("mo_lg", [128, 16]),
            "sc": A("mo_sc", [128, 16]),
            "rt": A("mo_rt", [128, 128]),
            "comb": A("mo_comb", [128, 16]),
            "combT": A("mo_combT", [16, MOE_Q]),
            "gsb": A("mo_gw", [128, 2 * 8 * 256]),
            "usb": A("mo_uw", [128, 2 * 8 * 256]),
            "dsb": A("mo_dw", [128, 2 * 2 * D]),
            "cfm": A("mo_cfm", [128, 2 * MOE_Q]),
            "hfm": A("mo_hfm", [128, 2 * 2 * 512]),
            "sil": A("mo_sil", [128, 2 * 512]),
            "htmp": A("mo_ht", [128, 512]),
            "xg2": A("mo_x2", [128, 2 * D]),
            "outf": A("mo_out", [128, 2 * D]),
        }
        for q in range(4):
            _moe_stage1(B, src, q, tl)
            _moe_stage2(B, src, dst, q, tl)
            tl["load_consts"] = False


def _attn_qkv(B, tl):
    nc = B.nc
    ld, st, tv, vv, sv = B.ld, B.st, B.tv, B.vv, B.sv
    inp = B.inp
    ld0, sv0, vv0, tv0, st0 = ld.v, sv.v, vv.v, tv.v, st.v
    wq_sb, wk_sb, wv_sb, lnT, qs, ks, vs, xg, lnx, qtmp, lnsc, svrow = (
        tl["wq"], tl["wk"], tl["wv"], tl["lnT"], tl["qs"], tl["ks"], tl["vs"],
        tl["xg"], tl["lnx"], tl["qtmp"], tl["lnsc"], tl["svrow"])
    NGR = 8

    s_sq, s_inv, s_ev, s_unit = {}, {}, {}, {}
    sp = sv0
    for g in range(NGR):
        for sb in range(4):
            u = g * 4 + sb
            sp += 1
            s_sq[u] = sp
            sp += 1
            s_inv[u] = sp
            if sb >= 1:
                for k in range(8):
                    sp += 1
                    s_ev[(g * 4 + sb - 1, k)] = sp
        for k in range(8):
            sp += 1
            s_ev[(g * 4 + 3, k)] = sp
        for j in range(24):
            sp += 1
            s_unit[(g, j)] = sp
    sv_fin = sp + 2
    v_stats, v_apply = {}, {}
    vp = vv0
    for u in range(4 * NGR):
        vp += 1
        v_stats[u] = vp
        vp += 1
        v_apply[u] = vp
    t_tr, t_trd, t_unit, t_sv = {}, {}, {}, {}
    tp = tv0
    for g in range(NGR):
        for sb in range(4):
            u = g * 4 + sb
            for k in range(8):
                tp += 1
                t_tr[(u, k)] = tp
            t_trd[u] = tp
        for j in range(24):
            tp += 8
            t_unit[(g, j)] = tp
        for idx in range(8):
            tp += 1
            t_sv[(g, idx)] = tp
    st_q, st_k, st_v = {}, {}, {}
    stp = st0
    for g in range(NGR):
        stp += 16
        st_q[g] = stp
        stp += 16
        st_k[g] = stp
        for ts in range(4):
            stp += 16
            st_v[(g, ts)] = stp

    def xw(u):
        return ld0 + 80 if u < 2 else ld0 + 48 + 16 * (u + 1)

    import contextlib
    with contextlib.ExitStack() as _stk:
        def PP(nm, shp):
            return _stk.enter_context(nc.psum_tensor(nm, shp, F32)).ap()

        ps_tr = [PP("aqtr0", [128, 128]), PP("aqtr1", [128, 128])]
        ps_mm = [PP(f"aqmm{i}", [128, 512]) for i in range(4)]
        ps_sv = [PP("aqsv0", [1, 512]), PP("aqsv1", [1, 512])]
        block = _stk.enter_context(nc.Block())
        decos, finish = barrier_wrap(B, block, None)
        wq3 = wq_sb.rearrange("p (k o) -> p k o", k=8)
        wk3 = wk_sb.rearrange("p (k o) -> p k o", k=8)
        wv3 = wv_sb.rearrange("p (k o) -> p k o", k=8)

        @decos["sync"]
        def _(sync):
            sync.dma_start(out=wq3, in_=inp["wq"].rearrange(
                "(k p) o -> p k o", p=128)).then_inc(ld.h, 16)
            sync.dma_start(out=wk3, in_=inp["wk"].rearrange(
                "(k p) o -> p k o", p=128)).then_inc(ld.h, 16)
            sync.dma_start(out=wv3, in_=inp["wv"].rearrange(
                "(k p) o -> p k o", p=128)).then_inc(ld.h, 16)
            ld.v += 48
            for u in range(4 * NGR):
                if u >= 2:
                    sync.wait_ge(vv.h, v_apply[u - 2])
                r0 = u * 128
                sync.dma_start(out=xg[:, (u % 2) * D:(u % 2 + 1) * D],
                               in_=B.dram["xres"][r0:r0 + 128, :]
                               ).then_inc(ld.h, 16)
                ld.v += 16

        @decos["scalar"]
        def _(scalar):
            def emit_ev(scalar, u):
                g, sb = u // 4, u % 4
                for k in range(8):
                    scalar.wait_ge(tv.h, t_tr[(u, k)])
                    if k == 0 and sb == 0 and g >= 1:
                        scalar.wait_ge(tv.h, t_unit[(g - 1, 23)])
                    inst = scalar.activation(
                        out=lnT[:, k * 512 + sb * 128:k * 512 + (sb + 1) * 128],
                        in_=ps_tr[k % 2], func=ACT.Copy)
                    B.inc(inst, sv, 1)

            for g in range(NGR):
                for sb in range(4):
                    u = g * 4 + sb
                    scalar.wait_ge(ld.h, xw(u))
                    if u >= 1:
                        scalar.wait_ge(vv.h, v_stats[u - 1])
                    inst = scalar.activation(
                        out=qtmp, in_=xg[:, (u % 2) * D:(u % 2 + 1) * D],
                        func=ACT.Square)
                    B.inc(inst, sv, 1)
                    scalar.wait_ge(vv.h, v_stats[u])
                    scalar.activation(out=lnsc[:, 4:5], in_=lnsc[:, 3:4],
                                      func=ACT.Ln, bias=EPS)
                    scalar.drain()
                    scalar.activation(out=lnsc[:, 4:5], in_=lnsc[:, 4:5],
                                      func=ACT.Exp, scale=-0.5)
                    scalar.drain()
                    scalar.sem_inc(sv.h, 1)
                    sv.v += 1
                    if sb >= 1:
                        emit_ev(scalar, u - 1)
                emit_ev(scalar, g * 4 + 3)
                for j in range(24):
                    scalar.wait_ge(tv.h, t_unit[(g, j)])
                    if g >= 1:
                        if j == 0:
                            scalar.wait_ge(st.h, st_q[g - 1])
                        elif j == 8:
                            scalar.wait_ge(st.h, st_k[g - 1])
                        elif j >= 16 and (j - 16) % 2 == 0:
                            scalar.wait_ge(st.h, st_v[(g - 1, (j - 16) // 2)])
                    if j < 8:
                        dstc = qs[:, j * 512:(j + 1) * 512]
                    elif j < 16:
                        dstc = ks[:, (j - 8) * 512:(j - 7) * 512]
                    else:
                        ts, hf = (j - 16) // 2, (j - 16) % 2
                        dstc = vs[:, ts * 1024 + hf * 512:
                                  ts * 1024 + (hf + 1) * 512]
                    inst = scalar.activation(out=dstc, in_=ps_mm[j % 4],
                                             func=ACT.Copy)
                    B.inc(inst, sv, 1)
            scalar.wait_ge(tv.h, t_sv[(NGR - 1, 7)])
            inst = scalar.activation(out=svrow[0:1, 0:512], in_=ps_sv[0],
                                     func=ACT.Copy)
            B.inc(inst, sv, 1)
            inst = scalar.activation(out=svrow[0:1, 512:1024], in_=ps_sv[1],
                                     func=ACT.Copy)
            B.inc(inst, sv, 1)

        @decos["vector"]
        def _(vector):
            for u in range(4 * NGR):
                xs_ = xg[:, (u % 2) * D:(u % 2 + 1) * D]
                vector.wait_ge(ld.h, xw(u))
                vector.wait_ge(sv.h, s_sq[u])
                B.ln_tile(vector, None, xs_, None, qtmp, lnsc)
                B.raw_inc(vector, vv, 1)
                vector.wait_ge(sv.h, s_inv[u])
                if u >= 2:
                    vector.wait_ge(tv.h, t_trd[u - 2])
                B.ln_apply(vector, xs_, lnx[:, (u % 2) * D:(u % 2 + 1) * D],
                           lnsc[:, 2:3], lnsc[:, 4:5], lnsc[:, 5:6])
                B.raw_inc(vector, vv, 1)

        @decos["tensor"]
        def _(tensor):
            tensor.wait_ge(ld.h, ld0 + 80)
            for g in range(NGR):
                for sb in range(4):
                    u = g * 4 + sb
                    tensor.wait_ge(vv.h, v_apply[u])
                    for k in range(8):
                        if k >= 2:
                            tensor.wait_ge(sv.h, s_ev[(u, k - 2)])
                        elif u >= 1:
                            tensor.wait_ge(sv.h, s_ev[(u - 1, 6 + k)])
                        inst = tensor.transpose(
                            ps_tr[k % 2],
                            lnx[:, (u % 2) * D + k * 128:
                                (u % 2) * D + (k + 1) * 128], B.ids)
                        B.inc(inst, tv, 1)
                tensor.wait_ge(sv.h, s_ev[(g * 4 + 3, 7)])
                for j in range(24):
                    gp = (g, j - 4)
                    if j < 4:
                        gp = (g - 1, 20 + j) if g >= 1 else None
                    if gp is not None:
                        tensor.wait_ge(sv.h, s_unit[gp])
                    for k in range(8):
                        if j < 8:
                            lhs = wq3[:, k, j * 128:(j + 1) * 128]
                            rhs = lnT[:, k * 512:(k + 1) * 512]
                        elif j < 16:
                            lhs = wk3[:, k, (j - 8) * 128:(j - 7) * 128]
                            rhs = lnT[:, k * 512:(k + 1) * 512]
                        else:
                            ts, hf = (j - 16) // 2, (j - 16) % 2
                            lhs = lnT[:, k * 512 + ts * 128:
                                      k * 512 + (ts + 1) * 128]
                            rhs = wv3[:, k, hf * 512:(hf + 1) * 512]
                        inst = tensor.matmul(ps_mm[j % 4], lhs, rhs,
                                             start=(k == 0), stop=(k == 7))
                        B.inc(inst, tv, 1)
                for idx in range(8):
                    ts, hf = idx // 2, idx % 2
                    tensor.wait_ge(sv.h, s_unit[(g, 16 + idx)])
                    inst = tensor.matmul(
                        ps_sv[hf], B.onesc,
                        vs[:, ts * 1024 + hf * 512:ts * 1024 + (hf + 1) * 512],
                        start=(g == 0 and ts == 0), stop=(g == NGR - 1 and
                                                          ts == 3))
                    B.inc(inst, tv, 1)

        @decos["gpsimd"]
        def _(gpsimd):
            for g in range(NGR):
                t0c = g * 512
                gpsimd.wait_ge(sv.h, s_unit[(g, 7)])
                dstq = B.dram["q_fm"].rearrange("(c p) t -> p c t", p=128)
                inst = gpsimd.dma_start(
                    out=dstq[:, :, t0c:t0c + 512],
                    in_=qs.rearrange("p (c t) -> p c t", c=8))
                B.inc(inst, st, 16)
                gpsimd.wait_ge(st.h, st.v)
                gpsimd.wait_ge(sv.h, s_unit[(g, 15)])
                dstk = B.dram["k_fm"].rearrange("(c p) t -> p c t", p=128)
                inst = gpsimd.dma_start(
                    out=dstk[:, :, t0c:t0c + 512],
                    in_=ks.rearrange("p (c t) -> p c t", c=8))
                B.inc(inst, st, 16)
                gpsimd.wait_ge(st.h, st.v)
                for ts in range(4):
                    gpsimd.wait_ge(sv.h, s_unit[(g, 16 + ts * 2 + 1)])
                    r0 = t0c + ts * 128
                    inst = gpsimd.dma_start(
                        out=B.dram["v_tok"][r0:r0 + 128, :],
                        in_=vs[:, ts * 1024:(ts + 1) * 1024])
                    B.inc(inst, st, 16)
                    gpsimd.wait_ge(st.h, st.v)

        finish()
    B.phase_idx += 1


def _attn_core(B, tl):
    nc = B.nc
    ld, st, tv, vv, sv = B.ld, B.st, B.tv, B.vv, B.sv
    inp = B.inp
    ld0, sv0, vv0, tv0, st0 = ld.v, sv.v, vv.v, tv.v, st.v
    qh_sb, kh_sb, vh_sb, svrow, svrep, msk, e_sb, et_sb, e0T, o_blk, mt = (
        tl["qh"], tl["kh"], tl["vh"], tl["svrow"], tl["svrep"], tl["msk"],
        tl["e_sb"], tl["et_sb"], tl["e0T"], tl["o_blk"], tl["mt"])
    NH, NQ = 16, 32

    def b0_of(qi):
        return min(max((qi - 1) * 128, 0), T - 384)

    def mvar(qi):
        return 0 if qi == 0 else (2 if qi == NQ - 1 else 1)

    def prev_it(h, qi):
        if qi >= 1:
            return (h, qi - 1)
        return (h - 1, NQ - 1) if h >= 1 else None

    t_m, t_band, t_tr, t_av = {}, {}, {}, {}
    tp = tv0 + 2
    for h in range(NH):
        for qi in range(NQ):
            for ch in range(8):
                tp += 1
                t_m[(h, qi, ch)] = tp
            tp += 1
            t_band[(h, qi)] = tp
            for c in range(3):
                tp += 1
                t_tr[(h, qi, c)] = tp
            tp += 3
            t_av[(h, qi)] = tp
    s_e, s_ze, s_trev = {}, {}, {}
    sp = sv0 + 2
    for h in range(NH):
        for qi in range(NQ):
            sp += 1
            s_e[(h, qi)] = sp
            sp += 1
            s_ze[(h, qi)] = sp
            for c in range(3):
                sp += 1
                s_trev[(h, qi, c)] = sp
    v_red, v_mneg, v_m2, v_e0, v_comb = {}, {}, {}, {}, {}
    vp = vv0
    for h in range(NH):
        for qi in range(NQ):
            for ch in range(8):
                vp += 1
                v_red[(h, qi, ch)] = vp
            vp += 1
            v_mneg[(h, qi)] = vp
            vp += 1
            v_m2[(h, qi)] = vp
            vp += 1
            v_e0[(h, qi)] = vp
            vp += 1
            v_comb[(h, qi)] = vp
    st_o = {}
    stp = st0
    for h in range(NH):
        for qi in range(NQ):
            stp += 16
            st_o[(h, qi)] = stp

    def ld_h(h):
        return ld0 + 112 if h < 2 else ld0 + 16 + 48 * (h + 1)

    import contextlib
    with contextlib.ExitStack() as _stk:
        def PP(nm, shp):
            return _stk.enter_context(nc.psum_tensor(nm, shp, F32)).ap()

        ps_m = [PP("acm0", [128, 512]), PP("acm1", [128, 512])]
        ps_b = PP("acb", [128, 384])
        ps_tr = [PP("actr0", [128, 128]), PP("actr1", [128, 128])]
        ps_av = PP("acav", [128, 64])
        ps_rep = PP("acrep", [128, 512])
        block = _stk.enter_context(nc.Block())
        decos, finish = barrier_wrap(B, block, None)
        vh3 = vh_sb.rearrange("p (s c f) -> p s c f", s=2, c=32)

        @decos["sync"]
        def _(sync):
            sync.dma_start(out=msk.rearrange("p (v c) -> p v c", v=3),
                           in_=inp["win_mask"].rearrange("v p c -> p v c")
                           ).then_inc(ld.h, 16)
            ld.v += 16
            for h in range(NH):
                if h >= 2:
                    sync.wait_ge(tv.h, t_av[(h - 2, NQ - 1)])
                sl = h % 2
                sync.dma_start(out=qh_sb[:, sl * T:(sl + 1) * T],
                               in_=B.dram["q_fm"][h * 64:(h + 1) * 64, :]
                               ).then_inc(ld.h, 16)
                sync.dma_start(out=kh_sb[:, sl * T:(sl + 1) * T],
                               in_=B.dram["k_fm"][h * 64:(h + 1) * 64, :]
                               ).then_inc(ld.h, 16)
                sync.dma_start(out=vh3[:, sl],
                               in_=B.dram["v_tok"][:, h * 64:(h + 1) * 64]
                               .rearrange("(c p) f -> p c f", p=128)
                               ).then_inc(ld.h, 16)
                ld.v += 48

        @decos["tensor"]
        def _(tensor):
            inst = tensor.matmul(ps_rep, B.onesr, svrow[0:1, 0:512],
                                 start=True, stop=True)
            B.inc(inst, tv, 1)
            tensor.wait_ge(sv.h, sv0 + 1)
            inst = tensor.matmul(ps_rep, B.onesr, svrow[0:1, 512:1024],
                                 start=True, stop=True)
            B.inc(inst, tv, 1)
            for h in range(NH):
                tensor.wait_ge(ld.h, ld_h(h))
                sl = h % 2
                for qi in range(NQ):
                    pit = prev_it(h, qi)
                    qcol = qh_sb[:, sl * T + qi * 128:sl * T + (qi + 1) * 128]
                    for ch in range(8):
                        if ch >= 2:
                            tensor.wait_ge(vv.h, v_red[(h, qi, ch - 2)])
                        elif pit is not None:
                            tensor.wait_ge(vv.h, v_red[(pit[0], pit[1],
                                                        6 + ch)])
                        inst = tensor.matmul(
                            ps_m[ch % 2], qcol,
                            kh_sb[:, sl * T + ch * 512:sl * T + (ch + 1) * 512],
                            start=True, stop=True)
                        B.inc(inst, tv, 1)
                    if pit is not None:
                        tensor.wait_ge(sv.h, s_e[pit])
                    b0 = b0_of(qi)
                    inst = tensor.matmul(ps_b, qcol,
                                         kh_sb[:, sl * T + b0:sl * T + b0 + 384],
                                         start=True, stop=True)
                    B.inc(inst, tv, 1)
                    for c in range(3):
                        if c == 0:
                            tensor.wait_ge(vv.h, v_e0[(h, qi)])
                        if c >= 2:
                            tensor.wait_ge(sv.h, s_trev[(h, qi, c - 2)])
                        elif pit is not None:
                            tensor.wait_ge(sv.h, s_trev[(pit[0], pit[1],
                                                         c + 1)])
                        inst = tensor.transpose(
                            ps_tr[c % 2], e_sb[:, c * 128:(c + 1) * 128],
                            B.ids)
                        B.inc(inst, tv, 1)
                    for c in range(3):
                        tensor.wait_ge(sv.h, s_trev[(h, qi, c)])
                        if c == 0 and pit is not None:
                            tensor.wait_ge(vv.h, v_comb[pit])
                        bc = b0 // 128 + c
                        inst = tensor.matmul(ps_av,
                                             e0T[:, c * 128:(c + 1) * 128],
                                             vh3[:, sl, bc, :],
                                             start=(c == 0), stop=(c == 2))
                        B.inc(inst, tv, 1)

        @decos["scalar"]
        def _(scalar):
            scalar.wait_ge(tv.h, tv0 + 1)
            inst = scalar.activation(out=svrep[:, 0:512], in_=ps_rep,
                                     func=ACT.Copy)
            B.inc(inst, sv, 1)
            scalar.wait_ge(tv.h, tv0 + 2)
            inst = scalar.activation(out=svrep[:, 512:1024], in_=ps_rep,
                                     func=ACT.Copy)
            B.inc(inst, sv, 1)
            mneg = mt[:, 9:10]
            m2neg = mt[:, 11:12]
            for h in range(NH):
                for qi in range(NQ):
                    scalar.wait_ge(tv.h, t_band[(h, qi)])
                    scalar.wait_ge(vv.h, v_mneg[(h, qi)])
                    inst = scalar.activation(out=e_sb, in_=ps_b, func=ACT.Exp,
                                             scale=0.125, bias=mneg)
                    B.inc(inst, sv, 1)
                    scalar.wait_ge(vv.h, v_m2[(h, qi)])
                    scalar.activation(out=mt[:, 12:13], in_=m2neg,
                                      func=ACT.Exp)
                    scalar.drain()
                    inst = scalar.activation(out=et_sb, in_=e_sb, func=ACT.Exp,
                                             bias=m2neg)
                    B.inc(inst, sv, 1)
                    for c in range(3):
                        scalar.wait_ge(tv.h, t_tr[(h, qi, c)])
                        inst = scalar.activation(
                            out=e0T[:, c * 128:(c + 1) * 128],
                            in_=ps_tr[c % 2], func=ACT.Copy)
                        B.inc(inst, sv, 1)

        @decos["vector"]
        def _(vector):
            vector.wait_ge(ld.h, ld0 + 112)
            mtmp = mt[:, 0:8]
            mraw = mt[:, 8:9]
            mneg = mt[:, 9:10]
            m2 = mt[:, 10:11]
            m2neg = mt[:, 11:12]
            z = mt[:, 12:13]
            den = mt[:, 13:14]
            den2 = mt[:, 14:15]
            rcp = mt[:, 15:16]
            for h in range(NH):
                for qi in range(NQ):
                    for ch in range(8):
                        vector.wait_ge(tv.h, t_m[(h, qi, ch)])
                        B.red(vector, mtmp[:, ch:ch + 1], ps_m[ch % 2],
                              ALU.max)
                        B.raw_inc(vector, vv, 1)
                    B.red(vector, mraw, mtmp, ALU.max)
                    vector.tensor_single_scalar(out=mneg, in_=mraw,
                                                scalar=-0.125, op=ALU.mult)
                    B.raw_inc(vector, vv, 1)
                    vector.wait_ge(sv.h, s_e[(h, qi)])
                    mv = mvar(qi)
                    vector.tensor_tensor(out=e_sb, in0=e_sb,
                                         in1=msk[:, mv * 384:(mv + 1) * 384],
                                         op=ALU.mult)
                    B.red(vector, m2, e_sb, ALU.max)
                    vector.tensor_single_scalar(out=m2neg, in_=m2,
                                                scalar=-1.0, op=ALU.mult)
                    B.raw_inc(vector, vv, 1)
                    vector.wait_ge(sv.h, s_ze[(h, qi)])
                    vector.tensor_scalar(out=e_sb, in0=et_sb, scalar1=z,
                                         scalar2=None, op0=ALU.subtract)
                    B.red(vector, den, e_sb, ALU.add)
                    vector.scalar_tensor_tensor(out=den2, in0=z,
                                                scalar=float(T), in1=den,
                                                op0=ALU.mult, op1=ALU.add)
                    B.recip(vector, rcp, den2)
                    B.raw_inc(vector, vv, 1)
                    vector.wait_ge(tv.h, t_av[(h, qi)])
                    pit2 = (h, qi - 2)
                    if qi < 2:
                        pit2 = (h - 1, NQ - 2 + qi) if h >= 1 else None
                    if pit2 is not None:
                        vector.wait_ge(st.h, st_o[pit2])
                    ob = o_blk[:, (qi % 2) * 64:(qi % 2 + 1) * 64]
                    vector.scalar_tensor_tensor(
                        out=ob, in0=svrep[:, h * 64:(h + 1) * 64], scalar=z,
                        in1=ps_av, op0=ALU.mult, op1=ALU.add)
                    vector.drain()
                    vector.tensor_scalar(out=ob, in0=ob, scalar1=rcp,
                                         scalar2=None, op0=ALU.mult)
                    B.raw_inc(vector, vv, 1)

        @decos["gpsimd"]
        def _(gpsimd):
            for h in range(NH):
                for qi in range(NQ):
                    gpsimd.wait_ge(vv.h, v_comb[(h, qi)])
                    inst = gpsimd.dma_start(
                        out=B.dram["o_tok"][qi * 128:(qi + 1) * 128,
                                            h * 64:(h + 1) * 64],
                        in_=o_blk[:, (qi % 2) * 64:(qi % 2 + 1) * 64])
                    B.inc(inst, st, 16)
                    gpsimd.wait_ge(st.h, st.v)

        finish()
    B.phase_idx += 1


def _attn_wo(B, tl):
    nc = B.nc
    ld, st, tv, vv, sv = B.ld, B.st, B.tv, B.vv, B.sv
    inp = B.inp
    ld0, sv0, vv0, tv0, st0 = ld.v, sv.v, vv.v, tv.v, st.v
    wo_sb, og, xg, oT, outf = (tl["wo"], tl["og"], tl["xg2"], tl["oT"],
                               tl["outf"])
    NQ = 32

    t_tr, t_half = {}, {}
    tp = tv0
    for qi in range(NQ):
        for k in range(8):
            tp += 1
            t_tr[(qi, k)] = tp
        for hf in range(2):
            tp += 8
            t_half[(qi, hf)] = tp
    s_ev = {}
    sp = sv0
    for qi in range(NQ):
        for k in range(8):
            sp += 1
            s_ev[(qi, k)] = sp
    v_fin = {}
    vp = vv0
    for qi in range(NQ):
        vp += 1
        v_fin[qi] = vp
    st_c = {}
    stp = st0
    for qi in range(NQ):
        stp += 16
        st_c[qi] = stp

    def ld_qi(qi):
        return ld0 + 80 if qi < 2 else ld0 + 16 + 32 * (qi + 1)

    import contextlib
    with contextlib.ExitStack() as _stk:
        def PP(nm, shp):
            return _stk.enter_context(nc.psum_tensor(nm, shp, F32)).ap()

        ps_tr = [PP("awtr0", [128, 128]), PP("awtr1", [128, 128])]
        ps_wo = [PP("awm0", [128, 512]), PP("awm1", [128, 512])]
        block = _stk.enter_context(nc.Block())
        decos, finish = barrier_wrap(B, block, None)
        wo3 = wo_sb.rearrange("p (k o) -> p k o", k=8)

        @decos["sync"]
        def _(sync):
            sync.dma_start(out=wo3, in_=inp["wo"].rearrange(
                "(k p) o -> p k o", p=128)).then_inc(ld.h, 16)
            ld.v += 16
            for qi in range(NQ):
                if qi >= 2:
                    sync.wait_ge(vv.h, v_fin[qi - 2])
                r0 = qi * 128
                sync.dma_start(out=og[:, (qi % 2) * D:(qi % 2 + 1) * D],
                               in_=B.dram["o_tok"][r0:r0 + 128, :]
                               ).then_inc(ld.h, 16)
                sync.dma_start(out=xg[:, (qi % 2) * D:(qi % 2 + 1) * D],
                               in_=B.dram["xres"][r0:r0 + 128, :]
                               ).then_inc(ld.h, 16)
                ld.v += 32

        @decos["tensor"]
        def _(tensor):
            for qi in range(NQ):
                tensor.wait_ge(ld.h, ld_qi(qi))
                for k in range(8):
                    if k >= 2:
                        tensor.wait_ge(sv.h, s_ev[(qi, k - 2)])
                    elif qi >= 1:
                        tensor.wait_ge(sv.h, s_ev[(qi - 1, 6 + k)])
                    inst = tensor.transpose(
                        ps_tr[k % 2],
                        og[:, (qi % 2) * D + k * 128:(qi % 2) * D +
                           (k + 1) * 128], B.ids)
                    B.inc(inst, tv, 1)
                for hf in range(2):
                    if hf == 0 and qi >= 1:
                        tensor.wait_ge(vv.h, v_fin[qi - 1])
                    for k in range(8):
                        if hf == 0:
                            tensor.wait_ge(sv.h, s_ev[(qi, k)])
                        inst = tensor.matmul(
                            ps_wo[hf], oT[:, k * 128:(k + 1) * 128],
                            wo3[:, k, hf * 512:(hf + 1) * 512],
                            start=(k == 0), stop=(k == 7))
                        B.inc(inst, tv, 1)

        @decos["scalar"]
        def _(scalar):
            for qi in range(NQ):
                for k in range(8):
                    scalar.wait_ge(tv.h, t_tr[(qi, k)])
                    inst = scalar.activation(out=oT[:, k * 128:(k + 1) * 128],
                                             in_=ps_tr[k % 2], func=ACT.Copy)
                    B.inc(inst, sv, 1)

        @decos["vector"]
        def _(vector):
            for qi in range(NQ):
                vector.wait_ge(ld.h, ld_qi(qi))
                if qi >= 2:
                    vector.wait_ge(st.h, st_c[qi - 2])
                xs_ = xg[:, (qi % 2) * D:(qi % 2 + 1) * D]
                of = outf[:, (qi % 2) * D:(qi % 2 + 1) * D]
                vector.wait_ge(tv.h, t_half[(qi, 0)])
                vector.tensor_tensor(out=of[:, 0:512], in0=ps_wo[0],
                                     in1=xs_[:, 0:512], op=ALU.add)
                vector.wait_ge(tv.h, t_half[(qi, 1)])
                vector.tensor_tensor(out=of[:, 512:1024], in0=ps_wo[1],
                                     in1=xs_[:, 512:1024], op=ALU.add)
                B.raw_inc(vector, vv, 1)

        @decos["gpsimd"]
        def _(gpsimd):
            for qi in range(NQ):
                gpsimd.wait_ge(vv.h, v_fin[qi])
                r0 = qi * 128
                inst = gpsimd.dma_start(
                    out=B.dram["xres"][r0:r0 + 128, :],
                    in_=outf[:, (qi % 2) * D:(qi % 2 + 1) * D])
                B.inc(inst, st, 16)
                gpsimd.wait_ge(st.h, st.v)

        finish()
    B.phase_idx += 1


def phase_attn(B):
    nc = B.nc
    import contextlib
    svrow = nc.alloc_sbuf_tensor("at_svrow", [1, D], F32).ap()
    with contextlib.ExitStack() as _stk:
        def A(nm, shp, pt=128):
            return _stk.enter_context(
                nc.sbuf_tensor(nm, [pt] + shp, F32)).ap()

        tl = {
            "wq": A("at_wq", [8 * D]),
            "wk": A("at_wk", [8 * D]),
            "wv": A("at_wv", [8 * D]),
            "lnT": A("at_lnT", [8 * 512]),
            "qs": A("at_qs", [8 * 512]),
            "ks": A("at_ks", [8 * 512]),
            "vs": A("at_vs", [4 * D]),
            "xg": A("at_x", [2 * D]),
            "lnx": A("at_ln", [2 * D]),
            "qtmp": A("at_sq", [D]),
            "lnsc": A("at_st", [8]),
            "svrow": svrow,
        }
        _attn_qkv(B, tl)
    with contextlib.ExitStack() as _stk:
        def A(nm, shp, pt=128):
            return _stk.enter_context(
                nc.sbuf_tensor(nm, [pt] + shp, F32)).ap()

        tl2 = {
            "qh": A("at_qh", [2 * T], pt=64),
            "kh": A("at_kh", [2 * T], pt=64),
            "vh": A("at_vh", [2 * 32 * 64]),
            "svrow": svrow,
            "svrep": A("at_svrep", [D]),
            "msk": A("at_msk", [3 * 384]),
            "e_sb": A("at_e", [384]),
            "et_sb": A("at_et", [384]),
            "e0T": A("at_e0T", [384]),
            "o_blk": A("at_ob", [2 * 64]),
            "mt": A("at_mt", [16]),
        }
        _attn_core(B, tl2)
    with contextlib.ExitStack() as _stk:
        def A(nm, shp, pt=128):
            return _stk.enter_context(
                nc.sbuf_tensor(nm, [pt] + shp, F32)).ap()

        tl3 = {
            "wo": A("at_wo", [8 * D]),
            "og": A("at_og", [2 * D]),
            "xg2": A("at_x2", [2 * D]),
            "oT": A("at_oT", [D]),
            "outf": A("at_of", [2 * D]),
        }
        _attn_wo(B, tl3)


def phase_copy_out(B):
    nc = B.nc
    ld, st, vv = B.ld, B.st, B.vv
    ld0, st0 = B.ld.v, B.st.v
    NT = T // 128
    with (
        nc.sbuf_tensor("pc_t", [128, 2 * D], F32) as tile,
        nc.Block() as block,
    ):
        tile = tile.ap()
        decos, finish = barrier_wrap(B, block, None)

        @decos["sync"]
        def _(sync):
            for i in range(NT):
                if i >= 2:
                    sync.wait_ge(st.h, st0 + 16 * (i - 1))
                inst = sync.dma_start(out=tile[:, (i % 2) * D:(i % 2 + 1) * D],
                                      in_=B.dram["xres"][i * 128:(i + 1) * 128, :])
                B.inc(inst, ld, 16)
                sync.wait_ge(ld.h, ld.v)

        @decos["gpsimd"]
        def _(gpsimd):
            for i in range(NT):
                gpsimd.wait_ge(ld.h, ld0 + 16 * (i + 1))
                inst = gpsimd.dma_start(out=B.dram["out_x"][i * 128:(i + 1) * 128, :],
                                        in_=tile[:, (i % 2) * D:(i % 2 + 1) * D])
                B.inc(inst, st, 16)
                gpsimd.wait_ge(st.h, st.v)

        finish()
    B.phase_idx += 1


def host_layout(inputs):
    L = {}
    L["w_in_t"] = np.ascontiguousarray(inputs["L0_attn_W_in"].T)
    L["conv_w_l"] = np.ascontiguousarray(
        inputs["L0_attn_conv_w"][:, 0, :].reshape(NS, 128, 4).transpose(1, 0, 2))
    L["conv_b_l"] = np.ascontiguousarray(inputs["L0_attn_conv_b"].reshape(NS, 128).T)
    L["w_x_t"] = np.ascontiguousarray(inputs["L0_attn_W_x"].T)
    L["wdt_l"] = np.ascontiguousarray(inputs["L0_attn_W_dt"][:, 0].reshape(NS, 128).T)
    L["bdt_l"] = np.ascontiguousarray(inputs["L0_attn_b_dt"].reshape(NS, 128).T)
    L["alog_c"] = np.ascontiguousarray(np.tile(inputs["L0_attn_A_log"], 8)[:, None])
    L["d_col"] = np.ascontiguousarray(inputs["L0_attn_D_param"].reshape(NS, 128).T)
    L["lng_col"] = np.ascontiguousarray(inputs["L0_attn_ln_g"].reshape(NS, 128).T)
    L["lnb_col"] = np.ascontiguousarray(inputs["L0_attn_ln_b"].reshape(NS, 128).T)
    L["w_out_t"] = np.ascontiguousarray(inputs["L0_attn_W_out"].T)
    L["ffn_w"] = np.ascontiguousarray(inputs["L0_ffn_weight"])
    L["ffn_b_col"] = np.ascontiguousarray(inputs["L0_ffn_bias"].reshape(8, 128).T)
    L["wq"] = np.ascontiguousarray(inputs["L1_attn_Wq"])
    L["wk"] = np.ascontiguousarray(inputs["L1_attn_Wk"])
    L["wv"] = np.ascontiguousarray(inputs["L1_attn_Wv"])
    L["wo"] = np.ascontiguousarray(inputs["L1_attn_Wo"])
    L["rw_t"] = np.ascontiguousarray(inputs["L1_ffn_router_w"].T)
    L["cb_rep"] = np.ascontiguousarray(
        np.tile(inputs["L1_ffn_correction_bias"][None, :], (128, 1)))
    L["gate_w"] = np.ascontiguousarray(inputs["L1_ffn_gate_w"])
    L["up_w"] = np.ascontiguousarray(inputs["L1_ffn_up_w"])
    L["down_w"] = np.ascontiguousarray(inputs["L1_ffn_down_w"])
    L["sgw_t"] = np.ascontiguousarray(inputs["L1_ffn_shared_gate_w"].T)
    L["suw_t"] = np.ascontiguousarray(inputs["L1_ffn_shared_up_w"].T)
    L["sdw_t"] = np.ascontiguousarray(inputs["L1_ffn_shared_down_w"].T)
    L["ident"] = np.eye(128, dtype=np.float32)
    L["ones_row"] = np.ones((1, 128), dtype=np.float32)
    L["ones_col"] = np.ones((128, 1), dtype=np.float32)
    first = np.zeros((128, 384), np.float32)
    for p in range(128):
        first[p, 0:p + 128] = 1.0
    mid = np.zeros((128, 384), np.float32)
    for p in range(128):
        mid[p, p:p + 256] = 1.0
    last = np.zeros((128, 384), np.float32)
    for p in range(128):
        last[p, p + 128:384] = 1.0
    L["win_mask"] = np.ascontiguousarray(np.stack([first, mid, last]))
    return L


_N_BUILD = [6]
_SKIP_QUANT = [False]
_SKIP_LN = [False]


def kernel(**inputs):
    L = host_layout(inputs)
    x = np.ascontiguousarray(inputs["L0_attn_x"], dtype=np.float32)
    nc, _ = build_program(_N_BUILD[0])
    in_maps = []
    for b in range(8):
        m = dict(L)
        m["x_in"] = np.ascontiguousarray(x[b])
        in_maps.append(m)
    res = run_bass_kernel_spmd(nc, in_maps, core_ids=list(range(8)))
    return np.stack([r["out_x"] for r in res.results]).astype(np.float32)


# version 02
# speedup vs baseline: 1.4874x; 1.4874x over previous
import numpy as np

import concourse.bass as bass
import concourse.mybir as mybir
from concourse.bass_utils import run_bass_kernel_spmd

F32 = mybir.dt.float32
AX = mybir.AxisListType
ALU = mybir.AluOpType
ACT = mybir.ActivationFunctionType

_DEBUG_OUT = [False]
T = 4096
D = 1024
DI = 2048
NS = 16
NG = 8
GT = T // NG
NSUB = GT // 128
EPS = 1e-5
TS_SCAN = 32
NCHUNK = T // TS_SCAN


class Sem:
    def __init__(self, h):
        self.h = h
        self.v = 0


class Build:

    def __init__(self, nc, n_sublayers=6):
        self.nc = nc
        self.n_sublayers = n_sublayers
        self.phase_idx = 0

    def inc(self, inst, sem, n=1):
        inst.then_inc(sem.h, n)
        sem.v += n

    def raw_inc(self, eng, sem, n=1):
        eng.drain()
        eng.sem_inc(sem.h, n)
        sem.v += n

    def red(self, vector, out, in_, op, absval=False):
        vector.drain()
        vector.tensor_reduce(out=out, in_=in_, axis=AX.X, op=op,
                             apply_absolute_value=absval)
        vector.drain()

    def recip(self, vector, out, in_):
        vector.drain()
        vector.reciprocal(out=out, in_=in_)
        vector.drain()

    def ln_tile(self, vector, scalar_ops, x, out, sq, sc):
        F = x.shape[1]
        s_sum = sc[:, 0:1]
        s_ssq = sc[:, 1:2]
        s_mu = sc[:, 2:3]
        s_var = sc[:, 3:4]
        s_inv = sc[:, 4:5]
        s_nms = sc[:, 5:6]
        self.red(vector, s_sum, x, ALU.add)
        self.red(vector, s_ssq, sq, ALU.add)
        vector.tensor_single_scalar(out=s_mu, in_=s_sum, scalar=1.0 / F, op=ALU.mult)
        vector.tensor_single_scalar(out=s_var, in_=s_ssq, scalar=1.0 / F, op=ALU.mult)
        vector.drain()
        vector.tensor_tensor(out=s_nms, in0=s_mu, in1=s_mu, op=ALU.mult)
        vector.drain()
        vector.tensor_tensor(out=s_var, in0=s_var, in1=s_nms, op=ALU.subtract)
        vector.drain()
        return s_var, s_inv, s_mu, s_nms

    def ln_apply(self, vector, x, out, s_mu, s_inv, s_nms):
        vector.tensor_tensor(out=s_nms, in0=s_mu, in1=s_inv, op=ALU.mult)
        vector.drain()
        vector.tensor_single_scalar(out=s_nms, in_=s_nms, scalar=-1.0, op=ALU.mult)
        vector.drain()
        vector.tensor_scalar(out=out, in0=x, scalar1=s_inv, scalar2=s_nms,
                             op0=ALU.mult, op1=ALU.add)

    def quant_pre(self, vector, src, v, qsc, F):
        nb = F // 32
        amax = qsc[:, 0:nb]
        scale = qsc[:, nb:2 * nb]
        iscl = qsc[:, 2 * nb:3 * nb]
        src3 = src.rearrange("p (b c) -> p b c", c=32)
        self.red(vector, amax, src3, ALU.max, absval=True)
        vector.tensor_scalar(out=scale, in0=amax, scalar1=1.0 / 7.5,
                             scalar2=1e-30, op0=ALU.mult, op1=ALU.max)
        self.recip(vector, iscl, scale)
        isb = iscl.unsqueeze(2).broadcast_to([128, nb, 32])
        vector.tensor_tensor(out=v.rearrange("p (b c) -> p b c", c=32),
                             in0=src3, in1=isb, op=ALU.mult)

    def quant_post(self, vector, v, sq, c1, qsc, F, dst):
        nb = F // 32
        scale = qsc[:, nb:2 * nb]
        vector.tensor_single_scalar(out=c1, in_=sq, scalar=4.0, op=ALU.is_ge)
        vector.tensor_single_scalar(out=sq, in_=sq, scalar=16.0, op=ALU.is_ge)
        vector.scalar_tensor_tensor(out=sq, in0=sq, scalar=0.5, in1=c1,
                                    op0=ALU.mult, op1=ALU.add)
        vector.tensor_scalar(out=c1, in0=sq, scalar1=-4.0, scalar2=8.0,
                             op0=ALU.mult, op1=ALU.add)
        vector.tensor_tensor(out=v, in0=v, in1=c1, op=ALU.mult)
        vector.tensor_single_scalar(out=c1, in_=sq, scalar=1.5, op=ALU.is_ge)
        vector.tensor_single_scalar(out=sq, in_=sq, scalar=1.0, op=ALU.is_ge)
        vector.scalar_tensor_tensor(out=sq, in0=c1, scalar=2.0, in1=sq,
                                    op0=ALU.mult, op1=ALU.add)
        vector.tensor_scalar(out=c1, in0=sq, scalar1=0.125, scalar2=0.125,
                             op0=ALU.mult, op1=ALU.add)
        vector.tensor_single_scalar(out=sq, in_=v, scalar=16.5, op=ALU.add)
        for dlev in (32.0, 16.0, 8.0, 4.0, 2.0, 1.0):
            vector.tensor_scalar(out=dst, in0=sq, scalar1=dlev, scalar2=dlev,
                                 op0=ALU.is_gt, op1=ALU.mult)
            vector.tensor_tensor(out=sq, in0=sq, in1=dst, op=ALU.subtract)
        vector.scalar_tensor_tensor(out=sq, in0=sq, scalar=-1.0, in1=v,
                                    op0=ALU.mult, op1=ALU.add)
        vector.tensor_single_scalar(out=sq, in_=sq, scalar=0.5, op=ALU.add)
        vector.tensor_tensor(out=sq, in0=sq, in1=c1, op=ALU.mult)
        scb = scale.unsqueeze(2).broadcast_to([128, nb, 32])
        vector.tensor_tensor(out=dst.rearrange("p (b c) -> p b c", c=32),
                             in0=sq.rearrange("p (b c) -> p b c", c=32),
                             in1=scb, op=ALU.mult)


def build_program(n_sublayers=6):
    nc = bass.Bass("TRN2", target_bir_lowering=False, debug=False,
                   detect_race_conditions=False)
    B = Build(nc, n_sublayers)

    inp = {}

    def I(name, shape):
        inp[name] = nc.dram_tensor(name, list(shape), F32, kind="ExternalInput").ap()
        return inp[name]

    x_in = I("x_in", (T, D))
    w_in_t = I("w_in_t", (D, 2 * DI))
    conv_w_l = I("conv_w_l", (128, NS, 4))
    conv_b_l = I("conv_b_l", (128, NS))
    w_x_t = I("w_x_t", (DI, 33))
    wdt_l = I("wdt_l", (128, NS))
    bdt_l = I("bdt_l", (128, NS))
    alog_c = I("alog_c", (128, 1))
    d_col = I("d_col", (128, NS))
    lng_col = I("lng_col", (128, NS))
    lnb_col = I("lnb_col", (128, NS))
    w_out_t = I("w_out_t", (DI, D))
    ffn_w = I("ffn_w", (D, D))
    ffn_b_col = I("ffn_b_col", (128, 8))
    wq_i = I("wq", (D, D))
    wk_i = I("wk", (D, D))
    wv_i = I("wv", (D, D))
    wo_i = I("wo", (D, D))
    rw_t = I("rw_t", (D, 16))
    cb_rep = I("cb_rep", (128, 16))
    gate_w = I("gate_w", (16, D, 256))
    up_w = I("up_w", (16, D, 256))
    down_w = I("down_w", (16, 256, D))
    sgw_t = I("sgw_t", (D, DI))
    suw_t = I("suw_t", (D, DI))
    sdw_t = I("sdw_t", (DI, D))
    ident = I("ident", (128, 128))
    ones_row = I("ones_row", (1, 128))
    ones_col = I("ones_col", (128, 1))
    win_mask = I("win_mask", (3, 128, 384))

    out_x = nc.dram_tensor("out_x", [T, D], F32, kind="ExternalOutput").ap()

    _dk = "ExternalOutput" if _DEBUG_OUT[0] else "Internal"
    xres = nc.dram_tensor("xres", [T, D], F32, kind=_dk).ap()
    u_fm = nc.dram_tensor("u_fm", [DI, T], F32, kind="Internal").ap()
    dt_fm = nc.dram_tensor("dt_fm", [DI, T], F32, kind="Internal").ap()
    du_fm = nc.dram_tensor("du_fm", [DI, T], F32, kind="Internal").ap()
    sz_fm = nc.dram_tensor("sz_fm", [DI, T], F32, kind="Internal").ap()
    y_fm = nc.dram_tensor("y_fm", [DI, T], F32, kind="Internal").ap()
    bc_fm = nc.dram_tensor("bc_fm", [32, T], F32, kind="Internal").ap()
    wdq_t = nc.dram_tensor("wdq_t", [D, D], F32, kind=_dk).ap()
    dbg = nc.dram_tensor("dbg", [128, 4 * D], F32, kind=_dk).ap()

    q_fm = nc.dram_tensor("q_fm", [D, T], F32, kind="Internal").ap()
    k_fm = nc.dram_tensor("k_fm", [D, T], F32, kind="Internal").ap()
    v_tok = nc.dram_tensor("v_tok", [T, D], F32, kind="Internal").ap()
    o_tok = nc.dram_tensor("o_tok", [T, D], F32, kind="Internal").ap()
    comb_fm = nc.dram_tensor("comb_fm", [16, T], F32, kind="Internal").ap()

    c_eps = nc.alloc_sbuf_tensor("c_eps", [128, 1], F32).ap()
    nc.const_aps.register(F32, EPS, c_eps)
    ids_g = nc.alloc_sbuf_tensor("ids_g", [128, 128], F32).ap()
    onesr_g = nc.alloc_sbuf_tensor("onesr_g", [1, 128], F32).ap()
    onesc_g = nc.alloc_sbuf_tensor("onesc_g", [128, 1], F32).ap()
    acol_g = nc.alloc_sbuf_tensor("acol_g", [128, 1], F32).ap()
    arep_g = nc.alloc_sbuf_tensor("arep_g", [128, 16], F32).ap()
    B.dram = dict(dbg=dbg, x_in=x_in, xres=xres, out_x=out_x, u_fm=u_fm, dt_fm=dt_fm,
                  du_fm=du_fm, sz_fm=sz_fm, y_fm=y_fm, bc_fm=bc_fm, wdq_t=wdq_t,
                  q_fm=q_fm, k_fm=k_fm, v_tok=v_tok, o_tok=o_tok, comb_fm=comb_fm)
    B.inp = inp
    B.ids = ids_g
    B.c_eps = c_eps
    B.onesr = onesr_g
    B.onesc = onesc_g
    B.acol = acol_g
    B.arep = arep_g

    with (
        nc.semaphore("bar") as h_bar,
        nc.semaphore("s_ld") as h_ld,
        nc.semaphore("s_st") as h_st,
        nc.semaphore("s_t") as h_t,
        nc.semaphore("s_v") as h_v,
        nc.semaphore("s_s") as h_s,
        nc.semaphore("s_g") as h_g,
    ):
        B.bar = Sem(h_bar)
        B.ld = Sem(h_ld)
        B.st = Sem(h_st)
        B.tv = Sem(h_t)
        B.vv = Sem(h_v)
        B.sv = Sem(h_s)
        B.gv = Sem(h_g)

        phase_a(B)
        phase_ffn(B, src=x_in, dst=xres)
        if n_sublayers >= 1.3:
            phase_mamba_in(B)
        if n_sublayers >= 1.6:
            phase_scan(B)
        if n_sublayers >= 2:
            phase_mamba_out(B)
        if n_sublayers >= 3:
            phase_ffn(B, src=xres, dst=xres)
        if n_sublayers >= 4:
            phase_moe(B, src=xres, dst=xres)
        if n_sublayers >= 5:
            phase_attn(B)
        if n_sublayers >= 6:
            phase_moe(B, src=xres, dst=out_x)
            copy_done = True
        else:
            phase_copy_out(B)

    return nc, inp


def barrier_wrap(B, block, used):
    k = B.phase_idx
    bar = B.bar
    emitted = set()

    def mk(name, attr):
        def deco(f):
            def wrapped(eng):
                eng.wait_ge(bar.h, 5 * k)
                f(eng)
                eng.drain()
                eng.sem_inc(bar.h, 1)
            attr(wrapped)
            emitted.add(name)
        return deco

    decos = {n: mk(n, getattr(block, n))
             for n in ("sync", "tensor", "vector", "scalar", "gpsimd")}

    def finish():
        for n in ("sync", "tensor", "vector", "scalar", "gpsimd"):
            if n not in emitted:
                def idle(eng):
                    pass
                decos[n](idle)
        B.bar.v += 5

    return decos, finish


def phase_a(B):
    nc = B.nc
    ld, st, tv, vv, sv = B.ld, B.st, B.tv, B.vv, B.sv
    inp = B.inp
    ld0, sv0, vv0, tv0, st0 = ld.v, sv.v, vv.v, tv.v, st.v
    with (
        nc.sbuf_tensor("pa_w", [128, D], F32) as wtile,
        nc.sbuf_tensor("pa_wq", [128, D], F32) as wq_s,
        nc.sbuf_tensor("pa_sc", [128, 96], F32) as qsc,
        nc.sbuf_tensor("pa_tmp", [128, 3 * D], F32) as qtmp,
        nc.sbuf_tensor("pa_al", [128, 1], F32) as al_s,
        nc.sbuf_tensor("pa_ar", [1, 16], F32) as arow,
        nc.sbuf_tensor("pa_tr", [128, 2 * 128], F32) as tr_s,
        nc.psum_tensor("pa_ps", [128, 128], F32) as ps_tr,
        nc.psum_tensor("pa_ps2", [128, 16], F32) as ps_rep,
        nc.Block() as block,
    ):
        wtile, wq_s, qsc, qtmp, al_s, arow, tr_s, ps_tr, ps_rep = [
            t.ap() for t in (wtile, wq_s, qsc, qtmp, al_s, arow, tr_s,
                             ps_tr, ps_rep)]
        decos, finish = barrier_wrap(B, block, None)

        @decos["sync"]
        def _(sync):
            sync.dma_start(out=B.ids, in_=inp["ident"]).then_inc(ld.h, 16)
            sync.dma_start(out=B.onesr, in_=inp["ones_row"]).then_inc(ld.h, 16)
            sync.dma_start(out=B.onesc, in_=inp["ones_col"]).then_inc(ld.h, 16)
            sync.dma_start(out=al_s, in_=inp["alog_c"]).then_inc(ld.h, 16)
            ld.v += 64
            for r in range(8):
                if r >= 1:
                    sync.wait_ge(vv.h, vv0 + 1 + 2 * r)
                sync.dma_start(out=wtile, in_=inp["ffn_w"][r * 128:(r + 1) * 128, :]
                               ).then_inc(ld.h, 16)
                ld.v += 16

        def sq_val_v(r):
            return 2 if r == 0 else 3 + 9 * (r - 1)

        @decos["vector"]
        def _(vector):
            vector.wait_ge(ld.h, ld0 + 80)
            vector.memset(B.c_eps, EPS)
            vector.tensor_single_scalar(out=B.acol, in_=al_s, scalar=5.0, op=ALU.min)
            B.raw_inc(vector, vv, 1)
            for r in range(8):
                vector.wait_ge(ld.h, ld0 + 80 + 16 * r)
                B.quant_pre(vector, wtile, qtmp[:, 0:D], qsc, D)
                B.raw_inc(vector, vv, 1)
                vector.wait_ge(sv.h, sv0 + sq_val_v(r))
                if r >= 1:
                    vector.wait_ge(tv.h, tv0 + 8 * r)
                B.quant_post(vector, qtmp[:, 0:D], qtmp[:, D:2 * D],
                             qtmp[:, 2 * D:3 * D], qsc, D, wq_s)
                B.raw_inc(vector, vv, 1)

        def sq_val(r):
            return 2 if r == 0 else 3 + 9 * (r - 1)

        def evac_val(j):
            rr, kk = j // 8, j % 8
            return (4 + 9 * rr + kk) if rr <= 6 else (66 + kk)

        def emit_evacs(scalar, rr):
            for kk in range(8):
                j = rr * 8 + kk
                scalar.wait_ge(tv.h, tv0 + j + 1)
                if j >= 2:
                    scalar.wait_ge(st.h, st0 + 16 * (j - 1))
                slot = tr_s[:, (j % 2) * 128:(j % 2 + 1) * 128]
                inst = scalar.activation(out=slot, in_=ps_tr, func=ACT.Copy)
                B.inc(inst, sv, 1)

        @decos["scalar"]
        def _(scalar):
            scalar.wait_ge(vv.h, vv0 + 1)
            scalar.activation(out=B.acol, in_=B.acol, func=ACT.Exp)
            scalar.mul(out=B.acol, in_=B.acol, mul=-1.0)
            B.raw_inc(scalar, sv, 1)
            for r in range(8):
                scalar.wait_ge(vv.h, vv0 + 1 + 2 * r + 1)
                inst = scalar.activation(out=qtmp[:, D:2 * D], in_=qtmp[:, 0:D],
                                         func=ACT.Square)
                B.inc(inst, sv, 1)
                if r >= 1:
                    emit_evacs(scalar, r - 1)
            emit_evacs(scalar, 7)
            scalar.wait_ge(tv.h, tv0 + 64 + 1)
            inst = scalar.activation(out=arow, in_=ps_tr[0:1, 0:16], func=ACT.Copy)
            B.inc(inst, sv, 1)
            scalar.wait_ge(tv.h, tv0 + 64 + 2)
            inst = scalar.activation(out=B.arep, in_=ps_rep, func=ACT.Copy)
            B.inc(inst, sv, 1)

        @decos["tensor"]
        def _(tensor):
            tensor.wait_ge(ld.h, ld0 + 80)
            for r in range(8):
                tensor.wait_ge(vv.h, vv0 + 1 + 2 * r + 2)
                for k in range(8):
                    idx = r * 8 + k
                    if idx >= 1:
                        j = idx - 1
                        ev = (4 + 9 * (j // 8) + j % 8) if j // 8 <= 6 else (66 + j % 8)
                        tensor.wait_ge(sv.h, sv0 + ev)
                    inst = tensor.transpose(ps_tr, wq_s[:, k * 128:(k + 1) * 128],
                                            B.ids)
                    B.inc(inst, tv, 1)
            tensor.wait_ge(sv.h, sv0 + 1)
            tensor.wait_ge(sv.h, sv0 + 73)
            inst = tensor.transpose(ps_tr[0:1, 0:16], B.acol[0:16, :],
                                    B.ids[0:16, 0:16])
            B.inc(inst, tv, 1)
            tensor.wait_ge(sv.h, sv0 + 74)
            inst = tensor.matmul(ps_rep, B.onesr, arow, start=True, stop=True)
            B.inc(inst, tv, 1)

        @decos["gpsimd"]
        def _(gpsimd):
            for idx in range(64):
                r, k = idx // 8, idx % 8
                ev = (4 + 9 * (idx // 8) + idx % 8) if idx // 8 <= 6 else (66 + idx % 8)
                gpsimd.wait_ge(sv.h, sv0 + ev)
                slot = tr_s[:, (idx % 2) * 128:(idx % 2 + 1) * 128]
                inst = gpsimd.dma_start(
                    out=B.dram["wdq_t"][k * 128:(k + 1) * 128,
                                        r * 128:(r + 1) * 128], in_=slot)
                B.inc(inst, st, 16)
                gpsimd.wait_ge(st.h, st.v)

        finish()
    B.phase_idx += 1


def phase_ffn(B, src, dst):
    nc = B.nc
    ld, st, tv, vv, sv = B.ld, B.st, B.tv, B.vv, B.sv
    inp = B.inp
    ld0, sv0, vv0, tv0, st0 = ld.v, sv.v, vv.v, tv.v, st.v
    VPG = 3 * NSUB + 8 + NSUB
    SPG = 3 * NSUB + 32 + 32
    TPG = 32 + 64 + 32
    sfx = f"_{B.phase_idx}"
    with (
        nc.sbuf_tensor("pf_x" + sfx, [128, NSUB * D], F32) as xg,
        nc.sbuf_tensor("pf_ln" + sfx, [128, D], F32) as lnx,
        nc.sbuf_tensor("pf_xq" + sfx, [128, NSUB * D], F32) as xq,
        nc.sbuf_tensor("pf_xqT" + sfx, [128, 8 * GT], F32) as xqT,
        nc.sbuf_tensor("pf_w" + sfx, [128, 8 * D], F32) as wdq,
        nc.sbuf_tensor("pf_yf" + sfx, [128, 8 * GT], F32) as yf,
        nc.sbuf_tensor("pf_yt" + sfx, [128, NSUB * D], F32) as yt,
        nc.sbuf_tensor("pf_sc" + sfx, [128, 96], F32) as qsc,
        nc.sbuf_tensor("pf_tmp" + sfx, [128, 3 * D], F32) as qtmp,
        nc.sbuf_tensor("pf_st" + sfx, [128, 8], F32) as lnsc,
        nc.sbuf_tensor("pf_b" + sfx, [128, 8], F32) as ffnb,
        nc.psum_tensor("pf_ptr0" + sfx, [128, 128], F32) as ps_tr0,
        nc.psum_tensor("pf_ptr1" + sfx, [128, 128], F32) as ps_tr1,
        nc.psum_tensor("pf_mm0" + sfx, [128, GT], F32) as ps_mm0,
        nc.psum_tensor("pf_mm1" + sfx, [128, GT], F32) as ps_mm1,
        nc.Block() as block,
    ):
        (xg, lnx, xq, xqT, wdq, yf, yt, qsc, qtmp, lnsc, ffnb, ps_tr0,
         ps_tr1, ps_mm0, ps_mm1) = [
            t.ap() for t in (xg, lnx, xq, xqT, wdq, yf, yt, qsc, qtmp, lnsc,
                             ffnb, ps_tr0, ps_tr1, ps_mm0, ps_mm1)]
        ps_tr_s = [ps_tr0, ps_tr1]
        ps_mm_s = [ps_mm0, ps_mm1]
        decos, finish = barrier_wrap(B, block, None)

        @decos["sync"]
        def _(sync):
            sync.dma_start(out=ffnb, in_=inp["ffn_b_col"]).then_inc(ld.h, 16)
            for kk in range(8):
                sync.dma_start(out=wdq[:, kk * D:(kk + 1) * D],
                               in_=B.dram["wdq_t"][kk * 128:(kk + 1) * 128, :]
                               ).then_inc(ld.h, 16)
            ld.v += 16 * 9
            for g in range(NG):
                if g >= 1:
                    sync.wait_ge(vv.h, vv0 + g * VPG)
                for sb in range(NSUB):
                    r0 = g * GT + sb * 128
                    sync.dma_start(out=xg[:, sb * D:(sb + 1) * D],
                                   in_=src[r0:r0 + 128, :]).then_inc(ld.h, 16)
                    ld.v += 16

        @decos["vector"]
        def _(vector):
            for g in range(NG):
                for sb in range(NSUB):
                    vector.wait_ge(ld.h, ld0 + 144 + 64 * (g + 1))
                    xs_ = xg[:, sb * D:(sb + 1) * D]
                    vector.wait_ge(sv.h, sv0 + g * SPG + 3 * sb + 1)
                    if _SKIP_LN[0]:
                        vector.engine_nop()
                        B.raw_inc(vector, vv, 1)
                        vector.wait_ge(sv.h, sv0 + g * SPG + 3 * sb + 2)
                        vector.tensor_copy(out=lnx, in_=xs_)
                    else:
                        B.ln_tile(vector, None, xs_, lnx, qtmp[:, D:2 * D], lnsc)
                        B.raw_inc(vector, vv, 1)
                        vector.wait_ge(sv.h, sv0 + g * SPG + 3 * sb + 2)
                        B.ln_apply(vector, xs_, lnx, lnsc[:, 2:3], lnsc[:, 4:5],
                                   lnsc[:, 5:6])
                    if _SKIP_QUANT[0]:
                        vector.tensor_copy(out=qtmp[:, 0:D], in_=lnx)
                        B.raw_inc(vector, vv, 1)
                        vector.wait_ge(sv.h, sv0 + g * SPG + 3 * sb + 3)
                        vector.tensor_copy(out=xq[:, sb * D:(sb + 1) * D], in_=lnx)
                        B.raw_inc(vector, vv, 1)
                    else:
                        B.quant_pre(vector, lnx, qtmp[:, 0:D], qsc, D)
                        B.raw_inc(vector, vv, 1)
                        vector.wait_ge(sv.h, sv0 + g * SPG + 3 * sb + 3)
                        B.quant_post(vector, qtmp[:, 0:D], qtmp[:, D:2 * D],
                                     qtmp[:, 2 * D:3 * D], qsc, D,
                                     xq[:, sb * D:(sb + 1) * D])
                        B.raw_inc(vector, vv, 1)
                for oc in range(8):
                    vector.wait_ge(tv.h, tv0 + g * TPG + 32 + (oc + 1) * 8)
                    pslot = ps_mm_s[oc % 2]
                    vector.tensor_scalar(out=yf[:, oc * GT:(oc + 1) * GT],
                                         in0=pslot, scalar1=ffnb[:, oc:oc + 1],
                                         scalar2=0.5, op0=ALU.add, op1=ALU.mult)
                    B.raw_inc(vector, vv, 1)
                for sb in range(NSUB):
                    vector.wait_ge(sv.h,
                                   sv0 + g * SPG + 3 * NSUB + 32 + (sb + 1) * 8)
                    vector.tensor_tensor(out=yt[:, sb * D:(sb + 1) * D],
                                         in0=yt[:, sb * D:(sb + 1) * D],
                                         in1=xg[:, sb * D:(sb + 1) * D], op=ALU.add)
                    B.raw_inc(vector, vv, 1)

        @decos["scalar"]
        def _(scalar):
            for g in range(NG):
                for sb in range(NSUB):
                    scalar.wait_ge(ld.h, ld0 + 144 + 64 * (g + 1))
                    if g * NSUB + sb >= 1:
                        scalar.wait_ge(vv.h, vv0 + g * VPG + 3 * sb)
                    inst = scalar.activation(out=qtmp[:, D:2 * D],
                                             in_=xg[:, sb * D:(sb + 1) * D],
                                             func=ACT.Square)
                    B.inc(inst, sv, 1)
                    scalar.wait_ge(vv.h, vv0 + g * VPG + 3 * sb + 1)
                    scalar.activation(out=lnsc[:, 4:5], in_=lnsc[:, 3:4],
                                      func=ACT.Ln, bias=EPS)
                    scalar.drain()
                    inst = scalar.activation(out=lnsc[:, 4:5], in_=lnsc[:, 4:5],
                                             func=ACT.Exp, scale=-0.5)
                    scalar.drain()
                    scalar.sem_inc(sv.h, 1)
                    sv.v += 1
                    scalar.wait_ge(vv.h, vv0 + g * VPG + 3 * sb + 2)
                    inst = scalar.activation(out=qtmp[:, D:2 * D],
                                             in_=qtmp[:, 0:D], func=ACT.Square)
                    B.inc(inst, sv, 1)
                for idx in range(32):
                    sb, k = idx // 8, idx % 8
                    scalar.wait_ge(tv.h, tv0 + g * TPG + idx + 1)
                    slot = ps_tr_s[idx % 2]
                    inst = scalar.activation(
                        out=xqT[:, k * GT + sb * 128:k * GT + (sb + 1) * 128],
                        in_=slot, func=ACT.Copy)
                    B.inc(inst, sv, 1)
                for idx in range(32):
                    sb, oc = idx // 8, idx % 8
                    scalar.wait_ge(tv.h, tv0 + g * TPG + 32 + 64 + idx + 1)
                    if idx == 0 and g >= 1:
                        scalar.wait_ge(st.h, st0 + 64 * g)
                    slot = ps_tr_s[idx % 2]
                    inst = scalar.activation(
                        out=yt[:, sb * D + oc * 128:sb * D + (oc + 1) * 128],
                        in_=slot, func=ACT.Copy)
                    B.inc(inst, sv, 1)

        @decos["tensor"]
        def _(tensor):
            tensor.wait_ge(ld.h, ld0 + 144 + 64)
            for g in range(NG):
                for idx in range(32):
                    sb, k = idx // 8, idx % 8
                    if k == 0:
                        tensor.wait_ge(vv.h, vv0 + g * VPG + 3 * sb + 3)
                    if idx >= 2:
                        tensor.wait_ge(sv.h, sv0 + g * SPG + 3 * NSUB + idx - 1)
                    elif g >= 1:
                        tensor.wait_ge(sv.h, sv0 + g * SPG)
                    slot = ps_tr_s[idx % 2]
                    inst = tensor.transpose(
                        slot, xq[:, sb * D + k * 128:sb * D + (k + 1) * 128], B.ids)
                    B.inc(inst, tv, 1)
                for oc in range(8):
                    if oc == 0:
                        tensor.wait_ge(sv.h, sv0 + g * SPG + 3 * NSUB + 32)
                    if oc >= 2:
                        tensor.wait_ge(vv.h, vv0 + g * VPG + 3 * NSUB + oc - 1)
                    elif g >= 1:
                        tensor.wait_ge(vv.h, vv0 + (g - 1) * VPG + 3 * NSUB + 8)
                    pslot = ps_mm_s[oc % 2]
                    for k in range(8):
                        inst = tensor.matmul(
                            pslot,
                            wdq[:, k * D + oc * 128:k * D + (oc + 1) * 128],
                            xqT[:, k * GT:(k + 1) * GT],
                            start=(k == 0), stop=(k == 7))
                        B.inc(inst, tv, 1)
                for idx in range(32):
                    sb, oc = idx // 8, idx % 8
                    if sb == 0:
                        tensor.wait_ge(vv.h, vv0 + g * VPG + 3 * NSUB + oc + 1)
                    if idx >= 2:
                        tensor.wait_ge(sv.h,
                                       sv0 + g * SPG + 3 * NSUB + 32 + idx - 1)
                    slot = ps_tr_s[idx % 2]
                    inst = tensor.transpose(
                        slot, yf[:, oc * GT + sb * 128:oc * GT + (sb + 1) * 128],
                        B.ids)
                    B.inc(inst, tv, 1)

        @decos["gpsimd"]
        def _(gpsimd):
            for g in range(NG):
                for sb in range(NSUB):
                    gpsimd.wait_ge(vv.h,
                                   vv0 + g * VPG + 3 * NSUB + 8 + sb + 1)
                    r0 = g * GT + sb * 128
                    inst = gpsimd.dma_start(out=dst[r0:r0 + 128, :],
                                            in_=yt[:, sb * D:(sb + 1) * D])
                    B.inc(inst, st, 16)
            if _DEBUG_OUT[0]:
                inst = gpsimd.dma_start(out=B.dram["dbg"][:, 0:D], in_=xq[:, 0:D])
                B.inc(inst, st, 16)
                inst = gpsimd.dma_start(out=B.dram["dbg"][:, D:D+GT], in_=xqT[:, 0:GT])
                B.inc(inst, st, 16)
                inst = gpsimd.dma_start(out=B.dram["dbg"][:, 2*D:2*D+GT], in_=yf[:, 0:GT])
                B.inc(inst, st, 16)
                gpsimd.wait_ge(st.h, st.v)
            gpsimd.wait_ge(st.h, st.v)

        finish()
    B.phase_idx += 1


def phase_mamba_in(B):
    nc = B.nc
    ld, st, tv, vv, sv = B.ld, B.st, B.tv, B.vv, B.sv
    inp = B.inp
    ld0, sv0, vv0, tv0, st0 = ld.v, sv.v, vv.v, tv.v, st.v
    VPGm = 44
    SPGm = 74
    TPGm = 305
    STG = 464

    import contextlib
    with contextlib.ExitStack() as _stk:
        def A(*args):
            return _stk.enter_context(nc.sbuf_tensor(*args)).ap()

        def PP(*args):
            return _stk.enter_context(nc.psum_tensor(*args)).ap()

        xg = A("pm_x", [128, NSUB * D], F32)
        lnx = A("pm_ln", [128, D], F32)
        lnT = A("pm_lnT", [128, 8 * GT], F32)
        wbuf = A("pm_w", [128, 2 * 8 * 128], F32)
        xc_sb = A("pm_xc", [128, NS * (GT + 3)], F32)
        u_sb = A("pm_u", [128, NS * GT], F32)
        dt_sb = A("pm_dt", [128, NS * GT], F32)
        du_sb = A("pm_du", [128, NS * GT], F32)
        szb = A("pm_sz", [128, GT], F32)
        wx_sb = A("pm_wx", [128, NS * 33], F32)
        cw_sb = A("pm_cw", [128, NS * 4], F32)
        cb_sb = A("pm_cb", [128, NS], F32)
        wdt_sb = A("pm_wdt", [128, NS], F32)
        bdt_sb = A("pm_bdt", [128, NS], F32)
        xdbl_sb = A("pm_xdbl", [33, GT], F32)
        dtr_sb = A("pm_dtr", [128, GT], F32)
        lnsc = A("pm_sc", [128, 8], F32)
        qtmp = A("pm_q", [128, 2 * D], F32)
        ps_tr0 = PP("pm_tr0", [128, 128], F32)
        ps_tr1 = PP("pm_tr1", [128, 128], F32)
        ps_mm0 = PP("pm_mm0", [128, GT], F32)
        ps_mm1 = PP("pm_mm1", [128, GT], F32)
        ps_xd = PP("pm_xd", [33, GT], F32)
        ps_rep = PP("pm_rep", [128, GT], F32)
        block = _stk.enter_context(nc.Block())
        ps_tr_s = [ps_tr0, ps_tr1]
        ps_mm_s = [ps_mm0, ps_mm1]
        decos, finish = barrier_wrap(B, block, None)

        def xw_val(g):
            return 176 if g == 0 else 80 + (g * 36 + 4) * 16

        @decos["sync"]
        def _(sync):
            sync.dma_start(out=wx_sb.rearrange("p (c e) -> p c e", c=NS),
                           in_=inp["w_x_t"].rearrange("(c p) e -> p c e", p=128)
                           ).then_inc(ld.h, 16)
            sync.dma_start(out=cw_sb,
                           in_=inp["conv_w_l"].rearrange("p c k -> p (c k)")
                           ).then_inc(ld.h, 16)
            sync.dma_start(out=cb_sb, in_=inp["conv_b_l"]).then_inc(ld.h, 16)
            sync.dma_start(out=wdt_sb, in_=inp["wdt_l"]).then_inc(ld.h, 16)
            sync.dma_start(out=bdt_sb, in_=inp["bdt_l"]).then_inc(ld.h, 16)
            ld.v += 80
            for g in range(NG):
                if g >= 1:
                    sync.wait_ge(vv.h, vv0 + g * VPGm)
                for sb in range(NSUB):
                    r0 = g * GT + sb * 128
                    sync.dma_start(out=xg[:, sb * D:(sb + 1) * D],
                                   in_=B.dram["xres"][r0:r0 + 128, :]
                                   ).then_inc(ld.h, 16)
                    ld.v += 16
                for ec in range(32):
                    if g * 32 + ec >= 2:
                        sync.wait_ge(tv.h, tv0 + g * TPGm + 32 + (ec - 1) * 8)
                    src = inp["w_in_t"][:, ec * 128:(ec + 1) * 128]
                    src = src.rearrange("(k p) c -> p k c", p=128)
                    wdst = wbuf[:, (ec % 2) * 1024:(ec % 2 + 1) * 1024]
                    sync.dma_start(out=wdst.rearrange("p (k c) -> p k c", k=8),
                                   in_=src).then_inc(ld.h, 16)
                    ld.v += 16

        @decos["vector"]
        def _(vector):
            xc3 = xc_sb.rearrange("p (c t) -> p c t", c=NS)
            vector.memset(xc3[:, :, 0:3], 0.0)
            vector.drain()
            for g in range(NG):
                vb = vv0 + g * VPGm
                sbb = sv0 + g * SPGm
                tb = tv0 + g * TPGm
                vector.wait_ge(ld.h, ld0 + xw_val(g))
                for sb in range(NSUB):
                    xs_ = xg[:, sb * D:(sb + 1) * D]
                    sqi = (1, 3, 13, 23)[sb]
                    vector.wait_ge(sv.h, sbb + sqi)
                    B.ln_tile(vector, None, xs_, lnx, qtmp[:, D:2 * D], lnsc)
                    B.raw_inc(vector, vv, 1)
                    vector.wait_ge(sv.h, sbb + sqi + 1)
                    if sb >= 1:
                        vector.wait_ge(tv.h, tb + 8 * sb)
                    elif g >= 1:
                        vector.wait_ge(tv.h, tv0 + (g - 1) * TPGm + 32)
                    B.ln_apply(vector, xs_, lnx, lnsc[:, 2:3], lnsc[:, 4:5],
                               lnsc[:, 5:6])
                    B.raw_inc(vector, vv, 1)
                for ec in range(16):
                    if ec == 0 and g >= 1:
                        vector.wait_ge(st.h, st0 + g * STG)
                    vector.wait_ge(tv.h, tb + 32 + (ec + 1) * 8)
                    pslot = ps_mm_s[ec % 2]
                    vector.tensor_copy(out=xc_sb[:, ec * (GT + 3) + 3:
                                                 (ec + 1) * (GT + 3)],
                                       in_=pslot)
                    B.raw_inc(vector, vv, 1)
                    base = ec * (GT + 3)
                    cacc = du_sb[:, ec * GT:(ec + 1) * GT]
                    vector.tensor_scalar(out=cacc,
                                         in0=xc_sb[:, base:base + GT],
                                         scalar1=cw_sb[:, ec * 4:ec * 4 + 1],
                                         scalar2=cb_sb[:, ec:ec + 1],
                                         op0=ALU.mult, op1=ALU.add)
                    for kk in range(1, 4):
                        vector.scalar_tensor_tensor(
                            out=cacc, in0=xc_sb[:, base + kk:base + kk + GT],
                            scalar=cw_sb[:, ec * 4 + kk:ec * 4 + kk + 1],
                            in1=cacc, op0=ALU.mult, op1=ALU.add)
                    B.raw_inc(vector, vv, 1)
                vector.wait_ge(tv.h, tb + 32 + 256 + 16)
                vector.tensor_copy(out=xdbl_sb, in_=ps_xd)
                B.raw_inc(vector, vv, 1)
                vector.wait_ge(tv.h, tb + 32 + 256 + 16 + 1)
                vector.tensor_copy(out=dtr_sb, in_=ps_rep)
                B.raw_inc(vector, vv, 1)
                vector.wait_ge(sv.h, sbb + 8 + 32 + 16 + 16)
                dp3 = du_sb.rearrange("p (c t) -> p c t", c=NS)
                dtr3 = dtr_sb.unsqueeze(1).broadcast_to([128, NS, GT])
                wdt3 = wdt_sb.unsqueeze(2).broadcast_to([128, NS, GT])
                bdt3 = bdt_sb.unsqueeze(2).broadcast_to([128, NS, GT])
                vector.tensor_tensor(out=dp3, in0=dtr3, in1=wdt3, op=ALU.mult)
                vector.tensor_tensor(out=dp3, in0=dp3, in1=bdt3, op=ALU.add)
                B.raw_inc(vector, vv, 1)
                vector.wait_ge(sv.h, sbb + SPGm)
                vector.tensor_scalar(out=dt_sb, in0=du_sb, scalar1=1e-4,
                                     scalar2=10.0, op0=ALU.max, op1=ALU.min)
                vector.tensor_tensor(out=du_sb, in0=dt_sb, in1=u_sb,
                                     op=ALU.mult)
                vector.tensor_copy(out=xc3[:, :, 0:3], in_=xc3[:, :, GT:GT + 3])
                B.raw_inc(vector, vv, 1)

        @decos["scalar"]
        def _(scalar):
            def emit_evacs(scalar, tb, lo, hi):
                for idx in range(lo, hi):
                    sb, k = idx // 8, idx % 8
                    scalar.wait_ge(tv.h, tb + idx + 1)
                    slot = ps_tr_s[idx % 2]
                    inst = scalar.activation(
                        out=lnT[:, k * GT + sb * 128:k * GT + (sb + 1) * 128],
                        in_=slot, func=ACT.Copy)
                    B.inc(inst, sv, 1)

            for g in range(NG):
                vb = vv0 + g * VPGm
                sbb = sv0 + g * SPGm
                tb = tv0 + g * TPGm
                scalar.wait_ge(ld.h, ld0 + xw_val(g))
                for sb in range(NSUB):
                    if g * NSUB + sb >= 1:
                        scalar.wait_ge(vv.h, vb + 2 * sb)
                    inst = scalar.activation(out=qtmp[:, D:2 * D],
                                             in_=xg[:, sb * D:(sb + 1) * D],
                                             func=ACT.Square)
                    B.inc(inst, sv, 1)
                    scalar.wait_ge(vv.h, vb + 2 * sb + 1)
                    scalar.activation(out=lnsc[:, 4:5], in_=lnsc[:, 3:4],
                                      func=ACT.Ln, bias=EPS)
                    scalar.drain()
                    scalar.activation(out=lnsc[:, 4:5], in_=lnsc[:, 4:5],
                                      func=ACT.Exp, scale=-0.5)
                    scalar.drain()
                    scalar.sem_inc(sv.h, 1)
                    sv.v += 1
                    if sb >= 1:
                        emit_evacs(scalar, tb, (sb - 1) * 8, sb * 8)
                emit_evacs(scalar, tb, 24, 32)
                for zc in range(16):
                    ec = 16 + zc
                    scalar.wait_ge(tv.h, tb + 32 + (ec + 1) * 8)
                    if zc >= 1:
                        scalar.wait_ge(st.h, st0 + g * STG + zc * 16)
                    pslot = ps_mm_s[ec % 2]
                    inst = scalar.activation(out=szb, in_=pslot, func=ACT.Silu)
                    B.inc(inst, sv, 1)
                for ec in range(16):
                    if ec == 0 and g >= 1:
                        scalar.wait_ge(st.h, st0 + (g - 1) * STG + 320)
                    scalar.wait_ge(vv.h, vb + 8 + 2 * ec + 2)
                    inst = scalar.activation(
                        out=u_sb[:, ec * GT:(ec + 1) * GT],
                        in_=du_sb[:, ec * GT:(ec + 1) * GT], func=ACT.Silu)
                    B.inc(inst, sv, 1)
                scalar.wait_ge(vv.h, vb + 43)
                inst = scalar.activation(out=dt_sb, in_=du_sb, func=ACT.Exp)
                B.inc(inst, sv, 1)
                scalar.drain()
                inst = scalar.activation(out=du_sb, in_=dt_sb, func=ACT.Ln,
                                         bias=1.0)
                B.inc(inst, sv, 1)

        @decos["tensor"]
        def _(tensor):
            tensor.wait_ge(ld.h, ld0 + 176)
            for g in range(NG):
                vb = vv0 + g * VPGm
                sbb = sv0 + g * SPGm
                tb = tv0 + g * TPGm
                for sb in range(NSUB):
                    tensor.wait_ge(vv.h, vb + 2 * sb + 2)
                    for k in range(8):
                        idx = sb * 8 + k
                        if idx >= 2:
                            j = idx - 2
                            ei = 5 + j if j < 8 else (j + 7 if j < 16 else j + 9)
                            tensor.wait_ge(sv.h, sbb + ei)
                        elif g >= 1:
                            tensor.wait_ge(sv.h, sv0 + g * SPGm)
                        slot = ps_tr_s[idx % 2]
                        inst = tensor.transpose(
                            slot, lnx[:, k * 128:(k + 1) * 128], B.ids)
                        B.inc(inst, tv, 1)
                for ec in range(32):
                    if ec == 0:
                        tensor.wait_ge(sv.h, sbb + 8 + 32)
                    wv = 80 + 16 * (g * 36 + 4 + ec + 1)
                    if g == 0 and ec < 2:
                        wv = 176
                    tensor.wait_ge(ld.h, ld0 + wv)
                    if ec >= 2:
                        if ec - 2 < 16:
                            tensor.wait_ge(vv.h, vb + 8 + 2 * (ec - 2) + 1)
                        else:
                            tensor.wait_ge(sv.h,
                                           sbb + 8 + 32 + (ec - 2 - 16) + 1)
                    pslot = ps_mm_s[ec % 2]
                    wslot = wbuf[:, (ec % 2) * 1024:(ec % 2 + 1) * 1024]
                    for k in range(8):
                        inst = tensor.matmul(pslot,
                                             wslot[:, k * 128:(k + 1) * 128],
                                             lnT[:, k * GT:(k + 1) * GT],
                                             start=(k == 0), stop=(k == 7))
                        B.inc(inst, tv, 1)
                tensor.wait_ge(sv.h, sbb + 8 + 32 + 16 + 16)
                for dch in range(NS):
                    inst = tensor.matmul(ps_xd,
                                         wx_sb[:, dch * 33:(dch + 1) * 33],
                                         u_sb[:, dch * GT:(dch + 1) * GT],
                                         start=(dch == 0), stop=(dch == NS - 1))
                    B.inc(inst, tv, 1)
                tensor.wait_ge(vv.h, vb + 41)
                inst = tensor.matmul(ps_rep, B.onesr, xdbl_sb[0:1, :],
                                     start=True, stop=True)
                B.inc(inst, tv, 1)

        @decos["gpsimd"]
        def _(gpsimd):
            for g in range(NG):
                vb = vv0 + g * VPGm
                sbb = sv0 + g * SPGm
                t0c = g * GT
                for zc in range(16):
                    gpsimd.wait_ge(sv.h, sbb + 8 + 32 + zc + 1)
                    inst = gpsimd.dma_start(
                        out=B.dram["sz_fm"][zc * 128:(zc + 1) * 128,
                                            t0c:t0c + GT], in_=szb)
                    B.inc(inst, st, 16)
                    gpsimd.wait_ge(st.h, st.v)
                gpsimd.wait_ge(sv.h, sbb + 8 + 32 + 16 + 16)
                u3 = u_sb.rearrange("p (c t) -> p c t", c=NS)
                dst_u = B.dram["u_fm"].rearrange("(c p) t -> p c t", p=128)
                for q in range(4):
                    inst = gpsimd.dma_start(
                        out=dst_u[:, q * 4:(q + 1) * 4, t0c:t0c + GT],
                        in_=u3[:, q * 4:(q + 1) * 4, :])
                    B.inc(inst, st, 16)
                gpsimd.wait_ge(vv.h, vb + 41)
                inst = gpsimd.dma_start(out=B.dram["bc_fm"][:, t0c:t0c + GT],
                                        in_=xdbl_sb[1:33, :])
                B.inc(inst, st, 16)
                gpsimd.wait_ge(vv.h, vb + VPGm)
                dt3 = dt_sb.rearrange("p (c t) -> p c t", c=NS)
                dst_dt = B.dram["dt_fm"].rearrange("(c p) t -> p c t", p=128)
                du3 = du_sb.rearrange("p (c t) -> p c t", c=NS)
                dst_du = B.dram["du_fm"].rearrange("(c p) t -> p c t", p=128)
                for q in range(4):
                    inst = gpsimd.dma_start(
                        out=dst_dt[:, q * 4:(q + 1) * 4, t0c:t0c + GT],
                        in_=dt3[:, q * 4:(q + 1) * 4, :])
                    B.inc(inst, st, 16)
                for q in range(4):
                    inst = gpsimd.dma_start(
                        out=dst_du[:, q * 4:(q + 1) * 4, t0c:t0c + GT],
                        in_=du3[:, q * 4:(q + 1) * 4, :])
                    B.inc(inst, st, 16)
                gpsimd.wait_ge(st.h, st.v)

        finish()
    B.phase_idx += 1


def phase_scan(B):
    nc = B.nc
    ld, st, tv, vv, sv, gv = B.ld, B.st, B.tv, B.vv, B.sv, B.gv
    ld0, sv0, vv0, tv0, st0, gv0 = ld.v, sv.v, vv.v, tv.v, st.v, gv.v
    TS = TS_SCAN
    FB = TS * 256
    with (
        nc.sbuf_tensor("sc_dt", [128, 2 * NS * TS], F32) as dt_t,
        nc.sbuf_tensor("sc_du", [128, 2 * NS * TS], F32) as du_t,
        nc.sbuf_tensor("sc_b", [128, 2 * NS * TS], F32) as b_t,
        nc.sbuf_tensor("sc_c", [128, 2 * NS * TS], F32) as c_t,
        nc.sbuf_tensor("sc_z", [128, FB], F32) as zbuf,
        nc.sbuf_tensor("sc_e", [128, FB], F32) as ebuf,
        nc.sbuf_tensor("sc_p", [128, FB], F32) as pbuf,
        nc.sbuf_tensor("sc_h", [128, FB], F32) as htraj,
        nc.sbuf_tensor("sc_hc", [128, 256], F32) as hc,
        nc.sbuf_tensor("sc_t1", [128, 256], F32) as t1,
        nc.sbuf_tensor("sc_y", [128, 2 * TS * NS], F32) as ybuf,
        nc.sbuf_tensor("sc_y2", [128, 2 * TS * NS], F32) as ybuf2,
        nc.Block() as block,
    ):
        (dt_t, du_t, b_t, c_t, zbuf, ebuf, pbuf, htraj, hc, t1, ybuf,
         ybuf2) = [t.ap() for t in (dt_t, du_t, b_t, c_t, zbuf, ebuf, pbuf,
                                    htraj, hc, t1, ybuf, ybuf2)]
        decos, finish = barrier_wrap(B, block, None)

        VPC, SPC = 2, 1

        @decos["sync"]
        def _(sync):
            for c in range(NCHUNK):
                pb = c % 2
                if c >= 2:
                    sync.wait_ge(vv.h, vv0 + (c - 1) * VPC)
                t0c = c * TS
                dt3 = dt_t[:, pb * NS * TS:(pb + 1) * NS * TS].rearrange(
                    "p (c t) -> p c t", c=NS)
                du3 = du_t[:, pb * NS * TS:(pb + 1) * NS * TS].rearrange(
                    "p (c t) -> p c t", c=NS)
                for q in range(4):
                    src = B.dram["dt_fm"].rearrange("(c p) t -> p c t", p=128)
                    sync.dma_start(out=dt3[:, q * 4:(q + 1) * 4, :],
                                   in_=src[:, q * 4:(q + 1) * 4, t0c:t0c + TS]
                                   ).then_inc(ld.h, 16)
                    src2 = B.dram["du_fm"].rearrange("(c p) t -> p c t", p=128)
                    sync.dma_start(out=du3[:, q * 4:(q + 1) * 4, :],
                                   in_=src2[:, q * 4:(q + 1) * 4, t0c:t0c + TS]
                                   ).then_inc(ld.h, 16)
                    ld.v += 32
                bsrc = B.dram["bc_fm"][0:16, t0c:t0c + TS]
                bsrc = bsrc.unsqueeze(0).broadcast_to([128, 16, TS])
                sync.dma_start(out=b_t[:, pb * NS * TS:(pb + 1) * NS * TS]
                               .rearrange("p (n t) -> p n t", n=NS),
                               in_=bsrc).then_inc(ld.h, 16)
                csrc = B.dram["bc_fm"][16:32, t0c:t0c + TS]
                csrc = csrc.unsqueeze(0).broadcast_to([128, 16, TS])
                sync.dma_start(out=c_t[:, pb * NS * TS:(pb + 1) * NS * TS]
                               .rearrange("p (n t) -> p n t", n=NS),
                               in_=csrc).then_inc(ld.h, 16)
                ld.v += 32

        @decos["vector"]
        def _(vector):
            for c in range(NCHUNK):
                pb = c % 2
                vector.wait_ge(ld.h, ld0 + max(c + 1, 2) * 160)
                dtv = dt_t[:, pb * NS * TS:(pb + 1) * NS * TS]
                duv = du_t[:, pb * NS * TS:(pb + 1) * NS * TS]
                bv = b_t[:, pb * NS * TS:(pb + 1) * NS * TS]
                cv_ = c_t[:, pb * NS * TS:(pb + 1) * NS * TS]
                z3 = zbuf.rearrange("p (t c n) -> p t c n", t=TS, c=NS)
                dt4 = dtv.rearrange("p (c t) -> p c t", c=NS)
                dt4 = dt4.transpose([0, 2, 1]).unsqueeze(3)
                dt4 = dt4.broadcast_to([128, TS, NS, NS])
                a4 = B.arep.unsqueeze(1).unsqueeze(1)
                a4 = a4.broadcast_to([128, TS, NS, NS])
                vector.tensor_tensor(out=z3, in0=dt4, in1=a4, op=ALU.mult)
                vector.tensor_single_scalar(out=ebuf, in_=zbuf, scalar=-20.0,
                                            op=ALU.max)
                B.raw_inc(vector, vv, 1)
                p3 = pbuf.rearrange("p (t c n) -> p t c n", t=TS, c=NS)
                du4 = duv.rearrange("p (c t) -> p c t", c=NS)
                du4 = du4.transpose([0, 2, 1]).unsqueeze(3)
                du4 = du4.broadcast_to([128, TS, NS, NS])
                b4 = bv.rearrange("p (n t) -> p n t", n=NS)
                b4 = b4.transpose([0, 2, 1]).unsqueeze(2)
                b4 = b4.broadcast_to([128, TS, NS, NS])
                vector.tensor_tensor(out=p3, in0=du4, in1=b4, op=ALU.mult)
                vector.wait_ge(sv.h, sv0 + (c + 1) * SPC)
                vector.tensor_scalar(out=ebuf, in0=pbuf, scalar1=-10.0,
                                     scalar2=10.0, op0=ALU.max, op1=ALU.min)
                if c == 0:
                    vector.memset(hc, 0.0)
                    vector.drain()
                for j in range(TS):
                    hsrc = hc if j == 0 else htraj[:, (j - 1) * 256:j * 256]
                    vector.tensor_tensor(out=t1, in0=hsrc,
                                         in1=zbuf[:, j * 256:(j + 1) * 256],
                                         op=ALU.mult)
                    vector.tensor_tensor(out=t1, in0=t1,
                                         in1=ebuf[:, j * 256:(j + 1) * 256],
                                         op=ALU.add)
                    vector.tensor_scalar(out=htraj[:, j * 256:(j + 1) * 256],
                                         in0=t1, scalar1=-100.0, scalar2=100.0,
                                         op0=ALU.max, op1=ALU.min)
                vector.tensor_copy(out=hc, in_=htraj[:, (TS - 1) * 256:TS * 256])
                h4 = htraj.rearrange("p (t c n) -> p t c n", t=TS, c=NS)
                c4 = cv_.rearrange("p (n t) -> p n t", n=NS)
                c4 = c4.transpose([0, 2, 1]).unsqueeze(2)
                c4 = c4.broadcast_to([128, TS, NS, NS])
                vector.tensor_tensor(out=h4, in0=h4, in1=c4, op=ALU.mult)
                ypb = ybuf2[:, pb * TS * NS:(pb + 1) * TS * NS]
                B.red(vector, ypb.rearrange("p (t c) -> p t c", t=TS), h4,
                      ALU.add)
                yo = ybuf[:, pb * TS * NS:(pb + 1) * TS * NS]
                if c >= 2:
                    vector.wait_ge(st.h, st0 + (c - 1) * 64)
                vector.tensor_copy(
                    out=yo.rearrange("p (c t) -> p c t", c=NS),
                    in_=ypb.rearrange("p (t c) -> p c t", c=NS))
                B.raw_inc(vector, vv, 1)
            vv_final = vv.v

        @decos["gpsimd"]
        def _(gpsimd):
            for c in range(NCHUNK):
                pb = c % 2
                gpsimd.wait_ge(vv.h, vv0 + (c + 1) * VPC)
                yo = ybuf[:, pb * TS * NS:(pb + 1) * TS * NS].rearrange(
                    "p (c t) -> p c t", c=NS)
                t0c = c * TS
                dst = B.dram["y_fm"].rearrange("(c p) t -> p c t", p=128)
                for q in range(4):
                    inst = gpsimd.dma_start(
                        out=dst[:, q * 4:(q + 1) * 4, t0c:t0c + TS],
                        in_=yo[:, q * 4:(q + 1) * 4, :])
                    B.inc(inst, st, 16)

        @decos["scalar"]
        def _(scalar):
            for c in range(NCHUNK):
                scalar.wait_ge(vv.h, vv0 + c * VPC + 1)
                inst = scalar.activation(out=zbuf, in_=ebuf, func=ACT.Exp)
                B.inc(inst, sv, 1)

        finish()
    B.phase_idx += 1


def phase_mamba_out(B):
    nc = B.nc
    ld, st, tv, vv, sv = B.ld, B.st, B.tv, B.vv, B.sv
    inp = B.inp
    ld0, sv0, vv0, tv0, st0 = ld.v, sv.v, vv.v, tv.v, st.v
    G2 = 256
    NG2 = T // G2
    NSUB2 = G2 // 128
    with (
        nc.sbuf_tensor("po_y", [128, NS * G2], F32) as yg,
        nc.sbuf_tensor("po_u", [128, NS * G2], F32) as ug,
        nc.sbuf_tensor("po_sz", [128, NS * G2], F32) as szg,
        nc.sbuf_tensor("po_x", [128, NSUB2 * D], F32) as xg,
        nc.sbuf_tensor("po_w", [128, NS * D], F32) as wout,
        nc.sbuf_tensor("po_sq", [128, 2 * G2], F32) as sqb,
        nc.sbuf_tensor("po_stat", [1, 6 * G2], F32) as stats,
        nc.sbuf_tensor("po_rep", [128, 2 * G2], F32) as reps,
        nc.sbuf_tensor("po_g", [128, NS], F32) as lngc,
        nc.sbuf_tensor("po_b", [128, NS], F32) as lnbc,
        nc.sbuf_tensor("po_d", [128, NS], F32) as dcolc,
        nc.sbuf_tensor("po_out", [128, NSUB2 * D], F32) as outt,
        nc.psum_tensor("po_s1", [1, G2], F32) as ps_s1,
        nc.psum_tensor("po_s2", [1, G2], F32) as ps_s2,
        nc.psum_tensor("po_rep0", [128, G2], F32) as ps_rep0,
        nc.psum_tensor("po_rep1", [128, G2], F32) as ps_rep1,
        nc.psum_tensor("po_o0", [128, 512], F32) as ps_o0,
        nc.psum_tensor("po_o1", [128, 512], F32) as ps_o1,
        nc.Block() as block,
    ):
        (yg, ug, szg, xg, wout, sqb, stats, reps, lngc, lnbc, dcolc, outt,
         ps_s1, ps_s2, ps_rep0, ps_rep1, ps_o0, ps_o1) = [t.ap() for t in (
             yg, ug, szg, xg, wout, sqb, stats, reps, lngc, lnbc, dcolc, outt,
             ps_s1, ps_s2, ps_rep0, ps_rep1, ps_o0, ps_o1)]
        decos, finish = barrier_wrap(B, block, None)
        VPGo = 5
        SPGo = 19
        TPGo = 98
        LDG = 14 * 16

        @decos["sync"]
        def _(sync):
            for k in range(NS):
                sync.dma_start(out=wout[:, k * D:(k + 1) * D],
                               in_=inp["w_out_t"][k * 128:(k + 1) * 128, :]
                               ).then_inc(ld.h, 16)
            sync.dma_start(out=lngc, in_=inp["lng_col"]).then_inc(ld.h, 16)
            sync.dma_start(out=lnbc, in_=inp["lnb_col"]).then_inc(ld.h, 16)
            sync.dma_start(out=dcolc, in_=inp["d_col"]).then_inc(ld.h, 16)
            ld.v += 304
            for g in range(NG2):
                t0c = g * G2
                if g >= 1:
                    sync.wait_ge(vv.h, vv0 + g * VPGo)
                for nm, buf in (("y_fm", yg), ("u_fm", ug), ("sz_fm", szg)):
                    src = B.dram[nm].rearrange("(c p) t -> p c t", p=128)
                    b3 = buf.rearrange("p (c t) -> p c t", c=NS)
                    for q in range(4):
                        sync.dma_start(out=b3[:, q * 4:(q + 1) * 4, :],
                                       in_=src[:, q * 4:(q + 1) * 4,
                                               t0c:t0c + G2]).then_inc(ld.h, 16)
                        ld.v += 16
                for sb in range(NSUB2):
                    r0 = t0c + sb * 128
                    sync.dma_start(out=xg[:, sb * D:(sb + 1) * D],
                                   in_=B.dram["xres"][r0:r0 + 128, :]
                                   ).then_inc(ld.h, 16)
                    ld.v += 16

        @decos["scalar"]
        def _(scalar):
            for g in range(NG2):
                vb = vv0 + g * VPGo
                sbb = sv0 + g * SPGo
                tb = tv0 + g * TPGo
                scalar.wait_ge(ld.h, ld0 + 304 + (g + 1) * LDG)
                for dch in range(NS):
                    if dch >= 2:
                        scalar.wait_ge(tv.h, tb + 16 + dch - 1)
                    if g >= 1 and dch < 2:
                        scalar.wait_ge(tv.h, tv0 + g * TPGo)
                    inst = scalar.activation(
                        out=sqb[:, (dch % 2) * G2:(dch % 2 + 1) * G2],
                        in_=yg[:, dch * G2:(dch + 1) * G2], func=ACT.Square)
                    B.inc(inst, sv, 1)
                scalar.wait_ge(vv.h, vb + 1)
                scalar.activation(out=stats[:, 4 * G2:5 * G2],
                                  in_=stats[:, 3 * G2:4 * G2], func=ACT.Ln,
                                  bias=EPS)
                scalar.drain()
                scalar.activation(out=stats[:, 4 * G2:5 * G2],
                                  in_=stats[:, 4 * G2:5 * G2], func=ACT.Exp,
                                  scale=-0.5)
                scalar.drain()
                scalar.sem_inc(sv.h, 1)
                sv.v += 1
                scalar.wait_ge(tv.h, tb + 32 + 1)
                inst = scalar.activation(out=reps[:, 0:G2], in_=ps_rep0,
                                         func=ACT.Copy)
                B.inc(inst, sv, 1)
                scalar.wait_ge(tv.h, tb + 32 + 2)
                inst = scalar.activation(out=reps[:, G2:2 * G2], in_=ps_rep1,
                                         func=ACT.Copy)
                B.inc(inst, sv, 1)

        @decos["vector"]
        def _(vector):
            for g in range(NG2):
                vb = vv0 + g * VPGo
                sbb = sv0 + g * SPGo
                tb = tv0 + g * TPGo
                vector.wait_ge(tv.h, tb + 32)
                s1 = stats[0:1, 0:G2]
                s2 = stats[0:1, G2:2 * G2]
                mu = stats[0:1, 2 * G2:3 * G2]
                var = stats[0:1, 3 * G2:4 * G2]
                inv = stats[0:1, 4 * G2:5 * G2]
                nms = stats[0:1, 5 * G2:6 * G2]
                vector.tensor_copy(out=s1, in_=ps_s1)
                vector.tensor_copy(out=s2, in_=ps_s2)
                vector.drain()
                vector.tensor_single_scalar(out=mu, in_=s1, scalar=1.0 / DI,
                                            op=ALU.mult)
                vector.tensor_single_scalar(out=var, in_=s2, scalar=1.0 / DI,
                                            op=ALU.mult)
                vector.drain()
                vector.tensor_tensor(out=nms, in0=mu, in1=mu, op=ALU.mult)
                vector.drain()
                vector.tensor_tensor(out=var, in0=var, in1=nms, op=ALU.subtract)
                B.raw_inc(vector, vv, 1)
                vector.wait_ge(sv.h, sbb + 16 + 1)
                vector.tensor_tensor(out=nms, in0=mu, in1=inv, op=ALU.mult)
                vector.drain()
                vector.tensor_single_scalar(out=nms, in_=nms, scalar=-1.0,
                                            op=ALU.mult)
                B.raw_inc(vector, vv, 1)
                vector.wait_ge(sv.h, sbb + 16 + 1 + 2)
                y4 = yg.rearrange("p (c t) -> p c t", c=NS)
                inv_r = reps[:, 0:G2].unsqueeze(1).broadcast_to([128, NS, G2])
                nms_r = reps[:, G2:2 * G2].unsqueeze(1).broadcast_to(
                    [128, NS, G2])
                vector.tensor_tensor(out=y4, in0=y4, in1=inv_r, op=ALU.mult)
                vector.tensor_tensor(out=y4, in0=y4, in1=nms_r, op=ALU.add)
                lng3 = lngc.unsqueeze(2).broadcast_to([128, NS, G2])
                lnb3 = lnbc.unsqueeze(2).broadcast_to([128, NS, G2])
                vector.tensor_tensor(out=y4, in0=y4, in1=lng3, op=ALU.mult)
                vector.tensor_tensor(out=y4, in0=y4, in1=lnb3, op=ALU.add)
                u4 = ug.rearrange("p (c t) -> p c t", c=NS)
                d3 = dcolc.unsqueeze(2).broadcast_to([128, NS, G2])
                vector.tensor_tensor(out=u4, in0=u4, in1=d3, op=ALU.mult)
                vector.tensor_tensor(out=y4, in0=y4, in1=u4, op=ALU.add)
                sz4 = szg.rearrange("p (c t) -> p c t", c=NS)
                vector.tensor_tensor(out=y4, in0=y4, in1=sz4, op=ALU.mult)
                B.raw_inc(vector, vv, 1)
                if g >= 1:
                    vector.wait_ge(st.h, st0 + g * 32)
                for sb in range(NSUB2):
                    vector.wait_ge(tv.h, tb + 34 + (sb + 1) * 32)
                    vector.tensor_copy(out=outt[:, sb * D:sb * D + 512],
                                       in_=ps_o0)
                    vector.tensor_copy(out=outt[:, sb * D + 512:(sb + 1) * D],
                                       in_=ps_o1)
                    vector.drain()
                    vector.tensor_tensor(out=outt[:, sb * D:(sb + 1) * D],
                                         in0=outt[:, sb * D:(sb + 1) * D],
                                         in1=xg[:, sb * D:(sb + 1) * D],
                                         op=ALU.add)
                    B.raw_inc(vector, vv, 1)

        @decos["tensor"]
        def _(tensor):
            for g in range(NG2):
                vb = vv0 + g * VPGo
                sbb = sv0 + g * SPGo
                tb = tv0 + g * TPGo
                tensor.wait_ge(ld.h, ld0 + 304 + (g + 1) * LDG)
                for dch in range(NS):
                    inst = tensor.matmul(ps_s1, B.onesc,
                                         yg[:, dch * G2:(dch + 1) * G2],
                                         start=(dch == 0), stop=(dch == NS - 1))
                    B.inc(inst, tv, 1)
                for dch in range(NS):
                    tensor.wait_ge(sv.h, sbb + dch + 1)
                    inst = tensor.matmul(
                        ps_s2, B.onesc,
                        sqb[:, (dch % 2) * G2:(dch % 2 + 1) * G2],
                        start=(dch == 0), stop=(dch == NS - 1))
                    B.inc(inst, tv, 1)
                tensor.wait_ge(vv.h, vb + 2)
                inst = tensor.matmul(ps_rep0, B.onesr,
                                     stats[0:1, 4 * G2:5 * G2],
                                     start=True, stop=True)
                B.inc(inst, tv, 1)
                inst = tensor.matmul(ps_rep1, B.onesr,
                                     stats[0:1, 5 * G2:6 * G2],
                                     start=True, stop=True)
                B.inc(inst, tv, 1)
                tensor.wait_ge(vv.h, vb + 3)
                for sb in range(NSUB2):
                    if g + sb > 0:
                        prev = vb + 3 + sb if sb >= 1 else vb - VPGo + 5
                        tensor.wait_ge(vv.h, prev)
                    for k in range(NS):
                        stat = yg[:, k * G2 + sb * 128:k * G2 + (sb + 1) * 128]
                        inst = tensor.matmul(ps_o0, stat,
                                             wout[:, k * D:k * D + 512],
                                             start=(k == 0), stop=(k == NS - 1))
                        B.inc(inst, tv, 1)
                        inst = tensor.matmul(ps_o1, stat,
                                             wout[:, k * D + 512:(k + 1) * D],
                                             start=(k == 0), stop=(k == NS - 1))
                        B.inc(inst, tv, 1)

        @decos["gpsimd"]
        def _(gpsimd):
            for g in range(NG2):
                vb = vv0 + g * VPGo
                for sb in range(NSUB2):
                    gpsimd.wait_ge(vv.h, vb + 3 + sb + 1)
                    r0 = g * G2 + sb * 128
                    inst = gpsimd.dma_start(out=B.dram["xres"][r0:r0 + 128, :],
                                            in_=outt[:, sb * D:(sb + 1) * D])
                    B.inc(inst, st, 16)
                gpsimd.wait_ge(st.h, st.v)

        finish()
    B.phase_idx += 1


MOE_Q = 1024
MOE_NB = MOE_Q // 128
MOE_NE = 24


def _moe_stage1(B, src, q, tl):
    nc = B.nc
    ld, st, tv, vv, sv = B.ld, B.st, B.tv, B.vv, B.sv
    inp = B.inp
    ld0, sv0, vv0, tv0, st0 = ld.v, sv.v, vv.v, tv.v, st.v
    lnT, xg, lnx, qtmp, lnsc, rw_sb, cbs, lg_sb, sc_sb, rt, comb_sb, combT = (
        tl["lnT"], tl["xg"], tl["lnx"], tl["qtmp"], tl["lnsc"], tl["rw"],
        tl["cb"], tl["lg"], tl["sc"], tl["rt"], tl["comb"], tl["combT"])
    NB = MOE_NB
    first = (q == 0 and tl["load_consts"])

    import contextlib
    with contextlib.ExitStack() as _stk:
        def PP(nm, shp):
            return _stk.enter_context(
                nc.psum_tensor(nm + tl["sfx"] + f"_{q}", shp, F32)).ap()

        ps_tr0 = PP("m1tr0", [128, 128])
        ps_tr1 = PP("m1tr1", [128, 128])
        ps_lg = PP("m1lg", [128, 16])
        ps_ct = PP("m1ct", [16, 128])
        ps_tr_s = [ps_tr0, ps_tr1]
        block = _stk.enter_context(nc.Block())
        decos, finish = barrier_wrap(B, block, None)

        cst = 32 if first else 0

        def xw(blk):
            if blk < 2:
                return ld0 + cst + 32
            return ld0 + cst + 16 * (blk + 1)

        @decos["sync"]
        def _(sync):
            if first:
                sync.dma_start(out=rw_sb.rearrange("p (k e) -> p k e", k=8),
                               in_=inp["rw_t"].rearrange("(k p) e -> p k e",
                                                         p=128)
                               ).then_inc(ld.h, 16)
                sync.dma_start(out=cbs, in_=inp["cb_rep"]).then_inc(ld.h, 16)
                ld.v += 32
            for blk in range(NB):
                if blk >= 2:
                    sync.wait_ge(vv.h, vv0 + (blk - 2) * 4 + 2)
                r0 = q * MOE_Q + blk * 128
                sync.dma_start(out=xg[:, (blk % 2) * D:(blk % 2 + 1) * D],
                               in_=src[r0:r0 + 128, :]).then_inc(ld.h, 16)
                ld.v += 16

        @decos["scalar"]
        def _(scalar):
            for blk in range(NB):
                sbb = sv0 + blk * 12
                vb = vv0 + blk * 4
                tb = tv0 + blk * 17
                scalar.wait_ge(ld.h, xw(blk))
                if blk >= 2:
                    scalar.wait_ge(vv.h, vv0 + (blk - 2) * 4 + 1)
                inst = scalar.activation(
                    out=qtmp[:, (blk % 2) * D:(blk % 2 + 1) * D],
                    in_=xg[:, (blk % 2) * D:(blk % 2 + 1) * D],
                    func=ACT.Square)
                B.inc(inst, sv, 1)
                scalar.wait_ge(vv.h, vb + 1)
                scalar.activation(out=lnsc[:, 4:5], in_=lnsc[:, 3:4],
                                  func=ACT.Ln, bias=EPS)
                scalar.drain()
                scalar.activation(out=lnsc[:, 4:5], in_=lnsc[:, 4:5],
                                  func=ACT.Exp, scale=-0.5)
                scalar.drain()
                scalar.sem_inc(sv.h, 1)
                sv.v += 1
                for k in range(8):
                    scalar.wait_ge(tv.h, tb + k + 1)
                    inst = scalar.activation(
                        out=lnT[:, k * MOE_Q + blk * 128:
                                k * MOE_Q + (blk + 1) * 128],
                        in_=ps_tr_s[k % 2], func=ACT.Copy)
                    B.inc(inst, sv, 1)
                scalar.wait_ge(vv.h, vb + 3)
                inst = scalar.activation(out=sc_sb, in_=lg_sb,
                                         func=ACT.Sigmoid)
                B.inc(inst, sv, 1)
                scalar.wait_ge(tv.h, tb + 17)
                inst = scalar.activation(
                    out=combT[:, blk * 128:(blk + 1) * 128],
                    in_=ps_ct, func=ACT.Copy)
                B.inc(inst, sv, 1)

        @decos["vector"]
        def _(vector):
            for blk in range(NB):
                sbb = sv0 + blk * 12
                vb = vv0 + blk * 4
                tb = tv0 + blk * 17
                xs_ = xg[:, (blk % 2) * D:(blk % 2 + 1) * D]
                sq_ = qtmp[:, (blk % 2) * D:(blk % 2 + 1) * D]
                vector.wait_ge(ld.h, xw(blk))
                vector.wait_ge(sv.h, sbb + 1)
                B.ln_tile(vector, None, xs_, None, sq_, lnsc)
                B.raw_inc(vector, vv, 1)
                vector.wait_ge(sv.h, sbb + 2)
                if blk >= 2:
                    vector.wait_ge(tv.h, tv0 + (blk - 2) * 17 + 8)
                B.ln_apply(vector, xs_, lnx[:, (blk % 2) * D:(blk % 2 + 1) * D],
                           lnsc[:, 2:3], lnsc[:, 4:5], lnsc[:, 5:6])
                B.raw_inc(vector, vv, 1)
                vector.wait_ge(tv.h, tb + 16)
                if blk >= 1:
                    vector.wait_ge(sv.h, sv0 + (blk - 1) * 12 + 11)
                vector.tensor_copy(out=lg_sb, in_=ps_lg)
                B.raw_inc(vector, vv, 1)
                vector.wait_ge(sv.h, sbb + 11)
                if blk >= 1:
                    vector.wait_ge(tv.h, tv0 + (blk - 1) * 17 + 17)
                sbv = rt[:, 0:16]
                ptmp = rt[:, 16:40]
                gsc = rt[:, 40:44]
                g2 = rt[:, 44:48]
                g3 = rt[:, 48:52]
                mtmp = rt[:, 52:58]
                thr2 = rt[:, 58:59]
                gmb = rt[:, 59:63]
                sbm = rt[:, 64:80]
                cnt = rt[:, 80:96]
                sel = rt[:, 96:112]
                wsum = rt[:, 112:113]
                wrcp = rt[:, 113:114]
                vector.tensor_tensor(out=sbv, in0=sc_sb, in1=cbs, op=ALU.add)
                vector.drain()
                sb3 = sbv.rearrange("p (g e) -> p g e", g=4)
                pairs = [(0, 1), (0, 2), (0, 3), (1, 2), (1, 3), (2, 3)]
                for ix, (i, j) in enumerate(pairs):
                    vector.tensor_tensor(out=ptmp[:, ix * 4:(ix + 1) * 4],
                                         in0=sb3[:, :, i], in1=sb3[:, :, j],
                                         op=ALU.add)
                vector.drain()
                vector.tensor_tensor(out=gsc, in0=ptmp[:, 0:4],
                                     in1=ptmp[:, 4:8], op=ALU.max)
                vector.tensor_tensor(out=g2, in0=ptmp[:, 8:12],
                                     in1=ptmp[:, 12:16], op=ALU.max)
                vector.tensor_tensor(out=g3, in0=ptmp[:, 16:20],
                                     in1=ptmp[:, 20:24], op=ALU.max)
                vector.drain()
                vector.tensor_tensor(out=gsc, in0=gsc, in1=g2, op=ALU.max)
                vector.drain()
                vector.tensor_tensor(out=gsc, in0=gsc, in1=g3, op=ALU.max)
                vector.drain()
                for ix, (i, j) in enumerate(pairs):
                    vector.tensor_tensor(out=mtmp[:, ix:ix + 1],
                                         in0=gsc[:, i:i + 1],
                                         in1=gsc[:, j:j + 1], op=ALU.min)
                B.red(vector, thr2, mtmp, ALU.max)
                vector.tensor_scalar(out=gmb, in0=gsc, scalar1=thr2,
                                     scalar2=1.0, op0=ALU.is_ge,
                                     op1=ALU.subtract)
                vector.drain()
                vector.tensor_single_scalar(out=gmb, in_=gmb, scalar=1e30,
                                            op=ALU.mult)
                vector.drain()
                gm4 = gmb.unsqueeze(2).broadcast_to([128, 4, 4])
                vector.tensor_tensor(out=sbm.rearrange("p (g e) -> p g e", g=4),
                                     in0=sb3, in1=gm4, op=ALU.add)
                vector.drain()
                vector.tensor_scalar(out=cnt, in0=sbm, scalar1=sbm[:, 0:1],
                                     scalar2=None, op0=ALU.is_lt)
                vector.drain()
                for j in range(1, 16):
                    vector.scalar_tensor_tensor(out=cnt, in0=sbm,
                                                scalar=sbm[:, j:j + 1],
                                                in1=cnt, op0=ALU.is_lt,
                                                op1=ALU.add)
                    vector.drain()
                vector.tensor_single_scalar(out=sel, in_=cnt, scalar=3.5,
                                            op=ALU.is_le)
                vector.drain()
                vector.tensor_tensor(out=sel, in0=sel, in1=sc_sb, op=ALU.mult)
                B.red(vector, wsum, sel, ALU.add)
                vector.tensor_single_scalar(out=wsum, in_=wsum, scalar=1e-20,
                                            op=ALU.add)
                B.recip(vector, wrcp, wsum)
                vector.tensor_scalar(out=comb_sb, in0=sel, scalar1=wrcp,
                                     scalar2=None, op0=ALU.mult)
                B.raw_inc(vector, vv, 1)

        @decos["tensor"]
        def _(tensor):
            for blk in range(NB):
                sbb = sv0 + blk * 12
                vb = vv0 + blk * 4
                tb = tv0 + blk * 17
                tensor.wait_ge(vv.h, vb + 2)
                for k in range(8):
                    if k >= 2:
                        tensor.wait_ge(sv.h, sbb + 3 + (k - 2))
                    elif blk >= 1:
                        tensor.wait_ge(sv.h, sv0 + (blk - 1) * 12 + 10)
                    inst = tensor.transpose(
                        ps_tr_s[k % 2],
                        lnx[:, (blk % 2) * D + k * 128:
                            (blk % 2) * D + (k + 1) * 128], B.ids)
                    B.inc(inst, tv, 1)
                if blk >= 1:
                    tensor.wait_ge(vv.h, vv0 + (blk - 1) * 4 + 3)
                if first and blk == 0:
                    tensor.wait_ge(ld.h, ld0 + 64)
                for k in range(8):
                    tensor.wait_ge(sv.h, sbb + 3 + k)
                    inst = tensor.matmul(
                        ps_lg,
                        lnT[:, k * MOE_Q + blk * 128:
                            k * MOE_Q + (blk + 1) * 128],
                        rw_sb[:, k * 16:(k + 1) * 16],
                        start=(k == 0), stop=(k == 7))
                    B.inc(inst, tv, 1)
                tensor.wait_ge(vv.h, vb + 4)
                if blk >= 1:
                    tensor.wait_ge(sv.h, sv0 + (blk - 1) * 12 + 12)
                inst = tensor.transpose(ps_ct, comb_sb, B.ids)
                B.inc(inst, tv, 1)

        @decos["gpsimd"]
        def _(gpsimd):
            gpsimd.wait_ge(sv.h, sv0 + (NB - 1) * 12 + 12)
            inst = gpsimd.dma_start(
                out=B.dram["comb_fm"][:, q * MOE_Q:(q + 1) * MOE_Q],
                in_=combT)
            B.inc(inst, st, 16)
            gpsimd.wait_ge(st.h, st.v)

        finish()
    B.phase_idx += 1


def _moe_stage2(B, src, dst, q, tl):
    nc = B.nc
    ld, st, tv, vv, sv = B.ld, B.st, B.tv, B.vv, B.sv
    inp = B.inp
    ld0, sv0, vv0, tv0, st0 = ld.v, sv.v, vv.v, tv.v, st.v
    lnT, acc, gsb, usb, dsb, cfm, hfm, sil, htmp, xg2, outf = (
        tl["lnT"], tl["acc"], tl["gsb"], tl["usb"], tl["dsb"], tl["cfm"],
        tl["hfm"], tl["sil"], tl["htmp"], tl["xg2"], tl["outf"])
    NE = MOE_NE
    NB = MOE_NB

    t_gate, t_up, t_down = {}, {}, {}
    tp = tv0
    for e in range(NE):
        for tg in range(2):
            for hh in range(2):
                tp += 8
                t_gate[(e, tg, hh)] = tp
                tp += 8
                t_up[(e, tg, hh)] = tp
            for ts in range(4):
                tp += 4
                t_down[(e, tg, ts)] = tp
    s_silu = {}
    sp = sv0
    for e in range(NE):
        for tg in range(2):
            for hh in range(2):
                sp += 1
                s_silu[(e, tg, hh)] = sp
    v_h, v_acc, v_fin = {}, {}, {}
    vp = vv0
    for e in range(NE):
        for tg in range(2):
            for hh in range(2):
                vp += 1
                v_h[(e, tg, hh)] = vp
            for ts in range(4):
                vp += 1
                v_acc[(e, tg, ts)] = vp
    for blk in range(NB):
        vp += 1
        v_fin[blk] = vp
    ld_e, ld_x = {}, {}
    lp = ld0
    for e in range(NE):
        lp += 48 + (16 if e < 16 else 0)
        ld_e[e] = lp
    for blk in range(NB):
        lp += 16
        ld_x[blk] = lp
    st_fin = {}
    stp = st0
    for blk in range(NB):
        stp += 16
        st_fin[blk] = stp

    def ldw_e(e):
        return ld_e[1] if e < 2 else ld_e[e]

    def ldw_x(blk):
        return ld_x[1] if blk < 2 else ld_x[blk]

    import contextlib
    with contextlib.ExitStack() as _stk:
        def PP(nm, shp):
            return _stk.enter_context(
                nc.psum_tensor(nm + tl["sfx"] + f"_{q}", shp, F32)).ap()

        ps_g = [PP("m2g0", [128, 512]), PP("m2g1", [128, 512])]
        ps_u = [PP("m2u0", [128, 512]), PP("m2u1", [128, 512])]
        ps_o0 = PP("m2o0", [128, 512])
        ps_o1 = PP("m2o1", [128, 512])
        block = _stk.enter_context(nc.Block())
        decos, finish = barrier_wrap(B, block, None)
        gsb3 = gsb.rearrange("p (s k h) -> p s k h", s=2, k=8)
        usb3 = usb.rearrange("p (s k h) -> p s k h", s=2, k=8)
        dsb3 = dsb.rearrange("p (s c o) -> p s c o", s=2, c=2)
        hfm3 = hfm.rearrange("p (s h t) -> p s h t", s=2, h=2)

        @decos["sync"]
        def _(sync):
            for e in range(NE):
                if e >= 2:
                    sync.wait_ge(tv.h, t_down[(e - 2, 1, 3)])
                sl = e % 2
                if e < 16:
                    gs = inp["gate_w"][e].rearrange("(k p) h -> p k h", p=128)
                    us = inp["up_w"][e].rearrange("(k p) h -> p k h", p=128)
                    ds = inp["down_w"][e].rearrange("(c p) o -> p c o", p=128)
                else:
                    es = e - 16
                    gs = inp["sgw_t"][:, es * 256:(es + 1) * 256].rearrange(
                        "(k p) h -> p k h", p=128)
                    us = inp["suw_t"][:, es * 256:(es + 1) * 256].rearrange(
                        "(k p) h -> p k h", p=128)
                    ds = inp["sdw_t"][es * 256:(es + 1) * 256, :].rearrange(
                        "(c p) o -> p c o", p=128)
                sync.dma_start(out=gsb3[:, sl], in_=gs).then_inc(ld.h, 16)
                sync.dma_start(out=usb3[:, sl], in_=us).then_inc(ld.h, 16)
                sync.dma_start(out=dsb3[:, sl], in_=ds).then_inc(ld.h, 16)
                ld.v += 48
                if e < 16:
                    if e >= 2:
                        sync.wait_ge(vv.h, v_h[(e - 2, 1, 1)])
                    csrc = B.dram["comb_fm"][e:e + 1, q * MOE_Q:(q + 1) * MOE_Q]
                    csrc = csrc.broadcast_to([128, MOE_Q])
                    sync.dma_start(out=cfm[:, sl * MOE_Q:(sl + 1) * MOE_Q],
                                   in_=csrc).then_inc(ld.h, 16)
                    ld.v += 16
            for blk in range(NB):
                if blk >= 2:
                    sync.wait_ge(vv.h, v_fin[blk - 2])
                r0 = q * MOE_Q + blk * 128
                sync.dma_start(out=xg2[:, (blk % 2) * D:(blk % 2 + 1) * D],
                               in_=src[r0:r0 + 128, :]).then_inc(ld.h, 16)
                ld.v += 16

        @decos["tensor"]
        def _(tensor):
            for e in range(NE):
                tensor.wait_ge(ld.h, ldw_e(e))
                sl = e % 2
                for tg in range(2):
                    prev = (e, tg - 1) if tg >= 1 else \
                        ((e - 1, 1) if e >= 1 else None)
                    for hh in range(2):
                        if prev is not None:
                            tensor.wait_ge(sv.h,
                                           s_silu[(prev[0], prev[1], hh)])
                        for k in range(8):
                            inst = tensor.matmul(
                                ps_g[hh],
                                gsb3[:, sl, k, hh * 128:(hh + 1) * 128],
                                lnT[:, k * MOE_Q + tg * 512:
                                    k * MOE_Q + (tg + 1) * 512],
                                start=(k == 0), stop=(k == 7))
                            B.inc(inst, tv, 1)
                        if prev is not None:
                            tensor.wait_ge(vv.h,
                                           v_h[(prev[0], prev[1], hh)])
                        for k in range(8):
                            inst = tensor.matmul(
                                ps_u[hh],
                                usb3[:, sl, k, hh * 128:(hh + 1) * 128],
                                lnT[:, k * MOE_Q + tg * 512:
                                    k * MOE_Q + (tg + 1) * 512],
                                start=(k == 0), stop=(k == 7))
                            B.inc(inst, tv, 1)
                    tensor.wait_ge(vv.h, v_h[(e, tg, 1)])
                    for ts in range(4):
                        pts = (e, tg, ts - 1) if ts >= 1 else \
                            ((e, tg - 1, 3) if tg >= 1 else
                             ((e - 1, 1, 3) if e >= 1 else None))
                        if pts is not None:
                            tensor.wait_ge(vv.h, v_acc[pts])
                        for hh in range(2):
                            inst = tensor.matmul(
                                ps_o0, hfm3[:, tg, hh, ts * 128:(ts + 1) * 128],
                                dsb3[:, sl, hh, 0:512],
                                start=(hh == 0), stop=(hh == 1))
                            B.inc(inst, tv, 1)
                            inst = tensor.matmul(
                                ps_o1, hfm3[:, tg, hh, ts * 128:(ts + 1) * 128],
                                dsb3[:, sl, hh, 512:1024],
                                start=(hh == 0), stop=(hh == 1))
                            B.inc(inst, tv, 1)

        @decos["scalar"]
        def _(scalar):
            for e in range(NE):
                for tg in range(2):
                    prev = (e, tg - 1) if tg >= 1 else \
                        ((e - 1, 1) if e >= 1 else None)
                    for hh in range(2):
                        scalar.wait_ge(tv.h, t_gate[(e, tg, hh)])
                        if prev is not None:
                            scalar.wait_ge(vv.h,
                                           v_h[(prev[0], prev[1], hh)])
                        inst = scalar.activation(
                            out=sil[:, hh * 512:(hh + 1) * 512],
                            in_=ps_g[hh], func=ACT.Silu)
                        B.inc(inst, sv, 1)

        @decos["vector"]
        def _(vector):
            for e in range(NE):
                sl = e % 2
                for tg in range(2):
                    for hh in range(2):
                        vector.wait_ge(sv.h, s_silu[(e, tg, hh)])
                        vector.wait_ge(tv.h, t_up[(e, tg, hh)])
                        if hh == 0 and e >= 1:
                            vector.wait_ge(tv.h, t_down[(e - 1, tg, 3)])
                        dstc = hfm3[:, tg, hh, :]
                        if e < 16:
                            vector.wait_ge(ld.h, ldw_e(e))
                            vector.tensor_tensor(
                                out=htmp, in0=sil[:, hh * 512:(hh + 1) * 512],
                                in1=ps_u[hh], op=ALU.mult)
                            vector.tensor_tensor(
                                out=dstc, in0=htmp,
                                in1=cfm[:, sl * MOE_Q + tg * 512:
                                        sl * MOE_Q + (tg + 1) * 512],
                                op=ALU.mult)
                        else:
                            vector.tensor_tensor(
                                out=dstc, in0=sil[:, hh * 512:(hh + 1) * 512],
                                in1=ps_u[hh], op=ALU.mult)
                        B.raw_inc(vector, vv, 1)
                    for ts in range(4):
                        vector.wait_ge(tv.h, t_down[(e, tg, ts)])
                        col = (tg * 4 + ts) * D
                        if e == 0:
                            vector.tensor_copy(out=acc[:, col:col + 512],
                                               in_=ps_o0)
                            vector.tensor_copy(out=acc[:, col + 512:col + 1024],
                                               in_=ps_o1)
                        else:
                            vector.tensor_tensor(out=acc[:, col:col + 512],
                                                 in0=acc[:, col:col + 512],
                                                 in1=ps_o0, op=ALU.add)
                            vector.tensor_tensor(
                                out=acc[:, col + 512:col + 1024],
                                in0=acc[:, col + 512:col + 1024],
                                in1=ps_o1, op=ALU.add)
                        B.raw_inc(vector, vv, 1)
            for blk in range(NB):
                vector.wait_ge(ld.h, ldw_x(blk))
                if blk >= 2:
                    vector.wait_ge(st.h, st_fin[blk - 2])
                vector.scalar_tensor_tensor(
                    out=outf[:, (blk % 2) * D:(blk % 2 + 1) * D],
                    in0=acc[:, blk * D:(blk + 1) * D], scalar=0.5,
                    in1=xg2[:, (blk % 2) * D:(blk % 2 + 1) * D],
                    op0=ALU.mult, op1=ALU.add)
                B.raw_inc(vector, vv, 1)

        @decos["gpsimd"]
        def _(gpsimd):
            for blk in range(NB):
                gpsimd.wait_ge(vv.h, v_fin[blk])
                r0 = q * MOE_Q + blk * 128
                inst = gpsimd.dma_start(
                    out=dst[r0:r0 + 128, :],
                    in_=outf[:, (blk % 2) * D:(blk % 2 + 1) * D])
                B.inc(inst, st, 16)
                gpsimd.wait_ge(st.h, st.v)

        finish()
    B.phase_idx += 1


def phase_moe(B, src, dst):
    nc = B.nc
    sfx = f"_{B.phase_idx}"
    import contextlib
    with contextlib.ExitStack() as _stk:
        def A(nm, shp):
            return _stk.enter_context(nc.sbuf_tensor(nm + sfx, shp, F32)).ap()

        tl = {
            "sfx": sfx,
            "load_consts": True,
            "lnT": A("mo_lnT", [128, 8 * MOE_Q]),
            "acc": A("mo_acc", [128, MOE_NB * D]),
            "xg": A("mo_x", [128, 2 * D]),
            "lnx": A("mo_ln", [128, 2 * D]),
            "qtmp": A("mo_sq", [128, 2 * D]),
            "lnsc": A("mo_st", [128, 8]),
            "rw": A("mo_rw", [128, 128]),
            "cb": A("mo_cb", [128, 16]),
            "lg": A("mo_lg", [128, 16]),
            "sc": A("mo_sc", [128, 16]),
            "rt": A("mo_rt", [128, 128]),
            "comb": A("mo_comb", [128, 16]),
            "combT": A("mo_combT", [16, MOE_Q]),
            "gsb": A("mo_gw", [128, 2 * 8 * 256]),
            "usb": A("mo_uw", [128, 2 * 8 * 256]),
            "dsb": A("mo_dw", [128, 2 * 2 * D]),
            "cfm": A("mo_cfm", [128, 2 * MOE_Q]),
            "hfm": A("mo_hfm", [128, 2 * 2 * 512]),
            "sil": A("mo_sil", [128, 2 * 512]),
            "htmp": A("mo_ht", [128, 512]),
            "xg2": A("mo_x2", [128, 2 * D]),
            "outf": A("mo_out", [128, 2 * D]),
        }
        for q in range(4):
            _moe_stage1(B, src, q, tl)
            _moe_stage2(B, src, dst, q, tl)
            tl["load_consts"] = False


def _attn_qkv(B, tl):
    nc = B.nc
    ld, st, tv, vv, sv = B.ld, B.st, B.tv, B.vv, B.sv
    inp = B.inp
    ld0, sv0, vv0, tv0, st0 = ld.v, sv.v, vv.v, tv.v, st.v
    wq_sb, wk_sb, wv_sb, lnT, qs, ks, vs, xg, lnx, qtmp, lnsc, svrow = (
        tl["wq"], tl["wk"], tl["wv"], tl["lnT"], tl["qs"], tl["ks"], tl["vs"],
        tl["xg"], tl["lnx"], tl["qtmp"], tl["lnsc"], tl["svrow"])
    NGR = 8

    s_sq, s_inv, s_ev, s_unit = {}, {}, {}, {}
    sp = sv0
    for g in range(NGR):
        for sb in range(4):
            u = g * 4 + sb
            sp += 1
            s_sq[u] = sp
            sp += 1
            s_inv[u] = sp
            if sb >= 1:
                for k in range(8):
                    sp += 1
                    s_ev[(g * 4 + sb - 1, k)] = sp
        for k in range(8):
            sp += 1
            s_ev[(g * 4 + 3, k)] = sp
        for j in range(24):
            sp += 1
            s_unit[(g, j)] = sp
    sv_fin = sp + 2
    v_stats, v_apply = {}, {}
    vp = vv0
    for u in range(4 * NGR):
        vp += 1
        v_stats[u] = vp
        vp += 1
        v_apply[u] = vp
    t_tr, t_trd, t_unit, t_sv = {}, {}, {}, {}
    tp = tv0
    for g in range(NGR):
        for sb in range(4):
            u = g * 4 + sb
            for k in range(8):
                tp += 1
                t_tr[(u, k)] = tp
            t_trd[u] = tp
        for j in range(24):
            tp += 8
            t_unit[(g, j)] = tp
        for idx in range(8):
            tp += 1
            t_sv[(g, idx)] = tp
    st_q, st_k, st_v = {}, {}, {}
    stp = st0
    for g in range(NGR):
        stp += 16
        st_q[g] = stp
        stp += 16
        st_k[g] = stp
        for ts in range(4):
            stp += 16
            st_v[(g, ts)] = stp

    def xw(u):
        return ld0 + 80 if u < 2 else ld0 + 48 + 16 * (u + 1)

    import contextlib
    with contextlib.ExitStack() as _stk:
        def PP(nm, shp):
            return _stk.enter_context(nc.psum_tensor(nm, shp, F32)).ap()

        ps_tr = [PP("aqtr0", [128, 128]), PP("aqtr1", [128, 128])]
        ps_mm = [PP(f"aqmm{i}", [128, 512]) for i in range(4)]
        ps_sv = [PP("aqsv0", [1, 512]), PP("aqsv1", [1, 512])]
        block = _stk.enter_context(nc.Block())
        decos, finish = barrier_wrap(B, block, None)
        wq3 = wq_sb.rearrange("p (k o) -> p k o", k=8)
        wk3 = wk_sb.rearrange("p (k o) -> p k o", k=8)
        wv3 = wv_sb.rearrange("p (k o) -> p k o", k=8)

        @decos["sync"]
        def _(sync):
            sync.dma_start(out=wq3, in_=inp["wq"].rearrange(
                "(k p) o -> p k o", p=128)).then_inc(ld.h, 16)
            sync.dma_start(out=wk3, in_=inp["wk"].rearrange(
                "(k p) o -> p k o", p=128)).then_inc(ld.h, 16)
            sync.dma_start(out=wv3, in_=inp["wv"].rearrange(
                "(k p) o -> p k o", p=128)).then_inc(ld.h, 16)
            ld.v += 48
            for u in range(4 * NGR):
                if u >= 2:
                    sync.wait_ge(vv.h, v_apply[u - 2])
                r0 = u * 128
                sync.dma_start(out=xg[:, (u % 2) * D:(u % 2 + 1) * D],
                               in_=B.dram["xres"][r0:r0 + 128, :]
                               ).then_inc(ld.h, 16)
                ld.v += 16

        @decos["scalar"]
        def _(scalar):
            def emit_ev(scalar, u):
                g, sb = u // 4, u % 4
                for k in range(8):
                    scalar.wait_ge(tv.h, t_tr[(u, k)])
                    if k == 0 and sb == 0 and g >= 1:
                        scalar.wait_ge(tv.h, t_unit[(g - 1, 23)])
                    inst = scalar.activation(
                        out=lnT[:, k * 512 + sb * 128:k * 512 + (sb + 1) * 128],
                        in_=ps_tr[k % 2], func=ACT.Copy)
                    B.inc(inst, sv, 1)

            for g in range(NGR):
                for sb in range(4):
                    u = g * 4 + sb
                    scalar.wait_ge(ld.h, xw(u))
                    if u >= 1:
                        scalar.wait_ge(vv.h, v_stats[u - 1])
                    inst = scalar.activation(
                        out=qtmp, in_=xg[:, (u % 2) * D:(u % 2 + 1) * D],
                        func=ACT.Square)
                    B.inc(inst, sv, 1)
                    scalar.wait_ge(vv.h, v_stats[u])
                    scalar.activation(out=lnsc[:, 4:5], in_=lnsc[:, 3:4],
                                      func=ACT.Ln, bias=EPS)
                    scalar.drain()
                    scalar.activation(out=lnsc[:, 4:5], in_=lnsc[:, 4:5],
                                      func=ACT.Exp, scale=-0.5)
                    scalar.drain()
                    scalar.sem_inc(sv.h, 1)
                    sv.v += 1
                    if sb >= 1:
                        emit_ev(scalar, u - 1)
                emit_ev(scalar, g * 4 + 3)
                for j in range(24):
                    scalar.wait_ge(tv.h, t_unit[(g, j)])
                    if g >= 1:
                        if j == 0:
                            scalar.wait_ge(st.h, st_q[g - 1])
                        elif j == 8:
                            scalar.wait_ge(st.h, st_k[g - 1])
                        elif j >= 16 and (j - 16) % 2 == 0:
                            scalar.wait_ge(st.h, st_v[(g - 1, (j - 16) // 2)])
                    if j < 8:
                        dstc = qs[:, j * 512:(j + 1) * 512]
                    elif j < 16:
                        dstc = ks[:, (j - 8) * 512:(j - 7) * 512]
                    else:
                        ts, hf = (j - 16) // 2, (j - 16) % 2
                        dstc = vs[:, ts * 1024 + hf * 512:
                                  ts * 1024 + (hf + 1) * 512]
                    inst = scalar.activation(out=dstc, in_=ps_mm[j % 4],
                                             func=ACT.Copy)
                    B.inc(inst, sv, 1)
            scalar.wait_ge(tv.h, t_sv[(NGR - 1, 7)])
            inst = scalar.activation(out=svrow[0:1, 0:512], in_=ps_sv[0],
                                     func=ACT.Copy)
            B.inc(inst, sv, 1)
            inst = scalar.activation(out=svrow[0:1, 512:1024], in_=ps_sv[1],
                                     func=ACT.Copy)
            B.inc(inst, sv, 1)

        @decos["vector"]
        def _(vector):
            for u in range(4 * NGR):
                xs_ = xg[:, (u % 2) * D:(u % 2 + 1) * D]
                vector.wait_ge(ld.h, xw(u))
                vector.wait_ge(sv.h, s_sq[u])
                B.ln_tile(vector, None, xs_, None, qtmp, lnsc)
                B.raw_inc(vector, vv, 1)
                vector.wait_ge(sv.h, s_inv[u])
                if u >= 2:
                    vector.wait_ge(tv.h, t_trd[u - 2])
                B.ln_apply(vector, xs_, lnx[:, (u % 2) * D:(u % 2 + 1) * D],
                           lnsc[:, 2:3], lnsc[:, 4:5], lnsc[:, 5:6])
                B.raw_inc(vector, vv, 1)

        @decos["tensor"]
        def _(tensor):
            tensor.wait_ge(ld.h, ld0 + 80)
            for g in range(NGR):
                for sb in range(4):
                    u = g * 4 + sb
                    tensor.wait_ge(vv.h, v_apply[u])
                    for k in range(8):
                        if k >= 2:
                            tensor.wait_ge(sv.h, s_ev[(u, k - 2)])
                        elif u >= 1:
                            tensor.wait_ge(sv.h, s_ev[(u - 1, 6 + k)])
                        inst = tensor.transpose(
                            ps_tr[k % 2],
                            lnx[:, (u % 2) * D + k * 128:
                                (u % 2) * D + (k + 1) * 128], B.ids)
                        B.inc(inst, tv, 1)
                tensor.wait_ge(sv.h, s_ev[(g * 4 + 3, 7)])
                for j in range(24):
                    gp = (g, j - 4)
                    if j < 4:
                        gp = (g - 1, 20 + j) if g >= 1 else None
                    if gp is not None:
                        tensor.wait_ge(sv.h, s_unit[gp])
                    for k in range(8):
                        if j < 8:
                            lhs = wq3[:, k, j * 128:(j + 1) * 128]
                            rhs = lnT[:, k * 512:(k + 1) * 512]
                        elif j < 16:
                            lhs = wk3[:, k, (j - 8) * 128:(j - 7) * 128]
                            rhs = lnT[:, k * 512:(k + 1) * 512]
                        else:
                            ts, hf = (j - 16) // 2, (j - 16) % 2
                            lhs = lnT[:, k * 512 + ts * 128:
                                      k * 512 + (ts + 1) * 128]
                            rhs = wv3[:, k, hf * 512:(hf + 1) * 512]
                        inst = tensor.matmul(ps_mm[j % 4], lhs, rhs,
                                             start=(k == 0), stop=(k == 7))
                        B.inc(inst, tv, 1)
                for idx in range(8):
                    ts, hf = idx // 2, idx % 2
                    tensor.wait_ge(sv.h, s_unit[(g, 16 + idx)])
                    inst = tensor.matmul(
                        ps_sv[hf], B.onesc,
                        vs[:, ts * 1024 + hf * 512:ts * 1024 + (hf + 1) * 512],
                        start=(g == 0 and ts == 0), stop=(g == NGR - 1 and
                                                          ts == 3))
                    B.inc(inst, tv, 1)

        @decos["gpsimd"]
        def _(gpsimd):
            for g in range(NGR):
                t0c = g * 512
                gpsimd.wait_ge(sv.h, s_unit[(g, 7)])
                dstq = B.dram["q_fm"].rearrange("(c p) t -> p c t", p=128)
                inst = gpsimd.dma_start(
                    out=dstq[:, :, t0c:t0c + 512],
                    in_=qs.rearrange("p (c t) -> p c t", c=8))
                B.inc(inst, st, 16)
                gpsimd.wait_ge(st.h, st.v)
                gpsimd.wait_ge(sv.h, s_unit[(g, 15)])
                dstk = B.dram["k_fm"].rearrange("(c p) t -> p c t", p=128)
                inst = gpsimd.dma_start(
                    out=dstk[:, :, t0c:t0c + 512],
                    in_=ks.rearrange("p (c t) -> p c t", c=8))
                B.inc(inst, st, 16)
                gpsimd.wait_ge(st.h, st.v)
                for ts in range(4):
                    gpsimd.wait_ge(sv.h, s_unit[(g, 16 + ts * 2 + 1)])
                    r0 = t0c + ts * 128
                    inst = gpsimd.dma_start(
                        out=B.dram["v_tok"][r0:r0 + 128, :],
                        in_=vs[:, ts * 1024:(ts + 1) * 1024])
                    B.inc(inst, st, 16)
                    gpsimd.wait_ge(st.h, st.v)

        finish()
    B.phase_idx += 1


def _attn_core(B, tl):
    nc = B.nc
    ld, st, tv, vv, sv = B.ld, B.st, B.tv, B.vv, B.sv
    inp = B.inp
    ld0, sv0, vv0, tv0, st0 = ld.v, sv.v, vv.v, tv.v, st.v
    qh_sb, kh_sb, vh_sb, svrow, svrep, msk, e_sb, et_sb, e0T, o_blk, mt = (
        tl["qh"], tl["kh"], tl["vh"], tl["svrow"], tl["svrep"], tl["msk"],
        tl["e_sb"], tl["et_sb"], tl["e0T"], tl["o_blk"], tl["mt"])
    NH, NQ = 16, 32

    def b0_of(qi):
        return min(max((qi - 1) * 128, 0), T - 384)

    def mvar(qi):
        return 0 if qi == 0 else (2 if qi == NQ - 1 else 1)

    def prev_it(h, qi):
        if qi >= 1:
            return (h, qi - 1)
        return (h - 1, NQ - 1) if h >= 1 else None

    t_m, t_band, t_tr, t_av = {}, {}, {}, {}
    tp = tv0 + 2
    for h in range(NH):
        for qi in range(NQ):
            for ch in range(8):
                tp += 1
                t_m[(h, qi, ch)] = tp
            tp += 1
            t_band[(h, qi)] = tp
            for c in range(3):
                tp += 1
                t_tr[(h, qi, c)] = tp
            tp += 3
            t_av[(h, qi)] = tp
    s_e, s_ze, s_trev = {}, {}, {}
    sp = sv0 + 2
    for h in range(NH):
        for qi in range(NQ):
            sp += 1
            s_e[(h, qi)] = sp
            sp += 1
            s_ze[(h, qi)] = sp
            for c in range(3):
                sp += 1
                s_trev[(h, qi, c)] = sp
    v_red, v_mneg, v_m2, v_e0, v_comb = {}, {}, {}, {}, {}
    vp = vv0
    for h in range(NH):
        for qi in range(NQ):
            for ch in range(8):
                vp += 1
                v_red[(h, qi, ch)] = vp
            vp += 1
            v_mneg[(h, qi)] = vp
            vp += 1
            v_m2[(h, qi)] = vp
            vp += 1
            v_e0[(h, qi)] = vp
            vp += 1
            v_comb[(h, qi)] = vp
    st_o = {}
    stp = st0
    for h in range(NH):
        for qi in range(NQ):
            stp += 16
            st_o[(h, qi)] = stp

    def ld_h(h):
        return ld0 + 112 if h < 2 else ld0 + 16 + 48 * (h + 1)

    import contextlib
    with contextlib.ExitStack() as _stk:
        def PP(nm, shp):
            return _stk.enter_context(nc.psum_tensor(nm, shp, F32)).ap()

        ps_m = [PP("acm0", [128, 512]), PP("acm1", [128, 512])]
        ps_b = PP("acb", [128, 384])
        ps_tr = [PP("actr0", [128, 128]), PP("actr1", [128, 128])]
        ps_av = PP("acav", [128, 64])
        ps_rep = PP("acrep", [128, 512])
        block = _stk.enter_context(nc.Block())
        decos, finish = barrier_wrap(B, block, None)
        vh3 = vh_sb.rearrange("p (s c f) -> p s c f", s=2, c=32)

        @decos["sync"]
        def _(sync):
            sync.dma_start(out=msk.rearrange("p (v c) -> p v c", v=3),
                           in_=inp["win_mask"].rearrange("v p c -> p v c")
                           ).then_inc(ld.h, 16)
            ld.v += 16
            for h in range(NH):
                if h >= 2:
                    sync.wait_ge(tv.h, t_av[(h - 2, NQ - 1)])
                sl = h % 2
                sync.dma_start(out=qh_sb[:, sl * T:(sl + 1) * T],
                               in_=B.dram["q_fm"][h * 64:(h + 1) * 64, :]
                               ).then_inc(ld.h, 16)
                sync.dma_start(out=kh_sb[:, sl * T:(sl + 1) * T],
                               in_=B.dram["k_fm"][h * 64:(h + 1) * 64, :]
                               ).then_inc(ld.h, 16)
                sync.dma_start(out=vh3[:, sl],
                               in_=B.dram["v_tok"][:, h * 64:(h + 1) * 64]
                               .rearrange("(c p) f -> p c f", p=128)
                               ).then_inc(ld.h, 16)
                ld.v += 48

        @decos["tensor"]
        def _(tensor):
            inst = tensor.matmul(ps_rep, B.onesr, svrow[0:1, 0:512],
                                 start=True, stop=True)
            B.inc(inst, tv, 1)
            tensor.wait_ge(sv.h, sv0 + 1)
            inst = tensor.matmul(ps_rep, B.onesr, svrow[0:1, 512:1024],
                                 start=True, stop=True)
            B.inc(inst, tv, 1)
            for h in range(NH):
                tensor.wait_ge(ld.h, ld_h(h))
                sl = h % 2
                for qi in range(NQ):
                    pit = prev_it(h, qi)
                    qcol = qh_sb[:, sl * T + qi * 128:sl * T + (qi + 1) * 128]
                    for ch in range(8):
                        if ch >= 2:
                            tensor.wait_ge(vv.h, v_red[(h, qi, ch - 2)])
                        elif pit is not None:
                            tensor.wait_ge(vv.h, v_red[(pit[0], pit[1],
                                                        6 + ch)])
                        inst = tensor.matmul(
                            ps_m[ch % 2], qcol,
                            kh_sb[:, sl * T + ch * 512:sl * T + (ch + 1) * 512],
                            start=True, stop=True)
                        B.inc(inst, tv, 1)
                    if pit is not None:
                        tensor.wait_ge(sv.h, s_e[pit])
                    b0 = b0_of(qi)
                    inst = tensor.matmul(ps_b, qcol,
                                         kh_sb[:, sl * T + b0:sl * T + b0 + 384],
                                         start=True, stop=True)
                    B.inc(inst, tv, 1)
                    for c in range(3):
                        if c == 0:
                            tensor.wait_ge(vv.h, v_e0[(h, qi)])
                        if c >= 2:
                            tensor.wait_ge(sv.h, s_trev[(h, qi, c - 2)])
                        elif pit is not None:
                            tensor.wait_ge(sv.h, s_trev[(pit[0], pit[1],
                                                         c + 1)])
                        inst = tensor.transpose(
                            ps_tr[c % 2], e_sb[:, c * 128:(c + 1) * 128],
                            B.ids)
                        B.inc(inst, tv, 1)
                    for c in range(3):
                        tensor.wait_ge(sv.h, s_trev[(h, qi, c)])
                        if c == 0 and pit is not None:
                            tensor.wait_ge(vv.h, v_comb[pit])
                        bc = b0 // 128 + c
                        inst = tensor.matmul(ps_av,
                                             e0T[:, c * 128:(c + 1) * 128],
                                             vh3[:, sl, bc, :],
                                             start=(c == 0), stop=(c == 2))
                        B.inc(inst, tv, 1)

        @decos["scalar"]
        def _(scalar):
            scalar.wait_ge(tv.h, tv0 + 1)
            inst = scalar.activation(out=svrep[:, 0:512], in_=ps_rep,
                                     func=ACT.Copy)
            B.inc(inst, sv, 1)
            scalar.wait_ge(tv.h, tv0 + 2)
            inst = scalar.activation(out=svrep[:, 512:1024], in_=ps_rep,
                                     func=ACT.Copy)
            B.inc(inst, sv, 1)
            mneg = mt[:, 9:10]
            m2neg = mt[:, 11:12]
            for h in range(NH):
                for qi in range(NQ):
                    scalar.wait_ge(tv.h, t_band[(h, qi)])
                    scalar.wait_ge(vv.h, v_mneg[(h, qi)])
                    inst = scalar.activation(out=e_sb, in_=ps_b, func=ACT.Exp,
                                             scale=0.125, bias=mneg)
                    B.inc(inst, sv, 1)
                    scalar.wait_ge(vv.h, v_m2[(h, qi)])
                    scalar.activation(out=mt[:, 12:13], in_=m2neg,
                                      func=ACT.Exp)
                    scalar.drain()
                    inst = scalar.activation(out=et_sb, in_=e_sb, func=ACT.Exp,
                                             bias=m2neg)
                    B.inc(inst, sv, 1)
                    for c in range(3):
                        scalar.wait_ge(tv.h, t_tr[(h, qi, c)])
                        inst = scalar.activation(
                            out=e0T[:, c * 128:(c + 1) * 128],
                            in_=ps_tr[c % 2], func=ACT.Copy)
                        B.inc(inst, sv, 1)

        @decos["vector"]
        def _(vector):
            vector.wait_ge(ld.h, ld0 + 112)
            mtmp = mt[:, 0:8]
            mraw = mt[:, 8:9]
            mneg = mt[:, 9:10]
            m2 = mt[:, 10:11]
            m2neg = mt[:, 11:12]
            z = mt[:, 12:13]
            den = mt[:, 13:14]
            den2 = mt[:, 14:15]
            rcp = mt[:, 15:16]
            for h in range(NH):
                for qi in range(NQ):
                    for ch in range(8):
                        vector.wait_ge(tv.h, t_m[(h, qi, ch)])
                        B.red(vector, mtmp[:, ch:ch + 1], ps_m[ch % 2],
                              ALU.max)
                        B.raw_inc(vector, vv, 1)
                    B.red(vector, mraw, mtmp, ALU.max)
                    vector.tensor_single_scalar(out=mneg, in_=mraw,
                                                scalar=-0.125, op=ALU.mult)
                    B.raw_inc(vector, vv, 1)
                    vector.wait_ge(sv.h, s_e[(h, qi)])
                    mv = mvar(qi)
                    vector.tensor_tensor(out=e_sb, in0=e_sb,
                                         in1=msk[:, mv * 384:(mv + 1) * 384],
                                         op=ALU.mult)
                    B.red(vector, m2, e_sb, ALU.max)
                    vector.tensor_single_scalar(out=m2neg, in_=m2,
                                                scalar=-1.0, op=ALU.mult)
                    B.raw_inc(vector, vv, 1)
                    vector.wait_ge(sv.h, s_ze[(h, qi)])
                    vector.tensor_scalar(out=e_sb, in0=et_sb, scalar1=z,
                                         scalar2=None, op0=ALU.subtract)
                    B.red(vector, den, e_sb, ALU.add)
                    vector.scalar_tensor_tensor(out=den2, in0=z,
                                                scalar=float(T), in1=den,
                                                op0=ALU.mult, op1=ALU.add)
                    B.recip(vector, rcp, den2)
                    B.raw_inc(vector, vv, 1)
                    vector.wait_ge(tv.h, t_av[(h, qi)])
                    pit2 = (h, qi - 2)
                    if qi < 2:
                        pit2 = (h - 1, NQ - 2 + qi) if h >= 1 else None
                    if pit2 is not None:
                        vector.wait_ge(st.h, st_o[pit2])
                    ob = o_blk[:, (qi % 2) * 64:(qi % 2 + 1) * 64]
                    vector.scalar_tensor_tensor(
                        out=ob, in0=svrep[:, h * 64:(h + 1) * 64], scalar=z,
                        in1=ps_av, op0=ALU.mult, op1=ALU.add)
                    vector.drain()
                    vector.tensor_scalar(out=ob, in0=ob, scalar1=rcp,
                                         scalar2=None, op0=ALU.mult)
                    B.raw_inc(vector, vv, 1)

        @decos["gpsimd"]
        def _(gpsimd):
            for h in range(NH):
                for qi in range(NQ):
                    gpsimd.wait_ge(vv.h, v_comb[(h, qi)])
                    inst = gpsimd.dma_start(
                        out=B.dram["o_tok"][qi * 128:(qi + 1) * 128,
                                            h * 64:(h + 1) * 64],
                        in_=o_blk[:, (qi % 2) * 64:(qi % 2 + 1) * 64])
                    B.inc(inst, st, 16)
                    gpsimd.wait_ge(st.h, st.v)

        finish()
    B.phase_idx += 1


def _attn_wo(B, tl):
    nc = B.nc
    ld, st, tv, vv, sv = B.ld, B.st, B.tv, B.vv, B.sv
    inp = B.inp
    ld0, sv0, vv0, tv0, st0 = ld.v, sv.v, vv.v, tv.v, st.v
    wo_sb, og, xg, oT, outf = (tl["wo"], tl["og"], tl["xg2"], tl["oT"],
                               tl["outf"])
    NQ = 32

    t_tr, t_half = {}, {}
    tp = tv0
    for qi in range(NQ):
        for k in range(8):
            tp += 1
            t_tr[(qi, k)] = tp
        for hf in range(2):
            tp += 8
            t_half[(qi, hf)] = tp
    s_ev = {}
    sp = sv0
    for qi in range(NQ):
        for k in range(8):
            sp += 1
            s_ev[(qi, k)] = sp
    v_fin = {}
    vp = vv0
    for qi in range(NQ):
        vp += 1
        v_fin[qi] = vp
    st_c = {}
    stp = st0
    for qi in range(NQ):
        stp += 16
        st_c[qi] = stp

    def ld_qi(qi):
        return ld0 + 80 if qi < 2 else ld0 + 16 + 32 * (qi + 1)

    import contextlib
    with contextlib.ExitStack() as _stk:
        def PP(nm, shp):
            return _stk.enter_context(nc.psum_tensor(nm, shp, F32)).ap()

        ps_tr = [PP("awtr0", [128, 128]), PP("awtr1", [128, 128])]
        ps_wo = [PP("awm0", [128, 512]), PP("awm1", [128, 512])]
        block = _stk.enter_context(nc.Block())
        decos, finish = barrier_wrap(B, block, None)
        wo3 = wo_sb.rearrange("p (k o) -> p k o", k=8)

        @decos["sync"]
        def _(sync):
            sync.dma_start(out=wo3, in_=inp["wo"].rearrange(
                "(k p) o -> p k o", p=128)).then_inc(ld.h, 16)
            ld.v += 16
            for qi in range(NQ):
                if qi >= 2:
                    sync.wait_ge(vv.h, v_fin[qi - 2])
                r0 = qi * 128
                sync.dma_start(out=og[:, (qi % 2) * D:(qi % 2 + 1) * D],
                               in_=B.dram["o_tok"][r0:r0 + 128, :]
                               ).then_inc(ld.h, 16)
                sync.dma_start(out=xg[:, (qi % 2) * D:(qi % 2 + 1) * D],
                               in_=B.dram["xres"][r0:r0 + 128, :]
                               ).then_inc(ld.h, 16)
                ld.v += 32

        @decos["tensor"]
        def _(tensor):
            for qi in range(NQ):
                tensor.wait_ge(ld.h, ld_qi(qi))
                for k in range(8):
                    if k >= 2:
                        tensor.wait_ge(sv.h, s_ev[(qi, k - 2)])
                    elif qi >= 1:
                        tensor.wait_ge(sv.h, s_ev[(qi - 1, 6 + k)])
                    inst = tensor.transpose(
                        ps_tr[k % 2],
                        og[:, (qi % 2) * D + k * 128:(qi % 2) * D +
                           (k + 1) * 128], B.ids)
                    B.inc(inst, tv, 1)
                for hf in range(2):
                    if hf == 0 and qi >= 1:
                        tensor.wait_ge(vv.h, v_fin[qi - 1])
                    for k in range(8):
                        if hf == 0:
                            tensor.wait_ge(sv.h, s_ev[(qi, k)])
                        inst = tensor.matmul(
                            ps_wo[hf], oT[:, k * 128:(k + 1) * 128],
                            wo3[:, k, hf * 512:(hf + 1) * 512],
                            start=(k == 0), stop=(k == 7))
                        B.inc(inst, tv, 1)

        @decos["scalar"]
        def _(scalar):
            for qi in range(NQ):
                for k in range(8):
                    scalar.wait_ge(tv.h, t_tr[(qi, k)])
                    inst = scalar.activation(out=oT[:, k * 128:(k + 1) * 128],
                                             in_=ps_tr[k % 2], func=ACT.Copy)
                    B.inc(inst, sv, 1)

        @decos["vector"]
        def _(vector):
            for qi in range(NQ):
                vector.wait_ge(ld.h, ld_qi(qi))
                if qi >= 2:
                    vector.wait_ge(st.h, st_c[qi - 2])
                xs_ = xg[:, (qi % 2) * D:(qi % 2 + 1) * D]
                of = outf[:, (qi % 2) * D:(qi % 2 + 1) * D]
                vector.wait_ge(tv.h, t_half[(qi, 0)])
                vector.tensor_tensor(out=of[:, 0:512], in0=ps_wo[0],
                                     in1=xs_[:, 0:512], op=ALU.add)
                vector.wait_ge(tv.h, t_half[(qi, 1)])
                vector.tensor_tensor(out=of[:, 512:1024], in0=ps_wo[1],
                                     in1=xs_[:, 512:1024], op=ALU.add)
                B.raw_inc(vector, vv, 1)

        @decos["gpsimd"]
        def _(gpsimd):
            for qi in range(NQ):
                gpsimd.wait_ge(vv.h, v_fin[qi])
                r0 = qi * 128
                inst = gpsimd.dma_start(
                    out=B.dram["xres"][r0:r0 + 128, :],
                    in_=outf[:, (qi % 2) * D:(qi % 2 + 1) * D])
                B.inc(inst, st, 16)
                gpsimd.wait_ge(st.h, st.v)

        finish()
    B.phase_idx += 1


def phase_attn(B):
    nc = B.nc
    import contextlib
    svrow = nc.alloc_sbuf_tensor("at_svrow", [1, D], F32).ap()
    with contextlib.ExitStack() as _stk:
        def A(nm, shp, pt=128):
            return _stk.enter_context(
                nc.sbuf_tensor(nm, [pt] + shp, F32)).ap()

        tl = {
            "wq": A("at_wq", [8 * D]),
            "wk": A("at_wk", [8 * D]),
            "wv": A("at_wv", [8 * D]),
            "lnT": A("at_lnT", [8 * 512]),
            "qs": A("at_qs", [8 * 512]),
            "ks": A("at_ks", [8 * 512]),
            "vs": A("at_vs", [4 * D]),
            "xg": A("at_x", [2 * D]),
            "lnx": A("at_ln", [2 * D]),
            "qtmp": A("at_sq", [D]),
            "lnsc": A("at_st", [8]),
            "svrow": svrow,
        }
        _attn_qkv(B, tl)
    with contextlib.ExitStack() as _stk:
        def A(nm, shp, pt=128):
            return _stk.enter_context(
                nc.sbuf_tensor(nm, [pt] + shp, F32)).ap()

        tl2 = {
            "qh": A("at_qh", [2 * T], pt=64),
            "kh": A("at_kh", [2 * T], pt=64),
            "vh": A("at_vh", [2 * 32 * 64]),
            "svrow": svrow,
            "svrep": A("at_svrep", [D]),
            "msk": A("at_msk", [3 * 384]),
            "e_sb": A("at_e", [384]),
            "et_sb": A("at_et", [384]),
            "e0T": A("at_e0T", [384]),
            "o_blk": A("at_ob", [2 * 64]),
            "mt": A("at_mt", [16]),
        }
        _attn_core(B, tl2)
    with contextlib.ExitStack() as _stk:
        def A(nm, shp, pt=128):
            return _stk.enter_context(
                nc.sbuf_tensor(nm, [pt] + shp, F32)).ap()

        tl3 = {
            "wo": A("at_wo", [8 * D]),
            "og": A("at_og", [2 * D]),
            "xg2": A("at_x2", [2 * D]),
            "oT": A("at_oT", [D]),
            "outf": A("at_of", [2 * D]),
        }
        _attn_wo(B, tl3)


def phase_copy_out(B):
    nc = B.nc
    ld, st, vv = B.ld, B.st, B.vv
    ld0, st0 = B.ld.v, B.st.v
    NT = T // 128
    with (
        nc.sbuf_tensor("pc_t", [128, 2 * D], F32) as tile,
        nc.Block() as block,
    ):
        tile = tile.ap()
        decos, finish = barrier_wrap(B, block, None)

        @decos["sync"]
        def _(sync):
            for i in range(NT):
                if i >= 2:
                    sync.wait_ge(st.h, st0 + 16 * (i - 1))
                inst = sync.dma_start(out=tile[:, (i % 2) * D:(i % 2 + 1) * D],
                                      in_=B.dram["xres"][i * 128:(i + 1) * 128, :])
                B.inc(inst, ld, 16)
                sync.wait_ge(ld.h, ld.v)

        @decos["gpsimd"]
        def _(gpsimd):
            for i in range(NT):
                gpsimd.wait_ge(ld.h, ld0 + 16 * (i + 1))
                inst = gpsimd.dma_start(out=B.dram["out_x"][i * 128:(i + 1) * 128, :],
                                        in_=tile[:, (i % 2) * D:(i % 2 + 1) * D])
                B.inc(inst, st, 16)
                gpsimd.wait_ge(st.h, st.v)

        finish()
    B.phase_idx += 1


def host_layout(inputs):
    L = {}
    L["w_in_t"] = np.ascontiguousarray(inputs["L0_attn_W_in"].T)
    L["conv_w_l"] = np.ascontiguousarray(
        inputs["L0_attn_conv_w"][:, 0, :].reshape(NS, 128, 4).transpose(1, 0, 2))
    L["conv_b_l"] = np.ascontiguousarray(inputs["L0_attn_conv_b"].reshape(NS, 128).T)
    L["w_x_t"] = np.ascontiguousarray(inputs["L0_attn_W_x"].T)
    L["wdt_l"] = np.ascontiguousarray(inputs["L0_attn_W_dt"][:, 0].reshape(NS, 128).T)
    L["bdt_l"] = np.ascontiguousarray(inputs["L0_attn_b_dt"].reshape(NS, 128).T)
    L["alog_c"] = np.ascontiguousarray(np.tile(inputs["L0_attn_A_log"], 8)[:, None])
    L["d_col"] = np.ascontiguousarray(inputs["L0_attn_D_param"].reshape(NS, 128).T)
    L["lng_col"] = np.ascontiguousarray(inputs["L0_attn_ln_g"].reshape(NS, 128).T)
    L["lnb_col"] = np.ascontiguousarray(inputs["L0_attn_ln_b"].reshape(NS, 128).T)
    L["w_out_t"] = np.ascontiguousarray(inputs["L0_attn_W_out"].T)
    L["ffn_w"] = np.ascontiguousarray(inputs["L0_ffn_weight"])
    L["ffn_b_col"] = np.ascontiguousarray(inputs["L0_ffn_bias"].reshape(8, 128).T)
    L["wq"] = np.ascontiguousarray(inputs["L1_attn_Wq"])
    L["wk"] = np.ascontiguousarray(inputs["L1_attn_Wk"])
    L["wv"] = np.ascontiguousarray(inputs["L1_attn_Wv"])
    L["wo"] = np.ascontiguousarray(inputs["L1_attn_Wo"])
    L["rw_t"] = np.ascontiguousarray(inputs["L1_ffn_router_w"].T)
    L["cb_rep"] = np.ascontiguousarray(
        np.tile(inputs["L1_ffn_correction_bias"][None, :], (128, 1)))
    L["gate_w"] = np.ascontiguousarray(inputs["L1_ffn_gate_w"])
    L["up_w"] = np.ascontiguousarray(inputs["L1_ffn_up_w"])
    L["down_w"] = np.ascontiguousarray(inputs["L1_ffn_down_w"])
    L["sgw_t"] = np.ascontiguousarray(inputs["L1_ffn_shared_gate_w"].T)
    L["suw_t"] = np.ascontiguousarray(inputs["L1_ffn_shared_up_w"].T)
    L["sdw_t"] = np.ascontiguousarray(inputs["L1_ffn_shared_down_w"].T)
    L["ident"] = np.eye(128, dtype=np.float32)
    L["ones_row"] = np.ones((1, 128), dtype=np.float32)
    L["ones_col"] = np.ones((128, 1), dtype=np.float32)
    first = np.zeros((128, 384), np.float32)
    for p in range(128):
        first[p, 0:p + 128] = 1.0
    mid = np.zeros((128, 384), np.float32)
    for p in range(128):
        mid[p, p:p + 256] = 1.0
    last = np.zeros((128, 384), np.float32)
    for p in range(128):
        last[p, p + 128:384] = 1.0
    L["win_mask"] = np.ascontiguousarray(np.stack([first, mid, last]))
    return L


_N_BUILD = [6]
_SKIP_QUANT = [False]
_SKIP_LN = [False]


def kernel(**inputs):
    L = host_layout(inputs)
    x = np.ascontiguousarray(inputs["L0_attn_x"], dtype=np.float32)
    nc, _ = build_program(_N_BUILD[0])
    in_maps = []
    for b in range(8):
        m = dict(L)
        m["x_in"] = np.ascontiguousarray(x[b])
        in_maps.append(m)
    res = run_bass_kernel_spmd(nc, in_maps, core_ids=list(range(8)))
    return np.stack([r["out_x"] for r in res.results]).astype(np.float32)


# version 03
# speedup vs baseline: 1.5514x; 1.0430x over previous
import numpy as np

import concourse.bass as bass
import concourse.mybir as mybir
from concourse.bass_utils import run_bass_kernel_spmd

F32 = mybir.dt.float32
AX = mybir.AxisListType
ALU = mybir.AluOpType
ACT = mybir.ActivationFunctionType

_DEBUG_OUT = [False]
T = 4096
D = 1024
DI = 2048
NS = 16
NG = 8
GT = T // NG
NSUB = GT // 128
EPS = 1e-5
TS_SCAN = 32
NCHUNK = T // TS_SCAN


class Sem:
    def __init__(self, h):
        self.h = h
        self.v = 0


class Build:

    def __init__(self, nc, n_sublayers=6):
        self.nc = nc
        self.n_sublayers = n_sublayers
        self.phase_idx = 0

    def inc(self, inst, sem, n=1):
        inst.then_inc(sem.h, n)
        sem.v += n

    def raw_inc(self, eng, sem, n=1):
        eng.drain()
        eng.sem_inc(sem.h, n)
        sem.v += n

    def red(self, vector, out, in_, op, absval=False):
        vector.drain()
        vector.tensor_reduce(out=out, in_=in_, axis=AX.X, op=op,
                             apply_absolute_value=absval)
        vector.drain()

    def recip(self, vector, out, in_):
        vector.drain()
        vector.reciprocal(out=out, in_=in_)
        vector.drain()

    def ln_tile(self, vector, scalar_ops, x, out, sq, sc):
        F = x.shape[1]
        s_sum = sc[:, 0:1]
        s_ssq = sc[:, 1:2]
        s_mu = sc[:, 2:3]
        s_var = sc[:, 3:4]
        s_inv = sc[:, 4:5]
        s_nms = sc[:, 5:6]
        self.red(vector, s_sum, x, ALU.add)
        self.red(vector, s_ssq, sq, ALU.add)
        vector.tensor_single_scalar(out=s_mu, in_=s_sum, scalar=1.0 / F, op=ALU.mult)
        vector.tensor_single_scalar(out=s_var, in_=s_ssq, scalar=1.0 / F, op=ALU.mult)
        vector.drain()
        vector.tensor_tensor(out=s_nms, in0=s_mu, in1=s_mu, op=ALU.mult)
        vector.drain()
        vector.tensor_tensor(out=s_var, in0=s_var, in1=s_nms, op=ALU.subtract)
        vector.drain()
        return s_var, s_inv, s_mu, s_nms

    def ln_apply(self, vector, x, out, s_mu, s_inv, s_nms):
        vector.tensor_tensor(out=s_nms, in0=s_mu, in1=s_inv, op=ALU.mult)
        vector.drain()
        vector.tensor_single_scalar(out=s_nms, in_=s_nms, scalar=-1.0, op=ALU.mult)
        vector.drain()
        vector.tensor_scalar(out=out, in0=x, scalar1=s_inv, scalar2=s_nms,
                             op0=ALU.mult, op1=ALU.add)

    def quant_pre(self, vector, src, v, qsc, F):
        nb = F // 32
        amax = qsc[:, 0:nb]
        scale = qsc[:, nb:2 * nb]
        iscl = qsc[:, 2 * nb:3 * nb]
        src3 = src.rearrange("p (b c) -> p b c", c=32)
        self.red(vector, amax, src3, ALU.max, absval=True)
        vector.tensor_scalar(out=scale, in0=amax, scalar1=1.0 / 7.5,
                             scalar2=1e-30, op0=ALU.mult, op1=ALU.max)
        self.recip(vector, iscl, scale)
        isb = iscl.unsqueeze(2).broadcast_to([128, nb, 32])
        vector.tensor_tensor(out=v.rearrange("p (b c) -> p b c", c=32),
                             in0=src3, in1=isb, op=ALU.mult)

    def quant_post(self, vector, v, sq, c1, qsc, F, dst):
        nb = F // 32
        scale = qsc[:, nb:2 * nb]
        vector.tensor_single_scalar(out=c1, in_=sq, scalar=4.0, op=ALU.is_ge)
        vector.tensor_single_scalar(out=sq, in_=sq, scalar=16.0, op=ALU.is_ge)
        vector.scalar_tensor_tensor(out=sq, in0=sq, scalar=0.5, in1=c1,
                                    op0=ALU.mult, op1=ALU.add)
        vector.tensor_scalar(out=c1, in0=sq, scalar1=-4.0, scalar2=8.0,
                             op0=ALU.mult, op1=ALU.add)
        vector.tensor_tensor(out=v, in0=v, in1=c1, op=ALU.mult)
        vector.tensor_single_scalar(out=c1, in_=sq, scalar=1.5, op=ALU.is_ge)
        vector.tensor_single_scalar(out=sq, in_=sq, scalar=1.0, op=ALU.is_ge)
        vector.scalar_tensor_tensor(out=sq, in0=c1, scalar=2.0, in1=sq,
                                    op0=ALU.mult, op1=ALU.add)
        vector.tensor_scalar(out=c1, in0=sq, scalar1=0.125, scalar2=0.125,
                             op0=ALU.mult, op1=ALU.add)
        vector.tensor_single_scalar(out=sq, in_=v, scalar=16.5, op=ALU.add)
        for dlev in (32.0, 16.0, 8.0, 4.0, 2.0, 1.0):
            vector.tensor_scalar(out=dst, in0=sq, scalar1=dlev, scalar2=dlev,
                                 op0=ALU.is_gt, op1=ALU.mult)
            vector.tensor_tensor(out=sq, in0=sq, in1=dst, op=ALU.subtract)
        vector.scalar_tensor_tensor(out=sq, in0=sq, scalar=-1.0, in1=v,
                                    op0=ALU.mult, op1=ALU.add)
        vector.tensor_single_scalar(out=sq, in_=sq, scalar=0.5, op=ALU.add)
        vector.tensor_tensor(out=sq, in0=sq, in1=c1, op=ALU.mult)
        scb = scale.unsqueeze(2).broadcast_to([128, nb, 32])
        vector.tensor_tensor(out=dst.rearrange("p (b c) -> p b c", c=32),
                             in0=sq.rearrange("p (b c) -> p b c", c=32),
                             in1=scb, op=ALU.mult)


def build_program(n_sublayers=6):
    nc = bass.Bass("TRN2", target_bir_lowering=False, debug=False,
                   detect_race_conditions=False)
    B = Build(nc, n_sublayers)

    inp = {}

    def I(name, shape):
        inp[name] = nc.dram_tensor(name, list(shape), F32, kind="ExternalInput").ap()
        return inp[name]

    x_in = I("x_in", (T, D))
    w_in_t = I("w_in_t", (D, 2 * DI))
    conv_w_l = I("conv_w_l", (128, NS, 4))
    conv_b_l = I("conv_b_l", (128, NS))
    w_x_t = I("w_x_t", (DI, 33))
    wdt_l = I("wdt_l", (128, NS))
    bdt_l = I("bdt_l", (128, NS))
    alog_c = I("alog_c", (128, 1))
    d_col = I("d_col", (128, NS))
    lng_col = I("lng_col", (128, NS))
    lnb_col = I("lnb_col", (128, NS))
    w_out_t = I("w_out_t", (DI, D))
    ffn_w = I("ffn_w", (D, D))
    ffn_b_col = I("ffn_b_col", (128, 8))
    wq_i = I("wq", (D, D))
    wk_i = I("wk", (D, D))
    wv_i = I("wv", (D, D))
    wo_i = I("wo", (D, D))
    rw_t = I("rw_t", (D, 16))
    cb_rep = I("cb_rep", (128, 16))
    gate_w = I("gate_w", (16, D, 256))
    up_w = I("up_w", (16, D, 256))
    down_w = I("down_w", (16, 256, D))
    sgw_t = I("sgw_t", (D, DI))
    suw_t = I("suw_t", (D, DI))
    sdw_t = I("sdw_t", (DI, D))
    ident = I("ident", (128, 128))
    ones_row = I("ones_row", (1, 128))
    ones_col = I("ones_col", (128, 1))
    win_mask = I("win_mask", (3, 128, 384))

    out_x = nc.dram_tensor("out_x", [T, D], F32, kind="ExternalOutput").ap()

    _dk = "ExternalOutput" if _DEBUG_OUT[0] else "Internal"
    xres = nc.dram_tensor("xres", [T, D], F32, kind=_dk).ap()
    u_fm = nc.dram_tensor("u_fm", [DI, T], F32, kind="Internal").ap()
    dt_fm = nc.dram_tensor("dt_fm", [DI, T], F32, kind="Internal").ap()
    du_fm = nc.dram_tensor("du_fm", [DI, T], F32, kind="Internal").ap()
    sz_fm = nc.dram_tensor("sz_fm", [DI, T], F32, kind="Internal").ap()
    y_fm = nc.dram_tensor("y_fm", [DI, T], F32, kind="Internal").ap()
    bc_fm = nc.dram_tensor("bc_fm", [32, T], F32, kind="Internal").ap()
    wdq_t = nc.dram_tensor("wdq_t", [D, D], F32, kind=_dk).ap()
    dbg = nc.dram_tensor("dbg", [128, 4 * D], F32, kind=_dk).ap()

    q_fm = nc.dram_tensor("q_fm", [D, T], F32, kind="Internal").ap()
    k_fm = nc.dram_tensor("k_fm", [D, T], F32, kind="Internal").ap()
    v_tok = nc.dram_tensor("v_tok", [T, D], F32, kind="Internal").ap()
    o_tok = nc.dram_tensor("o_tok", [T, D], F32, kind="Internal").ap()
    comb_fm = nc.dram_tensor("comb_fm", [16, T], F32, kind="Internal").ap()

    c_eps = nc.alloc_sbuf_tensor("c_eps", [128, 1], F32).ap()
    nc.const_aps.register(F32, EPS, c_eps)
    ids_g = nc.alloc_sbuf_tensor("ids_g", [128, 128], F32).ap()
    onesr_g = nc.alloc_sbuf_tensor("onesr_g", [1, 128], F32).ap()
    onesc_g = nc.alloc_sbuf_tensor("onesc_g", [128, 1], F32).ap()
    acol_g = nc.alloc_sbuf_tensor("acol_g", [128, 1], F32).ap()
    arep_g = nc.alloc_sbuf_tensor("arep_g", [128, 16], F32).ap()
    B.dram = dict(dbg=dbg, x_in=x_in, xres=xres, out_x=out_x, u_fm=u_fm, dt_fm=dt_fm,
                  du_fm=du_fm, sz_fm=sz_fm, y_fm=y_fm, bc_fm=bc_fm, wdq_t=wdq_t,
                  q_fm=q_fm, k_fm=k_fm, v_tok=v_tok, o_tok=o_tok, comb_fm=comb_fm)
    B.inp = inp
    B.ids = ids_g
    B.c_eps = c_eps
    B.onesr = onesr_g
    B.onesc = onesc_g
    B.acol = acol_g
    B.arep = arep_g

    with (
        nc.semaphore("bar") as h_bar,
        nc.semaphore("s_ld") as h_ld,
        nc.semaphore("s_st") as h_st,
        nc.semaphore("s_t") as h_t,
        nc.semaphore("s_v") as h_v,
        nc.semaphore("s_s") as h_s,
        nc.semaphore("s_g") as h_g,
    ):
        B.bar = Sem(h_bar)
        B.ld = Sem(h_ld)
        B.st = Sem(h_st)
        B.tv = Sem(h_t)
        B.vv = Sem(h_v)
        B.sv = Sem(h_s)
        B.gv = Sem(h_g)

        phase_a(B)
        phase_ffn(B, src=x_in, dst=xres)
        if n_sublayers >= 1.3:
            phase_mamba_in(B)
        if n_sublayers >= 1.6:
            phase_scan(B)
        if n_sublayers >= 2:
            phase_mamba_out(B)
        if n_sublayers >= 3:
            phase_ffn(B, src=xres, dst=xres)
        if n_sublayers >= 4:
            phase_moe(B, src=xres, dst=xres)
        if n_sublayers >= 5:
            phase_attn(B)
        if n_sublayers >= 6:
            phase_moe(B, src=xres, dst=out_x)
            copy_done = True
        else:
            phase_copy_out(B)

    return nc, inp


def barrier_wrap(B, block, used):
    k = B.phase_idx
    bar = B.bar
    emitted = set()

    def mk(name, attr):
        def deco(f):
            def wrapped(eng):
                eng.wait_ge(bar.h, 5 * k)
                f(eng)
                eng.drain()
                eng.sem_inc(bar.h, 1)
            attr(wrapped)
            emitted.add(name)
        return deco

    decos = {n: mk(n, getattr(block, n))
             for n in ("sync", "tensor", "vector", "scalar", "gpsimd")}

    def finish():
        for n in ("sync", "tensor", "vector", "scalar", "gpsimd"):
            if n not in emitted:
                def idle(eng):
                    pass
                decos[n](idle)
        B.bar.v += 5

    return decos, finish


def phase_a(B):
    nc = B.nc
    ld, st, tv, vv, sv = B.ld, B.st, B.tv, B.vv, B.sv
    inp = B.inp
    ld0, sv0, vv0, tv0, st0 = ld.v, sv.v, vv.v, tv.v, st.v
    with (
        nc.sbuf_tensor("pa_w", [128, D], F32) as wtile,
        nc.sbuf_tensor("pa_wq", [128, D], F32) as wq_s,
        nc.sbuf_tensor("pa_sc", [128, 96], F32) as qsc,
        nc.sbuf_tensor("pa_tmp", [128, 3 * D], F32) as qtmp,
        nc.sbuf_tensor("pa_al", [128, 1], F32) as al_s,
        nc.sbuf_tensor("pa_ar", [1, 16], F32) as arow,
        nc.sbuf_tensor("pa_tr", [128, 2 * 128], F32) as tr_s,
        nc.psum_tensor("pa_ps", [128, 128], F32) as ps_tr,
        nc.psum_tensor("pa_ps2", [128, 16], F32) as ps_rep,
        nc.Block() as block,
    ):
        wtile, wq_s, qsc, qtmp, al_s, arow, tr_s, ps_tr, ps_rep = [
            t.ap() for t in (wtile, wq_s, qsc, qtmp, al_s, arow, tr_s,
                             ps_tr, ps_rep)]
        decos, finish = barrier_wrap(B, block, None)

        @decos["sync"]
        def _(sync):
            sync.dma_start(out=B.ids, in_=inp["ident"]).then_inc(ld.h, 16)
            sync.dma_start(out=B.onesr, in_=inp["ones_row"]).then_inc(ld.h, 16)
            sync.dma_start(out=B.onesc, in_=inp["ones_col"]).then_inc(ld.h, 16)
            sync.dma_start(out=al_s, in_=inp["alog_c"]).then_inc(ld.h, 16)
            ld.v += 64
            for r in range(8):
                if r >= 1:
                    sync.wait_ge(vv.h, vv0 + 1 + 2 * r)
                sync.dma_start(out=wtile, in_=inp["ffn_w"][r * 128:(r + 1) * 128, :]
                               ).then_inc(ld.h, 16)
                ld.v += 16

        def sq_val_v(r):
            return 2 if r == 0 else 3 + 9 * (r - 1)

        @decos["vector"]
        def _(vector):
            vector.wait_ge(ld.h, ld0 + 80)
            vector.memset(B.c_eps, EPS)
            vector.tensor_single_scalar(out=B.acol, in_=al_s, scalar=5.0, op=ALU.min)
            B.raw_inc(vector, vv, 1)
            for r in range(8):
                vector.wait_ge(ld.h, ld0 + 80 + 16 * r)
                B.quant_pre(vector, wtile, qtmp[:, 0:D], qsc, D)
                B.raw_inc(vector, vv, 1)
                vector.wait_ge(sv.h, sv0 + sq_val_v(r))
                if r >= 1:
                    vector.wait_ge(tv.h, tv0 + 8 * r)
                B.quant_post(vector, qtmp[:, 0:D], qtmp[:, D:2 * D],
                             qtmp[:, 2 * D:3 * D], qsc, D, wq_s)
                B.raw_inc(vector, vv, 1)

        def sq_val(r):
            return 2 if r == 0 else 3 + 9 * (r - 1)

        def evac_val(j):
            rr, kk = j // 8, j % 8
            return (4 + 9 * rr + kk) if rr <= 6 else (66 + kk)

        def emit_evacs(scalar, rr):
            for kk in range(8):
                j = rr * 8 + kk
                scalar.wait_ge(tv.h, tv0 + j + 1)
                if j >= 2:
                    scalar.wait_ge(st.h, st0 + 16 * (j - 1))
                slot = tr_s[:, (j % 2) * 128:(j % 2 + 1) * 128]
                inst = scalar.activation(out=slot, in_=ps_tr, func=ACT.Copy)
                B.inc(inst, sv, 1)

        @decos["scalar"]
        def _(scalar):
            scalar.wait_ge(vv.h, vv0 + 1)
            scalar.activation(out=B.acol, in_=B.acol, func=ACT.Exp)
            scalar.mul(out=B.acol, in_=B.acol, mul=-1.0)
            B.raw_inc(scalar, sv, 1)
            for r in range(8):
                scalar.wait_ge(vv.h, vv0 + 1 + 2 * r + 1)
                inst = scalar.activation(out=qtmp[:, D:2 * D], in_=qtmp[:, 0:D],
                                         func=ACT.Square)
                B.inc(inst, sv, 1)
                if r >= 1:
                    emit_evacs(scalar, r - 1)
            emit_evacs(scalar, 7)
            scalar.wait_ge(tv.h, tv0 + 64 + 1)
            inst = scalar.activation(out=arow, in_=ps_tr[0:1, 0:16], func=ACT.Copy)
            B.inc(inst, sv, 1)
            scalar.wait_ge(tv.h, tv0 + 64 + 2)
            inst = scalar.activation(out=B.arep, in_=ps_rep, func=ACT.Copy)
            B.inc(inst, sv, 1)

        @decos["tensor"]
        def _(tensor):
            tensor.wait_ge(ld.h, ld0 + 80)
            for r in range(8):
                tensor.wait_ge(vv.h, vv0 + 1 + 2 * r + 2)
                for k in range(8):
                    idx = r * 8 + k
                    if idx >= 1:
                        j = idx - 1
                        ev = (4 + 9 * (j // 8) + j % 8) if j // 8 <= 6 else (66 + j % 8)
                        tensor.wait_ge(sv.h, sv0 + ev)
                    inst = tensor.transpose(ps_tr, wq_s[:, k * 128:(k + 1) * 128],
                                            B.ids)
                    B.inc(inst, tv, 1)
            tensor.wait_ge(sv.h, sv0 + 1)
            tensor.wait_ge(sv.h, sv0 + 73)
            inst = tensor.transpose(ps_tr[0:1, 0:16], B.acol[0:16, :],
                                    B.ids[0:16, 0:16])
            B.inc(inst, tv, 1)
            tensor.wait_ge(sv.h, sv0 + 74)
            inst = tensor.matmul(ps_rep, B.onesr, arow, start=True, stop=True)
            B.inc(inst, tv, 1)

        @decos["gpsimd"]
        def _(gpsimd):
            for idx in range(64):
                r, k = idx // 8, idx % 8
                ev = (4 + 9 * (idx // 8) + idx % 8) if idx // 8 <= 6 else (66 + idx % 8)
                gpsimd.wait_ge(sv.h, sv0 + ev)
                slot = tr_s[:, (idx % 2) * 128:(idx % 2 + 1) * 128]
                inst = gpsimd.dma_start(
                    out=B.dram["wdq_t"][k * 128:(k + 1) * 128,
                                        r * 128:(r + 1) * 128], in_=slot)
                B.inc(inst, st, 16)
                gpsimd.wait_ge(st.h, st.v)

        finish()
    B.phase_idx += 1


def phase_ffn(B, src, dst):
    nc = B.nc
    ld, st, tv, vv, sv = B.ld, B.st, B.tv, B.vv, B.sv
    inp = B.inp
    ld0, sv0, vv0, tv0, st0 = ld.v, sv.v, vv.v, tv.v, st.v
    VPG = 3 * NSUB + 8 + NSUB
    SPG = 3 * NSUB + 32 + 32
    TPG = 32 + 64 + 32
    sfx = f"_{B.phase_idx}"
    with (
        nc.sbuf_tensor("pf_x" + sfx, [128, NSUB * D], F32) as xg,
        nc.sbuf_tensor("pf_ln" + sfx, [128, D], F32) as lnx,
        nc.sbuf_tensor("pf_xq" + sfx, [128, NSUB * D], F32) as xq,
        nc.sbuf_tensor("pf_xqT" + sfx, [128, 8 * GT], F32) as xqT,
        nc.sbuf_tensor("pf_w" + sfx, [128, 8 * D], F32) as wdq,
        nc.sbuf_tensor("pf_yf" + sfx, [128, 8 * GT], F32) as yf,
        nc.sbuf_tensor("pf_yt" + sfx, [128, NSUB * D], F32) as yt,
        nc.sbuf_tensor("pf_sc" + sfx, [128, 96], F32) as qsc,
        nc.sbuf_tensor("pf_tmp" + sfx, [128, 3 * D], F32) as qtmp,
        nc.sbuf_tensor("pf_st" + sfx, [128, 8], F32) as lnsc,
        nc.sbuf_tensor("pf_b" + sfx, [128, 8], F32) as ffnb,
        nc.psum_tensor("pf_ptr0" + sfx, [128, 128], F32) as ps_tr0,
        nc.psum_tensor("pf_ptr1" + sfx, [128, 128], F32) as ps_tr1,
        nc.psum_tensor("pf_mm0" + sfx, [128, GT], F32) as ps_mm0,
        nc.psum_tensor("pf_mm1" + sfx, [128, GT], F32) as ps_mm1,
        nc.Block() as block,
    ):
        (xg, lnx, xq, xqT, wdq, yf, yt, qsc, qtmp, lnsc, ffnb, ps_tr0,
         ps_tr1, ps_mm0, ps_mm1) = [
            t.ap() for t in (xg, lnx, xq, xqT, wdq, yf, yt, qsc, qtmp, lnsc,
                             ffnb, ps_tr0, ps_tr1, ps_mm0, ps_mm1)]
        ps_tr_s = [ps_tr0, ps_tr1]
        ps_mm_s = [ps_mm0, ps_mm1]
        decos, finish = barrier_wrap(B, block, None)

        @decos["sync"]
        def _(sync):
            sync.dma_start(out=ffnb, in_=inp["ffn_b_col"]).then_inc(ld.h, 16)
            for kk in range(8):
                sync.dma_start(out=wdq[:, kk * D:(kk + 1) * D],
                               in_=B.dram["wdq_t"][kk * 128:(kk + 1) * 128, :]
                               ).then_inc(ld.h, 16)
            ld.v += 16 * 9
            for g in range(NG):
                if g >= 1:
                    sync.wait_ge(vv.h, vv0 + g * VPG)
                for sb in range(NSUB):
                    r0 = g * GT + sb * 128
                    sync.dma_start(out=xg[:, sb * D:(sb + 1) * D],
                                   in_=src[r0:r0 + 128, :]).then_inc(ld.h, 16)
                    ld.v += 16

        @decos["vector"]
        def _(vector):
            for g in range(NG):
                for sb in range(NSUB):
                    vector.wait_ge(ld.h, ld0 + 144 + 64 * (g + 1))
                    xs_ = xg[:, sb * D:(sb + 1) * D]
                    vector.wait_ge(sv.h, sv0 + g * SPG + 3 * sb + 1)
                    if _SKIP_LN[0]:
                        vector.engine_nop()
                        B.raw_inc(vector, vv, 1)
                        vector.wait_ge(sv.h, sv0 + g * SPG + 3 * sb + 2)
                        vector.tensor_copy(out=lnx, in_=xs_)
                    else:
                        B.ln_tile(vector, None, xs_, lnx, qtmp[:, D:2 * D], lnsc)
                        B.raw_inc(vector, vv, 1)
                        vector.wait_ge(sv.h, sv0 + g * SPG + 3 * sb + 2)
                        B.ln_apply(vector, xs_, lnx, lnsc[:, 2:3], lnsc[:, 4:5],
                                   lnsc[:, 5:6])
                    if _SKIP_QUANT[0]:
                        vector.tensor_copy(out=qtmp[:, 0:D], in_=lnx)
                        B.raw_inc(vector, vv, 1)
                        vector.wait_ge(sv.h, sv0 + g * SPG + 3 * sb + 3)
                        vector.tensor_copy(out=xq[:, sb * D:(sb + 1) * D], in_=lnx)
                        B.raw_inc(vector, vv, 1)
                    else:
                        B.quant_pre(vector, lnx, qtmp[:, 0:D], qsc, D)
                        B.raw_inc(vector, vv, 1)
                        vector.wait_ge(sv.h, sv0 + g * SPG + 3 * sb + 3)
                        B.quant_post(vector, qtmp[:, 0:D], qtmp[:, D:2 * D],
                                     qtmp[:, 2 * D:3 * D], qsc, D,
                                     xq[:, sb * D:(sb + 1) * D])
                        B.raw_inc(vector, vv, 1)
                for oc in range(8):
                    vector.wait_ge(tv.h, tv0 + g * TPG + 32 + (oc + 1) * 8)
                    pslot = ps_mm_s[oc % 2]
                    vector.tensor_scalar(out=yf[:, oc * GT:(oc + 1) * GT],
                                         in0=pslot, scalar1=ffnb[:, oc:oc + 1],
                                         scalar2=0.5, op0=ALU.add, op1=ALU.mult)
                    B.raw_inc(vector, vv, 1)
                for sb in range(NSUB):
                    vector.wait_ge(sv.h,
                                   sv0 + g * SPG + 3 * NSUB + 32 + (sb + 1) * 8)
                    vector.tensor_tensor(out=yt[:, sb * D:(sb + 1) * D],
                                         in0=yt[:, sb * D:(sb + 1) * D],
                                         in1=xg[:, sb * D:(sb + 1) * D], op=ALU.add)
                    B.raw_inc(vector, vv, 1)

        @decos["scalar"]
        def _(scalar):
            for g in range(NG):
                for sb in range(NSUB):
                    scalar.wait_ge(ld.h, ld0 + 144 + 64 * (g + 1))
                    if g * NSUB + sb >= 1:
                        scalar.wait_ge(vv.h, vv0 + g * VPG + 3 * sb)
                    inst = scalar.activation(out=qtmp[:, D:2 * D],
                                             in_=xg[:, sb * D:(sb + 1) * D],
                                             func=ACT.Square)
                    B.inc(inst, sv, 1)
                    scalar.wait_ge(vv.h, vv0 + g * VPG + 3 * sb + 1)
                    scalar.activation(out=lnsc[:, 4:5], in_=lnsc[:, 3:4],
                                      func=ACT.Ln, bias=EPS)
                    scalar.drain()
                    inst = scalar.activation(out=lnsc[:, 4:5], in_=lnsc[:, 4:5],
                                             func=ACT.Exp, scale=-0.5)
                    scalar.drain()
                    scalar.sem_inc(sv.h, 1)
                    sv.v += 1
                    scalar.wait_ge(vv.h, vv0 + g * VPG + 3 * sb + 2)
                    inst = scalar.activation(out=qtmp[:, D:2 * D],
                                             in_=qtmp[:, 0:D], func=ACT.Square)
                    B.inc(inst, sv, 1)
                for idx in range(32):
                    sb, k = idx // 8, idx % 8
                    scalar.wait_ge(tv.h, tv0 + g * TPG + idx + 1)
                    slot = ps_tr_s[idx % 2]
                    inst = scalar.activation(
                        out=xqT[:, k * GT + sb * 128:k * GT + (sb + 1) * 128],
                        in_=slot, func=ACT.Copy)
                    B.inc(inst, sv, 1)
                for idx in range(32):
                    sb, oc = idx // 8, idx % 8
                    scalar.wait_ge(tv.h, tv0 + g * TPG + 32 + 64 + idx + 1)
                    if idx == 0 and g >= 1:
                        scalar.wait_ge(st.h, st0 + 64 * g)
                    slot = ps_tr_s[idx % 2]
                    inst = scalar.activation(
                        out=yt[:, sb * D + oc * 128:sb * D + (oc + 1) * 128],
                        in_=slot, func=ACT.Copy)
                    B.inc(inst, sv, 1)

        @decos["tensor"]
        def _(tensor):
            tensor.wait_ge(ld.h, ld0 + 144 + 64)
            for g in range(NG):
                for idx in range(32):
                    sb, k = idx // 8, idx % 8
                    if k == 0:
                        tensor.wait_ge(vv.h, vv0 + g * VPG + 3 * sb + 3)
                    if idx >= 2:
                        tensor.wait_ge(sv.h, sv0 + g * SPG + 3 * NSUB + idx - 1)
                    elif g >= 1:
                        tensor.wait_ge(sv.h, sv0 + g * SPG)
                    slot = ps_tr_s[idx % 2]
                    inst = tensor.transpose(
                        slot, xq[:, sb * D + k * 128:sb * D + (k + 1) * 128], B.ids)
                    B.inc(inst, tv, 1)
                for oc in range(8):
                    if oc == 0:
                        tensor.wait_ge(sv.h, sv0 + g * SPG + 3 * NSUB + 32)
                    if oc >= 2:
                        tensor.wait_ge(vv.h, vv0 + g * VPG + 3 * NSUB + oc - 1)
                    elif g >= 1:
                        tensor.wait_ge(vv.h, vv0 + (g - 1) * VPG + 3 * NSUB + 8)
                    pslot = ps_mm_s[oc % 2]
                    for k in range(8):
                        inst = tensor.matmul(
                            pslot,
                            wdq[:, k * D + oc * 128:k * D + (oc + 1) * 128],
                            xqT[:, k * GT:(k + 1) * GT],
                            start=(k == 0), stop=(k == 7))
                        B.inc(inst, tv, 1)
                for idx in range(32):
                    sb, oc = idx // 8, idx % 8
                    if sb == 0:
                        tensor.wait_ge(vv.h, vv0 + g * VPG + 3 * NSUB + oc + 1)
                    if idx >= 2:
                        tensor.wait_ge(sv.h,
                                       sv0 + g * SPG + 3 * NSUB + 32 + idx - 1)
                    slot = ps_tr_s[idx % 2]
                    inst = tensor.transpose(
                        slot, yf[:, oc * GT + sb * 128:oc * GT + (sb + 1) * 128],
                        B.ids)
                    B.inc(inst, tv, 1)

        @decos["gpsimd"]
        def _(gpsimd):
            for g in range(NG):
                for sb in range(NSUB):
                    gpsimd.wait_ge(vv.h,
                                   vv0 + g * VPG + 3 * NSUB + 8 + sb + 1)
                    r0 = g * GT + sb * 128
                    inst = gpsimd.dma_start(out=dst[r0:r0 + 128, :],
                                            in_=yt[:, sb * D:(sb + 1) * D])
                    B.inc(inst, st, 16)
            if _DEBUG_OUT[0]:
                inst = gpsimd.dma_start(out=B.dram["dbg"][:, 0:D], in_=xq[:, 0:D])
                B.inc(inst, st, 16)
                inst = gpsimd.dma_start(out=B.dram["dbg"][:, D:D+GT], in_=xqT[:, 0:GT])
                B.inc(inst, st, 16)
                inst = gpsimd.dma_start(out=B.dram["dbg"][:, 2*D:2*D+GT], in_=yf[:, 0:GT])
                B.inc(inst, st, 16)
                gpsimd.wait_ge(st.h, st.v)
            gpsimd.wait_ge(st.h, st.v)

        finish()
    B.phase_idx += 1


def phase_mamba_in(B):
    nc = B.nc
    ld, st, tv, vv, sv = B.ld, B.st, B.tv, B.vv, B.sv
    inp = B.inp
    ld0, sv0, vv0, tv0, st0 = ld.v, sv.v, vv.v, tv.v, st.v
    VPGm = 44
    SPGm = 74
    TPGm = 305
    STG = 464

    import contextlib
    with contextlib.ExitStack() as _stk:
        def A(*args):
            return _stk.enter_context(nc.sbuf_tensor(*args)).ap()

        def PP(*args):
            return _stk.enter_context(nc.psum_tensor(*args)).ap()

        xg = A("pm_x", [128, NSUB * D], F32)
        lnx = A("pm_ln", [128, D], F32)
        lnT = A("pm_lnT", [128, 8 * GT], F32)
        wbuf = A("pm_w", [128, 2 * 8 * 128], F32)
        xc_sb = A("pm_xc", [128, NS * (GT + 3)], F32)
        u_sb = A("pm_u", [128, NS * GT], F32)
        dt_sb = A("pm_dt", [128, NS * GT], F32)
        du_sb = A("pm_du", [128, NS * GT], F32)
        szb = A("pm_sz", [128, GT], F32)
        wx_sb = A("pm_wx", [128, NS * 33], F32)
        cw_sb = A("pm_cw", [128, NS * 4], F32)
        cb_sb = A("pm_cb", [128, NS], F32)
        wdt_sb = A("pm_wdt", [128, NS], F32)
        bdt_sb = A("pm_bdt", [128, NS], F32)
        xdbl_sb = A("pm_xdbl", [33, GT], F32)
        dtr_sb = A("pm_dtr", [128, GT], F32)
        lnsc = A("pm_sc", [128, 8], F32)
        qtmp = A("pm_q", [128, 2 * D], F32)
        ps_tr0 = PP("pm_tr0", [128, 128], F32)
        ps_tr1 = PP("pm_tr1", [128, 128], F32)
        ps_mm0 = PP("pm_mm0", [128, GT], F32)
        ps_mm1 = PP("pm_mm1", [128, GT], F32)
        ps_xd = PP("pm_xd", [33, GT], F32)
        ps_rep = PP("pm_rep", [128, GT], F32)
        block = _stk.enter_context(nc.Block())
        ps_tr_s = [ps_tr0, ps_tr1]
        ps_mm_s = [ps_mm0, ps_mm1]
        decos, finish = barrier_wrap(B, block, None)

        def xw_val(g):
            return 176 if g == 0 else 80 + (g * 36 + 4) * 16

        @decos["sync"]
        def _(sync):
            sync.dma_start(out=wx_sb.rearrange("p (c e) -> p c e", c=NS),
                           in_=inp["w_x_t"].rearrange("(c p) e -> p c e", p=128)
                           ).then_inc(ld.h, 16)
            sync.dma_start(out=cw_sb,
                           in_=inp["conv_w_l"].rearrange("p c k -> p (c k)")
                           ).then_inc(ld.h, 16)
            sync.dma_start(out=cb_sb, in_=inp["conv_b_l"]).then_inc(ld.h, 16)
            sync.dma_start(out=wdt_sb, in_=inp["wdt_l"]).then_inc(ld.h, 16)
            sync.dma_start(out=bdt_sb, in_=inp["bdt_l"]).then_inc(ld.h, 16)
            ld.v += 80
            for g in range(NG):
                if g >= 1:
                    sync.wait_ge(vv.h, vv0 + g * VPGm)
                for sb in range(NSUB):
                    r0 = g * GT + sb * 128
                    sync.dma_start(out=xg[:, sb * D:(sb + 1) * D],
                                   in_=B.dram["xres"][r0:r0 + 128, :]
                                   ).then_inc(ld.h, 16)
                    ld.v += 16
                for ec in range(32):
                    if g * 32 + ec >= 2:
                        sync.wait_ge(tv.h, tv0 + g * TPGm + 32 + (ec - 1) * 8)
                    src = inp["w_in_t"][:, ec * 128:(ec + 1) * 128]
                    src = src.rearrange("(k p) c -> p k c", p=128)
                    wdst = wbuf[:, (ec % 2) * 1024:(ec % 2 + 1) * 1024]
                    sync.dma_start(out=wdst.rearrange("p (k c) -> p k c", k=8),
                                   in_=src).then_inc(ld.h, 16)
                    ld.v += 16

        @decos["vector"]
        def _(vector):
            xc3 = xc_sb.rearrange("p (c t) -> p c t", c=NS)
            vector.memset(xc3[:, :, 0:3], 0.0)
            vector.drain()
            for g in range(NG):
                vb = vv0 + g * VPGm
                sbb = sv0 + g * SPGm
                tb = tv0 + g * TPGm
                vector.wait_ge(ld.h, ld0 + xw_val(g))
                for sb in range(NSUB):
                    xs_ = xg[:, sb * D:(sb + 1) * D]
                    sqi = (1, 3, 13, 23)[sb]
                    vector.wait_ge(sv.h, sbb + sqi)
                    B.ln_tile(vector, None, xs_, lnx, qtmp[:, D:2 * D], lnsc)
                    B.raw_inc(vector, vv, 1)
                    vector.wait_ge(sv.h, sbb + sqi + 1)
                    if sb >= 1:
                        vector.wait_ge(tv.h, tb + 8 * sb)
                    elif g >= 1:
                        vector.wait_ge(tv.h, tv0 + (g - 1) * TPGm + 32)
                    B.ln_apply(vector, xs_, lnx, lnsc[:, 2:3], lnsc[:, 4:5],
                               lnsc[:, 5:6])
                    B.raw_inc(vector, vv, 1)
                for ec in range(16):
                    if ec == 0 and g >= 1:
                        vector.wait_ge(st.h, st0 + g * STG)
                    vector.wait_ge(tv.h, tb + 32 + (ec + 1) * 8)
                    pslot = ps_mm_s[ec % 2]
                    vector.tensor_copy(out=xc_sb[:, ec * (GT + 3) + 3:
                                                 (ec + 1) * (GT + 3)],
                                       in_=pslot)
                    B.raw_inc(vector, vv, 1)
                    base = ec * (GT + 3)
                    cacc = du_sb[:, ec * GT:(ec + 1) * GT]
                    vector.tensor_scalar(out=cacc,
                                         in0=xc_sb[:, base:base + GT],
                                         scalar1=cw_sb[:, ec * 4:ec * 4 + 1],
                                         scalar2=cb_sb[:, ec:ec + 1],
                                         op0=ALU.mult, op1=ALU.add)
                    for kk in range(1, 4):
                        vector.scalar_tensor_tensor(
                            out=cacc, in0=xc_sb[:, base + kk:base + kk + GT],
                            scalar=cw_sb[:, ec * 4 + kk:ec * 4 + kk + 1],
                            in1=cacc, op0=ALU.mult, op1=ALU.add)
                    B.raw_inc(vector, vv, 1)
                vector.wait_ge(tv.h, tb + 32 + 256 + 16)
                vector.tensor_copy(out=xdbl_sb, in_=ps_xd)
                B.raw_inc(vector, vv, 1)
                vector.wait_ge(tv.h, tb + 32 + 256 + 16 + 1)
                vector.tensor_copy(out=dtr_sb, in_=ps_rep)
                B.raw_inc(vector, vv, 1)
                vector.wait_ge(sv.h, sbb + 8 + 32 + 16 + 16)
                dp3 = du_sb.rearrange("p (c t) -> p c t", c=NS)
                dtr3 = dtr_sb.unsqueeze(1).broadcast_to([128, NS, GT])
                wdt3 = wdt_sb.unsqueeze(2).broadcast_to([128, NS, GT])
                bdt3 = bdt_sb.unsqueeze(2).broadcast_to([128, NS, GT])
                vector.tensor_tensor(out=dp3, in0=dtr3, in1=wdt3, op=ALU.mult)
                vector.tensor_tensor(out=dp3, in0=dp3, in1=bdt3, op=ALU.add)
                B.raw_inc(vector, vv, 1)
                vector.wait_ge(sv.h, sbb + SPGm)
                vector.tensor_scalar(out=dt_sb, in0=du_sb, scalar1=1e-4,
                                     scalar2=10.0, op0=ALU.max, op1=ALU.min)
                vector.tensor_tensor(out=du_sb, in0=dt_sb, in1=u_sb,
                                     op=ALU.mult)
                vector.tensor_copy(out=xc3[:, :, 0:3], in_=xc3[:, :, GT:GT + 3])
                B.raw_inc(vector, vv, 1)

        @decos["scalar"]
        def _(scalar):
            def emit_evacs(scalar, tb, lo, hi):
                for idx in range(lo, hi):
                    sb, k = idx // 8, idx % 8
                    scalar.wait_ge(tv.h, tb + idx + 1)
                    slot = ps_tr_s[idx % 2]
                    inst = scalar.activation(
                        out=lnT[:, k * GT + sb * 128:k * GT + (sb + 1) * 128],
                        in_=slot, func=ACT.Copy)
                    B.inc(inst, sv, 1)

            for g in range(NG):
                vb = vv0 + g * VPGm
                sbb = sv0 + g * SPGm
                tb = tv0 + g * TPGm
                scalar.wait_ge(ld.h, ld0 + xw_val(g))
                for sb in range(NSUB):
                    if g * NSUB + sb >= 1:
                        scalar.wait_ge(vv.h, vb + 2 * sb)
                    inst = scalar.activation(out=qtmp[:, D:2 * D],
                                             in_=xg[:, sb * D:(sb + 1) * D],
                                             func=ACT.Square)
                    B.inc(inst, sv, 1)
                    scalar.wait_ge(vv.h, vb + 2 * sb + 1)
                    scalar.activation(out=lnsc[:, 4:5], in_=lnsc[:, 3:4],
                                      func=ACT.Ln, bias=EPS)
                    scalar.drain()
                    scalar.activation(out=lnsc[:, 4:5], in_=lnsc[:, 4:5],
                                      func=ACT.Exp, scale=-0.5)
                    scalar.drain()
                    scalar.sem_inc(sv.h, 1)
                    sv.v += 1
                    if sb >= 1:
                        emit_evacs(scalar, tb, (sb - 1) * 8, sb * 8)
                emit_evacs(scalar, tb, 24, 32)
                for zc in range(16):
                    ec = 16 + zc
                    scalar.wait_ge(tv.h, tb + 32 + (ec + 1) * 8)
                    if zc >= 1:
                        scalar.wait_ge(st.h, st0 + g * STG + zc * 16)
                    pslot = ps_mm_s[ec % 2]
                    inst = scalar.activation(out=szb, in_=pslot, func=ACT.Silu)
                    B.inc(inst, sv, 1)
                for ec in range(16):
                    if ec == 0 and g >= 1:
                        scalar.wait_ge(st.h, st0 + (g - 1) * STG + 320)
                    scalar.wait_ge(vv.h, vb + 8 + 2 * ec + 2)
                    inst = scalar.activation(
                        out=u_sb[:, ec * GT:(ec + 1) * GT],
                        in_=du_sb[:, ec * GT:(ec + 1) * GT], func=ACT.Silu)
                    B.inc(inst, sv, 1)
                scalar.wait_ge(vv.h, vb + 43)
                inst = scalar.activation(out=dt_sb, in_=du_sb, func=ACT.Exp)
                B.inc(inst, sv, 1)
                scalar.drain()
                inst = scalar.activation(out=du_sb, in_=dt_sb, func=ACT.Ln,
                                         bias=1.0)
                B.inc(inst, sv, 1)

        @decos["tensor"]
        def _(tensor):
            tensor.wait_ge(ld.h, ld0 + 176)
            for g in range(NG):
                vb = vv0 + g * VPGm
                sbb = sv0 + g * SPGm
                tb = tv0 + g * TPGm
                for sb in range(NSUB):
                    tensor.wait_ge(vv.h, vb + 2 * sb + 2)
                    for k in range(8):
                        idx = sb * 8 + k
                        if idx >= 2:
                            j = idx - 2
                            ei = 5 + j if j < 8 else (j + 7 if j < 16 else j + 9)
                            tensor.wait_ge(sv.h, sbb + ei)
                        elif g >= 1:
                            tensor.wait_ge(sv.h, sv0 + g * SPGm)
                        slot = ps_tr_s[idx % 2]
                        inst = tensor.transpose(
                            slot, lnx[:, k * 128:(k + 1) * 128], B.ids)
                        B.inc(inst, tv, 1)
                for ec in range(32):
                    if ec == 0:
                        tensor.wait_ge(sv.h, sbb + 8 + 32)
                    wv = 80 + 16 * (g * 36 + 4 + ec + 1)
                    if g == 0 and ec < 2:
                        wv = 176
                    tensor.wait_ge(ld.h, ld0 + wv)
                    if ec >= 2:
                        if ec - 2 < 16:
                            tensor.wait_ge(vv.h, vb + 8 + 2 * (ec - 2) + 1)
                        else:
                            tensor.wait_ge(sv.h,
                                           sbb + 8 + 32 + (ec - 2 - 16) + 1)
                    pslot = ps_mm_s[ec % 2]
                    wslot = wbuf[:, (ec % 2) * 1024:(ec % 2 + 1) * 1024]
                    for k in range(8):
                        inst = tensor.matmul(pslot,
                                             wslot[:, k * 128:(k + 1) * 128],
                                             lnT[:, k * GT:(k + 1) * GT],
                                             start=(k == 0), stop=(k == 7))
                        B.inc(inst, tv, 1)
                tensor.wait_ge(sv.h, sbb + 8 + 32 + 16 + 16)
                for dch in range(NS):
                    inst = tensor.matmul(ps_xd,
                                         wx_sb[:, dch * 33:(dch + 1) * 33],
                                         u_sb[:, dch * GT:(dch + 1) * GT],
                                         start=(dch == 0), stop=(dch == NS - 1))
                    B.inc(inst, tv, 1)
                tensor.wait_ge(vv.h, vb + 41)
                inst = tensor.matmul(ps_rep, B.onesr, xdbl_sb[0:1, :],
                                     start=True, stop=True)
                B.inc(inst, tv, 1)

        @decos["gpsimd"]
        def _(gpsimd):
            for g in range(NG):
                vb = vv0 + g * VPGm
                sbb = sv0 + g * SPGm
                t0c = g * GT
                for zc in range(16):
                    gpsimd.wait_ge(sv.h, sbb + 8 + 32 + zc + 1)
                    inst = gpsimd.dma_start(
                        out=B.dram["sz_fm"][zc * 128:(zc + 1) * 128,
                                            t0c:t0c + GT], in_=szb)
                    B.inc(inst, st, 16)
                    gpsimd.wait_ge(st.h, st.v)
                gpsimd.wait_ge(sv.h, sbb + 8 + 32 + 16 + 16)
                u3 = u_sb.rearrange("p (c t) -> p c t", c=NS)
                dst_u = B.dram["u_fm"].rearrange("(c p) t -> p c t", p=128)
                for q in range(4):
                    inst = gpsimd.dma_start(
                        out=dst_u[:, q * 4:(q + 1) * 4, t0c:t0c + GT],
                        in_=u3[:, q * 4:(q + 1) * 4, :])
                    B.inc(inst, st, 16)
                gpsimd.wait_ge(vv.h, vb + 41)
                inst = gpsimd.dma_start(out=B.dram["bc_fm"][:, t0c:t0c + GT],
                                        in_=xdbl_sb[1:33, :])
                B.inc(inst, st, 16)
                gpsimd.wait_ge(vv.h, vb + VPGm)
                dt3 = dt_sb.rearrange("p (c t) -> p c t", c=NS)
                dst_dt = B.dram["dt_fm"].rearrange("(c p) t -> p c t", p=128)
                du3 = du_sb.rearrange("p (c t) -> p c t", c=NS)
                dst_du = B.dram["du_fm"].rearrange("(c p) t -> p c t", p=128)
                for q in range(4):
                    inst = gpsimd.dma_start(
                        out=dst_dt[:, q * 4:(q + 1) * 4, t0c:t0c + GT],
                        in_=dt3[:, q * 4:(q + 1) * 4, :])
                    B.inc(inst, st, 16)
                for q in range(4):
                    inst = gpsimd.dma_start(
                        out=dst_du[:, q * 4:(q + 1) * 4, t0c:t0c + GT],
                        in_=du3[:, q * 4:(q + 1) * 4, :])
                    B.inc(inst, st, 16)
                gpsimd.wait_ge(st.h, st.v)

        finish()
    B.phase_idx += 1


def phase_scan(B):
    nc = B.nc
    ld, st, tv, vv, sv, gv = B.ld, B.st, B.tv, B.vv, B.sv, B.gv
    ld0, sv0, vv0, tv0, st0, gv0 = ld.v, sv.v, vv.v, tv.v, st.v, gv.v
    TS = TS_SCAN
    FB = TS * 256
    with (
        nc.sbuf_tensor("sc_dt", [128, 2 * NS * TS], F32) as dt_t,
        nc.sbuf_tensor("sc_du", [128, 2 * NS * TS], F32) as du_t,
        nc.sbuf_tensor("sc_b", [128, 2 * NS * TS], F32) as b_t,
        nc.sbuf_tensor("sc_c", [128, 2 * NS * TS], F32) as c_t,
        nc.sbuf_tensor("sc_z", [128, FB], F32) as zbuf,
        nc.sbuf_tensor("sc_e", [128, FB], F32) as ebuf,
        nc.sbuf_tensor("sc_p", [128, FB], F32) as pbuf,
        nc.sbuf_tensor("sc_h", [128, FB], F32) as htraj,
        nc.sbuf_tensor("sc_hc", [128, 256], F32) as hc,
        nc.sbuf_tensor("sc_t1", [128, 256], F32) as t1,
        nc.sbuf_tensor("sc_y", [128, 2 * TS * NS], F32) as ybuf,
        nc.sbuf_tensor("sc_y2", [128, 2 * TS * NS], F32) as ybuf2,
        nc.Block() as block,
    ):
        (dt_t, du_t, b_t, c_t, zbuf, ebuf, pbuf, htraj, hc, t1, ybuf,
         ybuf2) = [t.ap() for t in (dt_t, du_t, b_t, c_t, zbuf, ebuf, pbuf,
                                    htraj, hc, t1, ybuf, ybuf2)]
        decos, finish = barrier_wrap(B, block, None)

        VPC, SPC = 2, 1

        @decos["sync"]
        def _(sync):
            for c in range(NCHUNK):
                pb = c % 2
                if c >= 2:
                    sync.wait_ge(vv.h, vv0 + (c - 1) * VPC)
                t0c = c * TS
                dt3 = dt_t[:, pb * NS * TS:(pb + 1) * NS * TS].rearrange(
                    "p (c t) -> p c t", c=NS)
                du3 = du_t[:, pb * NS * TS:(pb + 1) * NS * TS].rearrange(
                    "p (c t) -> p c t", c=NS)
                for q in range(4):
                    src = B.dram["dt_fm"].rearrange("(c p) t -> p c t", p=128)
                    sync.dma_start(out=dt3[:, q * 4:(q + 1) * 4, :],
                                   in_=src[:, q * 4:(q + 1) * 4, t0c:t0c + TS]
                                   ).then_inc(ld.h, 16)
                    src2 = B.dram["du_fm"].rearrange("(c p) t -> p c t", p=128)
                    sync.dma_start(out=du3[:, q * 4:(q + 1) * 4, :],
                                   in_=src2[:, q * 4:(q + 1) * 4, t0c:t0c + TS]
                                   ).then_inc(ld.h, 16)
                    ld.v += 32
                bsrc = B.dram["bc_fm"][0:16, t0c:t0c + TS]
                bsrc = bsrc.unsqueeze(0).broadcast_to([128, 16, TS])
                sync.dma_start(out=b_t[:, pb * NS * TS:(pb + 1) * NS * TS]
                               .rearrange("p (n t) -> p n t", n=NS),
                               in_=bsrc).then_inc(ld.h, 16)
                csrc = B.dram["bc_fm"][16:32, t0c:t0c + TS]
                csrc = csrc.unsqueeze(0).broadcast_to([128, 16, TS])
                sync.dma_start(out=c_t[:, pb * NS * TS:(pb + 1) * NS * TS]
                               .rearrange("p (n t) -> p n t", n=NS),
                               in_=csrc).then_inc(ld.h, 16)
                ld.v += 32

        @decos["vector"]
        def _(vector):
            for c in range(NCHUNK):
                pb = c % 2
                vector.wait_ge(ld.h, ld0 + max(c + 1, 2) * 160)
                dtv = dt_t[:, pb * NS * TS:(pb + 1) * NS * TS]
                duv = du_t[:, pb * NS * TS:(pb + 1) * NS * TS]
                bv = b_t[:, pb * NS * TS:(pb + 1) * NS * TS]
                cv_ = c_t[:, pb * NS * TS:(pb + 1) * NS * TS]
                z3 = zbuf.rearrange("p (t c n) -> p t c n", t=TS, c=NS)
                dt4 = dtv.rearrange("p (c t) -> p c t", c=NS)
                dt4 = dt4.transpose([0, 2, 1]).unsqueeze(3)
                dt4 = dt4.broadcast_to([128, TS, NS, NS])
                a4 = B.arep.unsqueeze(1).unsqueeze(1)
                a4 = a4.broadcast_to([128, TS, NS, NS])
                vector.tensor_tensor(out=z3, in0=dt4, in1=a4, op=ALU.mult)
                vector.tensor_single_scalar(out=ebuf, in_=zbuf, scalar=-20.0,
                                            op=ALU.max)
                B.raw_inc(vector, vv, 1)
                p3 = pbuf.rearrange("p (t c n) -> p t c n", t=TS, c=NS)
                du4 = duv.rearrange("p (c t) -> p c t", c=NS)
                du4 = du4.transpose([0, 2, 1]).unsqueeze(3)
                du4 = du4.broadcast_to([128, TS, NS, NS])
                b4 = bv.rearrange("p (n t) -> p n t", n=NS)
                b4 = b4.transpose([0, 2, 1]).unsqueeze(2)
                b4 = b4.broadcast_to([128, TS, NS, NS])
                vector.tensor_tensor(out=p3, in0=du4, in1=b4, op=ALU.mult)
                vector.wait_ge(sv.h, sv0 + (c + 1) * SPC)
                vector.tensor_scalar(out=ebuf, in0=pbuf, scalar1=-10.0,
                                     scalar2=10.0, op0=ALU.max, op1=ALU.min)
                if c == 0:
                    vector.memset(hc, 0.0)
                    vector.drain()
                for j in range(TS):
                    hsrc = hc if j == 0 else htraj[:, (j - 1) * 256:j * 256]
                    vector.tensor_tensor(out=t1, in0=hsrc,
                                         in1=zbuf[:, j * 256:(j + 1) * 256],
                                         op=ALU.mult)
                    vector.tensor_tensor(out=t1, in0=t1,
                                         in1=ebuf[:, j * 256:(j + 1) * 256],
                                         op=ALU.add)
                    vector.tensor_scalar(out=htraj[:, j * 256:(j + 1) * 256],
                                         in0=t1, scalar1=-100.0, scalar2=100.0,
                                         op0=ALU.max, op1=ALU.min)
                vector.tensor_copy(out=hc, in_=htraj[:, (TS - 1) * 256:TS * 256])
                h4 = htraj.rearrange("p (t c n) -> p t c n", t=TS, c=NS)
                c4 = cv_.rearrange("p (n t) -> p n t", n=NS)
                c4 = c4.transpose([0, 2, 1]).unsqueeze(2)
                c4 = c4.broadcast_to([128, TS, NS, NS])
                vector.tensor_tensor(out=h4, in0=h4, in1=c4, op=ALU.mult)
                ypb = ybuf2[:, pb * TS * NS:(pb + 1) * TS * NS]
                B.red(vector, ypb.rearrange("p (t c) -> p t c", t=TS), h4,
                      ALU.add)
                yo = ybuf[:, pb * TS * NS:(pb + 1) * TS * NS]
                if c >= 2:
                    vector.wait_ge(st.h, st0 + (c - 1) * 64)
                vector.tensor_copy(
                    out=yo.rearrange("p (c t) -> p c t", c=NS),
                    in_=ypb.rearrange("p (t c) -> p c t", c=NS))
                B.raw_inc(vector, vv, 1)
            vv_final = vv.v

        @decos["gpsimd"]
        def _(gpsimd):
            for c in range(NCHUNK):
                pb = c % 2
                gpsimd.wait_ge(vv.h, vv0 + (c + 1) * VPC)
                yo = ybuf[:, pb * TS * NS:(pb + 1) * TS * NS].rearrange(
                    "p (c t) -> p c t", c=NS)
                t0c = c * TS
                dst = B.dram["y_fm"].rearrange("(c p) t -> p c t", p=128)
                for q in range(4):
                    inst = gpsimd.dma_start(
                        out=dst[:, q * 4:(q + 1) * 4, t0c:t0c + TS],
                        in_=yo[:, q * 4:(q + 1) * 4, :])
                    B.inc(inst, st, 16)

        @decos["scalar"]
        def _(scalar):
            for c in range(NCHUNK):
                scalar.wait_ge(vv.h, vv0 + c * VPC + 1)
                inst = scalar.activation(out=zbuf, in_=ebuf, func=ACT.Exp)
                B.inc(inst, sv, 1)

        finish()
    B.phase_idx += 1


def phase_mamba_out(B):
    nc = B.nc
    ld, st, tv, vv, sv = B.ld, B.st, B.tv, B.vv, B.sv
    inp = B.inp
    ld0, sv0, vv0, tv0, st0 = ld.v, sv.v, vv.v, tv.v, st.v
    G2 = 256
    NG2 = T // G2
    NSUB2 = G2 // 128
    with (
        nc.sbuf_tensor("po_y", [128, NS * G2], F32) as yg,
        nc.sbuf_tensor("po_u", [128, NS * G2], F32) as ug,
        nc.sbuf_tensor("po_sz", [128, NS * G2], F32) as szg,
        nc.sbuf_tensor("po_x", [128, NSUB2 * D], F32) as xg,
        nc.sbuf_tensor("po_w", [128, NS * D], F32) as wout,
        nc.sbuf_tensor("po_sq", [128, 2 * G2], F32) as sqb,
        nc.sbuf_tensor("po_stat", [1, 6 * G2], F32) as stats,
        nc.sbuf_tensor("po_rep", [128, 2 * G2], F32) as reps,
        nc.sbuf_tensor("po_g", [128, NS], F32) as lngc,
        nc.sbuf_tensor("po_b", [128, NS], F32) as lnbc,
        nc.sbuf_tensor("po_d", [128, NS], F32) as dcolc,
        nc.sbuf_tensor("po_out", [128, NSUB2 * D], F32) as outt,
        nc.psum_tensor("po_s1", [1, G2], F32) as ps_s1,
        nc.psum_tensor("po_s2", [1, G2], F32) as ps_s2,
        nc.psum_tensor("po_rep0", [128, G2], F32) as ps_rep0,
        nc.psum_tensor("po_rep1", [128, G2], F32) as ps_rep1,
        nc.psum_tensor("po_o0", [128, 512], F32) as ps_o0,
        nc.psum_tensor("po_o1", [128, 512], F32) as ps_o1,
        nc.Block() as block,
    ):
        (yg, ug, szg, xg, wout, sqb, stats, reps, lngc, lnbc, dcolc, outt,
         ps_s1, ps_s2, ps_rep0, ps_rep1, ps_o0, ps_o1) = [t.ap() for t in (
             yg, ug, szg, xg, wout, sqb, stats, reps, lngc, lnbc, dcolc, outt,
             ps_s1, ps_s2, ps_rep0, ps_rep1, ps_o0, ps_o1)]
        decos, finish = barrier_wrap(B, block, None)
        VPGo = 5
        SPGo = 19
        TPGo = 98
        LDG = 14 * 16

        @decos["sync"]
        def _(sync):
            for k in range(NS):
                sync.dma_start(out=wout[:, k * D:(k + 1) * D],
                               in_=inp["w_out_t"][k * 128:(k + 1) * 128, :]
                               ).then_inc(ld.h, 16)
            sync.dma_start(out=lngc, in_=inp["lng_col"]).then_inc(ld.h, 16)
            sync.dma_start(out=lnbc, in_=inp["lnb_col"]).then_inc(ld.h, 16)
            sync.dma_start(out=dcolc, in_=inp["d_col"]).then_inc(ld.h, 16)
            ld.v += 304
            for g in range(NG2):
                t0c = g * G2
                if g >= 1:
                    sync.wait_ge(vv.h, vv0 + g * VPGo)
                for nm, buf in (("y_fm", yg), ("u_fm", ug), ("sz_fm", szg)):
                    src = B.dram[nm].rearrange("(c p) t -> p c t", p=128)
                    b3 = buf.rearrange("p (c t) -> p c t", c=NS)
                    for q in range(4):
                        sync.dma_start(out=b3[:, q * 4:(q + 1) * 4, :],
                                       in_=src[:, q * 4:(q + 1) * 4,
                                               t0c:t0c + G2]).then_inc(ld.h, 16)
                        ld.v += 16
                for sb in range(NSUB2):
                    r0 = t0c + sb * 128
                    sync.dma_start(out=xg[:, sb * D:(sb + 1) * D],
                                   in_=B.dram["xres"][r0:r0 + 128, :]
                                   ).then_inc(ld.h, 16)
                    ld.v += 16

        @decos["scalar"]
        def _(scalar):
            for g in range(NG2):
                vb = vv0 + g * VPGo
                sbb = sv0 + g * SPGo
                tb = tv0 + g * TPGo
                scalar.wait_ge(ld.h, ld0 + 304 + (g + 1) * LDG)
                for dch in range(NS):
                    if dch >= 2:
                        scalar.wait_ge(tv.h, tb + 16 + dch - 1)
                    if g >= 1 and dch < 2:
                        scalar.wait_ge(tv.h, tv0 + g * TPGo)
                    inst = scalar.activation(
                        out=sqb[:, (dch % 2) * G2:(dch % 2 + 1) * G2],
                        in_=yg[:, dch * G2:(dch + 1) * G2], func=ACT.Square)
                    B.inc(inst, sv, 1)
                scalar.wait_ge(vv.h, vb + 1)
                scalar.activation(out=stats[:, 4 * G2:5 * G2],
                                  in_=stats[:, 3 * G2:4 * G2], func=ACT.Ln,
                                  bias=EPS)
                scalar.drain()
                scalar.activation(out=stats[:, 4 * G2:5 * G2],
                                  in_=stats[:, 4 * G2:5 * G2], func=ACT.Exp,
                                  scale=-0.5)
                scalar.drain()
                scalar.sem_inc(sv.h, 1)
                sv.v += 1
                scalar.wait_ge(tv.h, tb + 32 + 1)
                inst = scalar.activation(out=reps[:, 0:G2], in_=ps_rep0,
                                         func=ACT.Copy)
                B.inc(inst, sv, 1)
                scalar.wait_ge(tv.h, tb + 32 + 2)
                inst = scalar.activation(out=reps[:, G2:2 * G2], in_=ps_rep1,
                                         func=ACT.Copy)
                B.inc(inst, sv, 1)

        @decos["vector"]
        def _(vector):
            for g in range(NG2):
                vb = vv0 + g * VPGo
                sbb = sv0 + g * SPGo
                tb = tv0 + g * TPGo
                vector.wait_ge(tv.h, tb + 32)
                s1 = stats[0:1, 0:G2]
                s2 = stats[0:1, G2:2 * G2]
                mu = stats[0:1, 2 * G2:3 * G2]
                var = stats[0:1, 3 * G2:4 * G2]
                inv = stats[0:1, 4 * G2:5 * G2]
                nms = stats[0:1, 5 * G2:6 * G2]
                vector.tensor_copy(out=s1, in_=ps_s1)
                vector.tensor_copy(out=s2, in_=ps_s2)
                vector.drain()
                vector.tensor_single_scalar(out=mu, in_=s1, scalar=1.0 / DI,
                                            op=ALU.mult)
                vector.tensor_single_scalar(out=var, in_=s2, scalar=1.0 / DI,
                                            op=ALU.mult)
                vector.drain()
                vector.tensor_tensor(out=nms, in0=mu, in1=mu, op=ALU.mult)
                vector.drain()
                vector.tensor_tensor(out=var, in0=var, in1=nms, op=ALU.subtract)
                B.raw_inc(vector, vv, 1)
                vector.wait_ge(sv.h, sbb + 16 + 1)
                vector.tensor_tensor(out=nms, in0=mu, in1=inv, op=ALU.mult)
                vector.drain()
                vector.tensor_single_scalar(out=nms, in_=nms, scalar=-1.0,
                                            op=ALU.mult)
                B.raw_inc(vector, vv, 1)
                vector.wait_ge(sv.h, sbb + 16 + 1 + 2)
                y4 = yg.rearrange("p (c t) -> p c t", c=NS)
                inv_r = reps[:, 0:G2].unsqueeze(1).broadcast_to([128, NS, G2])
                nms_r = reps[:, G2:2 * G2].unsqueeze(1).broadcast_to(
                    [128, NS, G2])
                vector.tensor_tensor(out=y4, in0=y4, in1=inv_r, op=ALU.mult)
                vector.tensor_tensor(out=y4, in0=y4, in1=nms_r, op=ALU.add)
                lng3 = lngc.unsqueeze(2).broadcast_to([128, NS, G2])
                lnb3 = lnbc.unsqueeze(2).broadcast_to([128, NS, G2])
                vector.tensor_tensor(out=y4, in0=y4, in1=lng3, op=ALU.mult)
                vector.tensor_tensor(out=y4, in0=y4, in1=lnb3, op=ALU.add)
                u4 = ug.rearrange("p (c t) -> p c t", c=NS)
                d3 = dcolc.unsqueeze(2).broadcast_to([128, NS, G2])
                vector.tensor_tensor(out=u4, in0=u4, in1=d3, op=ALU.mult)
                vector.tensor_tensor(out=y4, in0=y4, in1=u4, op=ALU.add)
                sz4 = szg.rearrange("p (c t) -> p c t", c=NS)
                vector.tensor_tensor(out=y4, in0=y4, in1=sz4, op=ALU.mult)
                B.raw_inc(vector, vv, 1)
                if g >= 1:
                    vector.wait_ge(st.h, st0 + g * 32)
                for sb in range(NSUB2):
                    vector.wait_ge(tv.h, tb + 34 + (sb + 1) * 32)
                    vector.tensor_copy(out=outt[:, sb * D:sb * D + 512],
                                       in_=ps_o0)
                    vector.tensor_copy(out=outt[:, sb * D + 512:(sb + 1) * D],
                                       in_=ps_o1)
                    vector.drain()
                    vector.tensor_tensor(out=outt[:, sb * D:(sb + 1) * D],
                                         in0=outt[:, sb * D:(sb + 1) * D],
                                         in1=xg[:, sb * D:(sb + 1) * D],
                                         op=ALU.add)
                    B.raw_inc(vector, vv, 1)

        @decos["tensor"]
        def _(tensor):
            for g in range(NG2):
                vb = vv0 + g * VPGo
                sbb = sv0 + g * SPGo
                tb = tv0 + g * TPGo
                tensor.wait_ge(ld.h, ld0 + 304 + (g + 1) * LDG)
                for dch in range(NS):
                    inst = tensor.matmul(ps_s1, B.onesc,
                                         yg[:, dch * G2:(dch + 1) * G2],
                                         start=(dch == 0), stop=(dch == NS - 1))
                    B.inc(inst, tv, 1)
                for dch in range(NS):
                    tensor.wait_ge(sv.h, sbb + dch + 1)
                    inst = tensor.matmul(
                        ps_s2, B.onesc,
                        sqb[:, (dch % 2) * G2:(dch % 2 + 1) * G2],
                        start=(dch == 0), stop=(dch == NS - 1))
                    B.inc(inst, tv, 1)
                tensor.wait_ge(vv.h, vb + 2)
                inst = tensor.matmul(ps_rep0, B.onesr,
                                     stats[0:1, 4 * G2:5 * G2],
                                     start=True, stop=True)
                B.inc(inst, tv, 1)
                inst = tensor.matmul(ps_rep1, B.onesr,
                                     stats[0:1, 5 * G2:6 * G2],
                                     start=True, stop=True)
                B.inc(inst, tv, 1)
                tensor.wait_ge(vv.h, vb + 3)
                for sb in range(NSUB2):
                    if g + sb > 0:
                        prev = vb + 3 + sb if sb >= 1 else vb - VPGo + 5
                        tensor.wait_ge(vv.h, prev)
                    for k in range(NS):
                        stat = yg[:, k * G2 + sb * 128:k * G2 + (sb + 1) * 128]
                        inst = tensor.matmul(ps_o0, stat,
                                             wout[:, k * D:k * D + 512],
                                             start=(k == 0), stop=(k == NS - 1))
                        B.inc(inst, tv, 1)
                        inst = tensor.matmul(ps_o1, stat,
                                             wout[:, k * D + 512:(k + 1) * D],
                                             start=(k == 0), stop=(k == NS - 1))
                        B.inc(inst, tv, 1)

        @decos["gpsimd"]
        def _(gpsimd):
            for g in range(NG2):
                vb = vv0 + g * VPGo
                for sb in range(NSUB2):
                    gpsimd.wait_ge(vv.h, vb + 3 + sb + 1)
                    r0 = g * G2 + sb * 128
                    inst = gpsimd.dma_start(out=B.dram["xres"][r0:r0 + 128, :],
                                            in_=outt[:, sb * D:(sb + 1) * D])
                    B.inc(inst, st, 16)
                gpsimd.wait_ge(st.h, st.v)

        finish()
    B.phase_idx += 1


MOE_Q = 1024
MOE_NB = MOE_Q // 128
MOE_NE = 24


def _moe_stage1(B, src, q, tl):
    nc = B.nc
    ld, st, tv, vv, sv = B.ld, B.st, B.tv, B.vv, B.sv
    inp = B.inp
    ld0, sv0, vv0, tv0, st0 = ld.v, sv.v, vv.v, tv.v, st.v
    lnT, xg, lnx, qtmp, lnsc, rw_sb, cbs, lg_sb, sc_sb, rt, comb_sb, combT = (
        tl["lnT"], tl["xg"], tl["lnx"], tl["qtmp"], tl["lnsc"], tl["rw"],
        tl["cb"], tl["lg"], tl["sc"], tl["rt"], tl["comb"], tl["combT"])
    NB = MOE_NB
    first = (q == 0 and tl["load_consts"])

    import contextlib
    with contextlib.ExitStack() as _stk:
        def PP(nm, shp):
            return _stk.enter_context(
                nc.psum_tensor(nm + tl["sfx"] + f"_{q}", shp, F32)).ap()

        ps_tr0 = PP("m1tr0", [128, 128])
        ps_tr1 = PP("m1tr1", [128, 128])
        ps_lg = PP("m1lg", [128, 16])
        ps_ct = PP("m1ct", [16, 128])
        ps_tr_s = [ps_tr0, ps_tr1]
        block = _stk.enter_context(nc.Block())
        decos, finish = barrier_wrap(B, block, None)

        cst = 32 if first else 0

        def xw(blk):
            if blk < 2:
                return ld0 + cst + 32
            return ld0 + cst + 16 * (blk + 1)

        @decos["sync"]
        def _(sync):
            if first:
                sync.dma_start(out=rw_sb.rearrange("p (k e) -> p k e", k=8),
                               in_=inp["rw_t"].rearrange("(k p) e -> p k e",
                                                         p=128)
                               ).then_inc(ld.h, 16)
                sync.dma_start(out=cbs, in_=inp["cb_rep"]).then_inc(ld.h, 16)
                ld.v += 32
            for blk in range(NB):
                if blk >= 2:
                    sync.wait_ge(vv.h, vv0 + (blk - 2) * 4 + 2)
                r0 = q * MOE_Q + blk * 128
                sync.dma_start(out=xg[:, (blk % 2) * D:(blk % 2 + 1) * D],
                               in_=src[r0:r0 + 128, :]).then_inc(ld.h, 16)
                ld.v += 16

        @decos["scalar"]
        def _(scalar):
            for blk in range(NB):
                sbb = sv0 + blk * 12
                vb = vv0 + blk * 4
                tb = tv0 + blk * 17
                scalar.wait_ge(ld.h, xw(blk))
                if blk >= 2:
                    scalar.wait_ge(vv.h, vv0 + (blk - 2) * 4 + 1)
                inst = scalar.activation(
                    out=qtmp[:, (blk % 2) * D:(blk % 2 + 1) * D],
                    in_=xg[:, (blk % 2) * D:(blk % 2 + 1) * D],
                    func=ACT.Square)
                B.inc(inst, sv, 1)
                scalar.wait_ge(vv.h, vb + 1)
                scalar.activation(out=lnsc[:, 4:5], in_=lnsc[:, 3:4],
                                  func=ACT.Ln, bias=EPS)
                scalar.drain()
                scalar.activation(out=lnsc[:, 4:5], in_=lnsc[:, 4:5],
                                  func=ACT.Exp, scale=-0.5)
                scalar.drain()
                scalar.sem_inc(sv.h, 1)
                sv.v += 1
                for k in range(8):
                    scalar.wait_ge(tv.h, tb + k + 1)
                    inst = scalar.activation(
                        out=lnT[:, k * MOE_Q + blk * 128:
                                k * MOE_Q + (blk + 1) * 128],
                        in_=ps_tr_s[k % 2], func=ACT.Copy)
                    B.inc(inst, sv, 1)
                scalar.wait_ge(vv.h, vb + 3)
                inst = scalar.activation(out=sc_sb, in_=lg_sb,
                                         func=ACT.Sigmoid)
                B.inc(inst, sv, 1)
                scalar.wait_ge(tv.h, tb + 17)
                inst = scalar.activation(
                    out=combT[:, blk * 128:(blk + 1) * 128],
                    in_=ps_ct, func=ACT.Copy)
                B.inc(inst, sv, 1)

        @decos["vector"]
        def _(vector):
            for blk in range(NB):
                sbb = sv0 + blk * 12
                vb = vv0 + blk * 4
                tb = tv0 + blk * 17
                xs_ = xg[:, (blk % 2) * D:(blk % 2 + 1) * D]
                sq_ = qtmp[:, (blk % 2) * D:(blk % 2 + 1) * D]
                vector.wait_ge(ld.h, xw(blk))
                vector.wait_ge(sv.h, sbb + 1)
                B.ln_tile(vector, None, xs_, None, sq_, lnsc)
                B.raw_inc(vector, vv, 1)
                vector.wait_ge(sv.h, sbb + 2)
                if blk >= 2:
                    vector.wait_ge(tv.h, tv0 + (blk - 2) * 17 + 8)
                B.ln_apply(vector, xs_, lnx[:, (blk % 2) * D:(blk % 2 + 1) * D],
                           lnsc[:, 2:3], lnsc[:, 4:5], lnsc[:, 5:6])
                B.raw_inc(vector, vv, 1)
                vector.wait_ge(tv.h, tb + 16)
                if blk >= 1:
                    vector.wait_ge(sv.h, sv0 + (blk - 1) * 12 + 11)
                vector.tensor_copy(out=lg_sb, in_=ps_lg)
                B.raw_inc(vector, vv, 1)
                vector.wait_ge(sv.h, sbb + 11)
                if blk >= 1:
                    vector.wait_ge(tv.h, tv0 + (blk - 1) * 17 + 17)
                sbv = rt[:, 0:16]
                ptmp = rt[:, 16:40]
                gsc = rt[:, 40:44]
                g2 = rt[:, 44:48]
                g3 = rt[:, 48:52]
                mtmp = rt[:, 52:58]
                thr2 = rt[:, 58:59]
                gmb = rt[:, 59:63]
                sbm = rt[:, 64:80]
                cnt = rt[:, 80:96]
                sel = rt[:, 96:112]
                wsum = rt[:, 112:113]
                wrcp = rt[:, 113:114]
                vector.tensor_tensor(out=sbv, in0=sc_sb, in1=cbs, op=ALU.add)
                vector.drain()
                sb3 = sbv.rearrange("p (g e) -> p g e", g=4)
                pairs = [(0, 1), (0, 2), (0, 3), (1, 2), (1, 3), (2, 3)]
                for ix, (i, j) in enumerate(pairs):
                    vector.tensor_tensor(out=ptmp[:, ix * 4:(ix + 1) * 4],
                                         in0=sb3[:, :, i], in1=sb3[:, :, j],
                                         op=ALU.add)
                vector.drain()
                vector.tensor_tensor(out=gsc, in0=ptmp[:, 0:4],
                                     in1=ptmp[:, 4:8], op=ALU.max)
                vector.tensor_tensor(out=g2, in0=ptmp[:, 8:12],
                                     in1=ptmp[:, 12:16], op=ALU.max)
                vector.tensor_tensor(out=g3, in0=ptmp[:, 16:20],
                                     in1=ptmp[:, 20:24], op=ALU.max)
                vector.drain()
                vector.tensor_tensor(out=gsc, in0=gsc, in1=g2, op=ALU.max)
                vector.drain()
                vector.tensor_tensor(out=gsc, in0=gsc, in1=g3, op=ALU.max)
                vector.drain()
                for ix, (i, j) in enumerate(pairs):
                    vector.tensor_tensor(out=mtmp[:, ix:ix + 1],
                                         in0=gsc[:, i:i + 1],
                                         in1=gsc[:, j:j + 1], op=ALU.min)
                B.red(vector, thr2, mtmp, ALU.max)
                vector.tensor_scalar(out=gmb, in0=gsc, scalar1=thr2,
                                     scalar2=1.0, op0=ALU.is_ge,
                                     op1=ALU.subtract)
                vector.drain()
                vector.tensor_single_scalar(out=gmb, in_=gmb, scalar=1e30,
                                            op=ALU.mult)
                vector.drain()
                gm4 = gmb.unsqueeze(2).broadcast_to([128, 4, 4])
                vector.tensor_tensor(out=sbm.rearrange("p (g e) -> p g e", g=4),
                                     in0=sb3, in1=gm4, op=ALU.add)
                vector.drain()
                vector.tensor_scalar(out=cnt, in0=sbm, scalar1=sbm[:, 0:1],
                                     scalar2=None, op0=ALU.is_lt)
                vector.drain()
                for j in range(1, 16):
                    vector.scalar_tensor_tensor(out=cnt, in0=sbm,
                                                scalar=sbm[:, j:j + 1],
                                                in1=cnt, op0=ALU.is_lt,
                                                op1=ALU.add)
                    vector.drain()
                vector.tensor_single_scalar(out=sel, in_=cnt, scalar=3.5,
                                            op=ALU.is_le)
                vector.drain()
                vector.tensor_tensor(out=sel, in0=sel, in1=sc_sb, op=ALU.mult)
                B.red(vector, wsum, sel, ALU.add)
                vector.tensor_single_scalar(out=wsum, in_=wsum, scalar=1e-20,
                                            op=ALU.add)
                B.recip(vector, wrcp, wsum)
                vector.tensor_scalar(out=comb_sb, in0=sel, scalar1=wrcp,
                                     scalar2=None, op0=ALU.mult)
                B.raw_inc(vector, vv, 1)

        @decos["tensor"]
        def _(tensor):
            for blk in range(NB):
                sbb = sv0 + blk * 12
                vb = vv0 + blk * 4
                tb = tv0 + blk * 17
                tensor.wait_ge(vv.h, vb + 2)
                for k in range(8):
                    if k >= 2:
                        tensor.wait_ge(sv.h, sbb + 3 + (k - 2))
                    elif blk >= 1:
                        tensor.wait_ge(sv.h, sv0 + (blk - 1) * 12 + 10)
                    inst = tensor.transpose(
                        ps_tr_s[k % 2],
                        lnx[:, (blk % 2) * D + k * 128:
                            (blk % 2) * D + (k + 1) * 128], B.ids)
                    B.inc(inst, tv, 1)
                if blk >= 1:
                    tensor.wait_ge(vv.h, vv0 + (blk - 1) * 4 + 3)
                if first and blk == 0:
                    tensor.wait_ge(ld.h, ld0 + 64)
                for k in range(8):
                    tensor.wait_ge(sv.h, sbb + 3 + k)
                    inst = tensor.matmul(
                        ps_lg,
                        lnT[:, k * MOE_Q + blk * 128:
                            k * MOE_Q + (blk + 1) * 128],
                        rw_sb[:, k * 16:(k + 1) * 16],
                        start=(k == 0), stop=(k == 7))
                    B.inc(inst, tv, 1)
                tensor.wait_ge(vv.h, vb + 4)
                if blk >= 1:
                    tensor.wait_ge(sv.h, sv0 + (blk - 1) * 12 + 12)
                inst = tensor.transpose(ps_ct, comb_sb, B.ids)
                B.inc(inst, tv, 1)

        @decos["gpsimd"]
        def _(gpsimd):
            gpsimd.wait_ge(sv.h, sv0 + (NB - 1) * 12 + 12)
            inst = gpsimd.dma_start(
                out=B.dram["comb_fm"][:, q * MOE_Q:(q + 1) * MOE_Q],
                in_=combT)
            B.inc(inst, st, 16)
            gpsimd.wait_ge(st.h, st.v)

        finish()
    B.phase_idx += 1


def _moe_stage2(B, src, dst, q, tl):
    nc = B.nc
    ld, st, tv, vv, sv = B.ld, B.st, B.tv, B.vv, B.sv
    inp = B.inp
    ld0, sv0, vv0, tv0, st0 = ld.v, sv.v, vv.v, tv.v, st.v
    lnT, acc, gsb, usb, dsb, cfm, hfm, sil, htmp, xg2, outf, crow = (
        tl["lnT"], tl["acc"], tl["gsb"], tl["usb"], tl["dsb"], tl["cfm"],
        tl["hfm"], tl["sil"], tl["htmp"], tl["xg2"], tl["outf"], tl["crow"])
    NE = MOE_NE
    NB = MOE_NB

    its = [(e, tg) for e in range(NE) for tg in range(2)]
    t_gate, t_up, t_down, t_repi = {}, {}, {}, {}
    tp = tv0
    for idx, (e, tg) in enumerate(its):
        if e < 16 and tg == 0:
            for i in range(2):
                tp += 1
                t_repi[(e, i)] = tp
        for hh in range(2):
            tp += 8
            t_gate[(e, tg, hh)] = tp
            tp += 8
            t_up[(e, tg, hh)] = tp
        de, dtg = its[idx - 1] if idx >= 1 else (None, None)
        if de is not None:
            for ts in range(4):
                tp += 4
                t_down[(de, dtg, ts)] = tp
    for ts in range(4):
        tp += 4
        t_down[(its[-1][0], its[-1][1], ts)] = tp
    s_silu, s_repev = {}, {}
    sp = sv0
    for e in range(NE):
        if e < 16:
            for i in range(2):
                sp += 1
                s_repev[(e, i)] = sp
        for tg in range(2):
            for hh in range(2):
                sp += 1
                s_silu[(e, tg, hh)] = sp
    v_h, v_acc, v_fin = {}, {}, {}
    vp = vv0
    for e in range(NE):
        for tg in range(2):
            for hh in range(2):
                vp += 1
                v_h[(e, tg, hh)] = vp
            for ts in range(4):
                vp += 1
                v_acc[(e, tg, ts)] = vp
    for blk in range(NB):
        vp += 1
        v_fin[blk] = vp
    ld_e, ld_x = {}, {}
    lp = ld0
    for e in range(NE):
        lp += 48 + (16 if e < 16 else 0)
        ld_e[e] = lp
    for blk in range(NB):
        lp += 16
        ld_x[blk] = lp
    st_fin = {}
    stp = st0
    for blk in range(NB):
        stp += 16
        st_fin[blk] = stp

    def ldw_e(e):
        return ld_e[1] if e < 2 else ld_e[e]

    def ldw_x(blk):
        return ld_x[1] if blk < 2 else ld_x[blk]

    import contextlib
    with contextlib.ExitStack() as _stk:
        def PP(nm, shp):
            return _stk.enter_context(
                nc.psum_tensor(nm + tl["sfx"] + f"_{q}", shp, F32)).ap()

        ps_g = [PP("m2g0", [128, 512]), PP("m2g1", [128, 512])]
        ps_u = [PP("m2u0", [128, 512]), PP("m2u1", [128, 512])]
        ps_o0 = PP("m2o0", [128, 512])
        ps_o1 = PP("m2o1", [128, 512])
        ps_rep = PP("m2rep", [128, 1024])
        block = _stk.enter_context(nc.Block())
        decos, finish = barrier_wrap(B, block, None)
        gsb3 = gsb.rearrange("p (s k h) -> p s k h", s=2, k=8)
        usb3 = usb.rearrange("p (s k h) -> p s k h", s=2, k=8)
        dsb3 = dsb.rearrange("p (s c o) -> p s c o", s=2, c=2)
        hfm3 = hfm.rearrange("p (s h t) -> p s h t", s=2, h=2)

        @decos["sync"]
        def _(sync):
            for e in range(NE):
                if e >= 2:
                    sync.wait_ge(tv.h, t_down[(e - 2, 1, 3)])
                sl = e % 2
                if e < 16:
                    gs = inp["gate_w"][e].rearrange("(k p) h -> p k h", p=128)
                    us = inp["up_w"][e].rearrange("(k p) h -> p k h", p=128)
                    ds = inp["down_w"][e].rearrange("(c p) o -> p c o", p=128)
                else:
                    es = e - 16
                    gs = inp["sgw_t"][:, es * 256:(es + 1) * 256].rearrange(
                        "(k p) h -> p k h", p=128)
                    us = inp["suw_t"][:, es * 256:(es + 1) * 256].rearrange(
                        "(k p) h -> p k h", p=128)
                    ds = inp["sdw_t"][es * 256:(es + 1) * 256, :].rearrange(
                        "(c p) o -> p c o", p=128)
                sync.dma_start(out=gsb3[:, sl], in_=gs).then_inc(ld.h, 16)
                sync.dma_start(out=usb3[:, sl], in_=us).then_inc(ld.h, 16)
                sync.dma_start(out=dsb3[:, sl], in_=ds).then_inc(ld.h, 16)
                ld.v += 48
                if e < 16:
                    sync.dma_start(
                        out=crow[0:1, sl * MOE_Q:(sl + 1) * MOE_Q],
                        in_=B.dram["comb_fm"][e:e + 1,
                                              q * MOE_Q:(q + 1) * MOE_Q]
                        ).then_inc(ld.h, 16)
                    ld.v += 16
            for blk in range(NB):
                if blk >= 2:
                    sync.wait_ge(vv.h, v_fin[blk - 2])
                r0 = q * MOE_Q + blk * 128
                sync.dma_start(out=xg2[:, (blk % 2) * D:(blk % 2 + 1) * D],
                               in_=src[r0:r0 + 128, :]).then_inc(ld.h, 16)
                ld.v += 16

        def emit_downs(tensor, idx2, de, dtg):
            dsl = de % 2
            tensor.wait_ge(vv.h, v_h[(de, dtg, 1)])
            for ts in range(4):
                if ts >= 1:
                    tensor.wait_ge(vv.h, v_acc[(de, dtg, ts - 1)])
                elif idx2 >= 1:
                    pe, ptg = its[idx2 - 1]
                    tensor.wait_ge(vv.h, v_acc[(pe, ptg, 3)])
                for hh in range(2):
                    inst = tensor.matmul(
                        ps_o0, hfm3[:, dtg, hh, ts * 128:(ts + 1) * 128],
                        dsb3[:, dsl, hh, 0:512],
                        start=(hh == 0), stop=(hh == 1))
                    B.inc(inst, tv, 1)
                    inst = tensor.matmul(
                        ps_o1, hfm3[:, dtg, hh, ts * 128:(ts + 1) * 128],
                        dsb3[:, dsl, hh, 512:1024],
                        start=(hh == 0), stop=(hh == 1))
                    B.inc(inst, tv, 1)

        @decos["tensor"]
        def _(tensor):
            for idx, (e, tg) in enumerate(its):
                sl = e % 2
                if tg == 0:
                    tensor.wait_ge(ld.h, ldw_e(e))
                    if e < 16:
                        if 1 <= e:
                            tensor.wait_ge(sv.h, s_repev[(e - 1, 1)])
                        for i in range(2):
                            inst = tensor.matmul(
                                ps_rep[:, i * 512:(i + 1) * 512], B.onesr,
                                crow[0:1, sl * MOE_Q + i * 512:
                                     sl * MOE_Q + (i + 1) * 512],
                                start=True, stop=True)
                            B.inc(inst, tv, 1)
                prev = its[idx - 1] if idx >= 1 else None
                for hh in range(2):
                    if prev is not None:
                        tensor.wait_ge(sv.h, s_silu[(prev[0], prev[1], hh)])
                    for k in range(8):
                        inst = tensor.matmul(
                            ps_g[hh],
                            gsb3[:, sl, k, hh * 128:(hh + 1) * 128],
                            lnT[:, k * MOE_Q + tg * 512:
                                k * MOE_Q + (tg + 1) * 512],
                            start=(k == 0), stop=(k == 7))
                        B.inc(inst, tv, 1)
                    if prev is not None:
                        tensor.wait_ge(vv.h, v_h[(prev[0], prev[1], hh)])
                    for k in range(8):
                        inst = tensor.matmul(
                            ps_u[hh],
                            usb3[:, sl, k, hh * 128:(hh + 1) * 128],
                            lnT[:, k * MOE_Q + tg * 512:
                                k * MOE_Q + (tg + 1) * 512],
                            start=(k == 0), stop=(k == 7))
                        B.inc(inst, tv, 1)
                if idx >= 1:
                    de, dtg = its[idx - 1]
                    emit_downs(tensor, idx - 1, de, dtg)
            emit_downs(tensor, len(its) - 1, its[-1][0], its[-1][1])

        @decos["scalar"]
        def _(scalar):
            for e in range(NE):
                sl = e % 2
                if e < 16:
                    if e >= 2:
                        scalar.wait_ge(vv.h, v_h[(e - 2, 1, 1)])
                    for i in range(2):
                        scalar.wait_ge(tv.h, t_repi[(e, i)])
                        inst = scalar.activation(
                            out=cfm[:, sl * MOE_Q + i * 512:
                                    sl * MOE_Q + (i + 1) * 512],
                            in_=ps_rep[:, i * 512:(i + 1) * 512],
                            func=ACT.Copy)
                        B.inc(inst, sv, 1)
                for tg in range(2):
                    prev = (e, tg - 1) if tg >= 1 else \
                        ((e - 1, 1) if e >= 1 else None)
                    for hh in range(2):
                        scalar.wait_ge(tv.h, t_gate[(e, tg, hh)])
                        if prev is not None:
                            scalar.wait_ge(vv.h,
                                           v_h[(prev[0], prev[1], hh)])
                        inst = scalar.activation(
                            out=sil[:, hh * 512:(hh + 1) * 512],
                            in_=ps_g[hh], func=ACT.Silu)
                        B.inc(inst, sv, 1)

        @decos["vector"]
        def _(vector):
            for e in range(NE):
                sl = e % 2
                for tg in range(2):
                    for hh in range(2):
                        vector.wait_ge(sv.h, s_silu[(e, tg, hh)])
                        vector.wait_ge(tv.h, t_up[(e, tg, hh)])
                        if hh == 0 and e >= 1:
                            vector.wait_ge(tv.h, t_down[(e - 1, tg, 3)])
                        dstc = hfm3[:, tg, hh, :]
                        if e < 16:
                            vector.wait_ge(sv.h, s_repev[(e, 1)])
                            vector.tensor_tensor(
                                out=htmp, in0=sil[:, hh * 512:(hh + 1) * 512],
                                in1=ps_u[hh], op=ALU.mult)
                            vector.tensor_tensor(
                                out=dstc, in0=htmp,
                                in1=cfm[:, sl * MOE_Q + tg * 512:
                                        sl * MOE_Q + (tg + 1) * 512],
                                op=ALU.mult)
                        else:
                            vector.tensor_tensor(
                                out=dstc, in0=sil[:, hh * 512:(hh + 1) * 512],
                                in1=ps_u[hh], op=ALU.mult)
                        B.raw_inc(vector, vv, 1)
                    for ts in range(4):
                        vector.wait_ge(tv.h, t_down[(e, tg, ts)])
                        col = (tg * 4 + ts) * D
                        if e == 0:
                            vector.tensor_copy(out=acc[:, col:col + 512],
                                               in_=ps_o0)
                            vector.tensor_copy(out=acc[:, col + 512:col + 1024],
                                               in_=ps_o1)
                        else:
                            vector.tensor_tensor(out=acc[:, col:col + 512],
                                                 in0=acc[:, col:col + 512],
                                                 in1=ps_o0, op=ALU.add)
                            vector.tensor_tensor(
                                out=acc[:, col + 512:col + 1024],
                                in0=acc[:, col + 512:col + 1024],
                                in1=ps_o1, op=ALU.add)
                        B.raw_inc(vector, vv, 1)
            for blk in range(NB):
                vector.wait_ge(ld.h, ldw_x(blk))
                if blk >= 2:
                    vector.wait_ge(st.h, st_fin[blk - 2])
                vector.scalar_tensor_tensor(
                    out=outf[:, (blk % 2) * D:(blk % 2 + 1) * D],
                    in0=acc[:, blk * D:(blk + 1) * D], scalar=0.5,
                    in1=xg2[:, (blk % 2) * D:(blk % 2 + 1) * D],
                    op0=ALU.mult, op1=ALU.add)
                B.raw_inc(vector, vv, 1)

        @decos["gpsimd"]
        def _(gpsimd):
            for blk in range(NB):
                gpsimd.wait_ge(vv.h, v_fin[blk])
                r0 = q * MOE_Q + blk * 128
                inst = gpsimd.dma_start(
                    out=dst[r0:r0 + 128, :],
                    in_=outf[:, (blk % 2) * D:(blk % 2 + 1) * D])
                B.inc(inst, st, 16)
                gpsimd.wait_ge(st.h, st.v)

        finish()
    B.phase_idx += 1


def phase_moe(B, src, dst):
    nc = B.nc
    sfx = f"_{B.phase_idx}"
    import contextlib
    with contextlib.ExitStack() as _stk:
        def A(nm, shp):
            return _stk.enter_context(nc.sbuf_tensor(nm + sfx, shp, F32)).ap()

        tl = {
            "sfx": sfx,
            "load_consts": True,
            "lnT": A("mo_lnT", [128, 8 * MOE_Q]),
            "acc": A("mo_acc", [128, MOE_NB * D]),
            "xg": A("mo_x", [128, 2 * D]),
            "lnx": A("mo_ln", [128, 2 * D]),
            "qtmp": A("mo_sq", [128, 2 * D]),
            "lnsc": A("mo_st", [128, 8]),
            "rw": A("mo_rw", [128, 128]),
            "cb": A("mo_cb", [128, 16]),
            "lg": A("mo_lg", [128, 16]),
            "sc": A("mo_sc", [128, 16]),
            "rt": A("mo_rt", [128, 128]),
            "comb": A("mo_comb", [128, 16]),
            "combT": A("mo_combT", [16, MOE_Q]),
            "gsb": A("mo_gw", [128, 2 * 8 * 256]),
            "usb": A("mo_uw", [128, 2 * 8 * 256]),
            "dsb": A("mo_dw", [128, 2 * 2 * D]),
            "cfm": A("mo_cfm", [128, 2 * MOE_Q]),
            "crow": A("mo_crow", [1, 2 * MOE_Q]),
            "hfm": A("mo_hfm", [128, 2 * 2 * 512]),
            "sil": A("mo_sil", [128, 2 * 512]),
            "htmp": A("mo_ht", [128, 512]),
            "xg2": A("mo_x2", [128, 2 * D]),
            "outf": A("mo_out", [128, 2 * D]),
        }
        for q in range(4):
            _moe_stage1(B, src, q, tl)
            _moe_stage2(B, src, dst, q, tl)
            tl["load_consts"] = False


def _attn_qkv(B, tl):
    nc = B.nc
    ld, st, tv, vv, sv = B.ld, B.st, B.tv, B.vv, B.sv
    inp = B.inp
    ld0, sv0, vv0, tv0, st0 = ld.v, sv.v, vv.v, tv.v, st.v
    wq_sb, wk_sb, wv_sb, lnT, qs, ks, vs, xg, lnx, qtmp, lnsc, svrow = (
        tl["wq"], tl["wk"], tl["wv"], tl["lnT"], tl["qs"], tl["ks"], tl["vs"],
        tl["xg"], tl["lnx"], tl["qtmp"], tl["lnsc"], tl["svrow"])
    NGR = 8

    s_sq, s_inv, s_ev, s_unit = {}, {}, {}, {}
    sp = sv0
    for g in range(NGR):
        for sb in range(4):
            u = g * 4 + sb
            sp += 1
            s_sq[u] = sp
            sp += 1
            s_inv[u] = sp
            if sb >= 1:
                for k in range(8):
                    sp += 1
                    s_ev[(g * 4 + sb - 1, k)] = sp
        for k in range(8):
            sp += 1
            s_ev[(g * 4 + 3, k)] = sp
        for j in range(24):
            sp += 1
            s_unit[(g, j)] = sp
    sv_fin = sp + 2
    v_stats, v_apply = {}, {}
    vp = vv0
    for u in range(4 * NGR):
        vp += 1
        v_stats[u] = vp
        vp += 1
        v_apply[u] = vp
    t_tr, t_trd, t_unit, t_sv = {}, {}, {}, {}
    tp = tv0
    for g in range(NGR):
        for sb in range(4):
            u = g * 4 + sb
            for k in range(8):
                tp += 1
                t_tr[(u, k)] = tp
            t_trd[u] = tp
        for j in range(24):
            tp += 8
            t_unit[(g, j)] = tp
        for idx in range(8):
            tp += 1
            t_sv[(g, idx)] = tp
    st_q, st_k, st_v = {}, {}, {}
    stp = st0
    for g in range(NGR):
        stp += 16
        st_q[g] = stp
        stp += 16
        st_k[g] = stp
        for ts in range(4):
            stp += 16
            st_v[(g, ts)] = stp

    def xw(u):
        return ld0 + 80 if u < 2 else ld0 + 48 + 16 * (u + 1)

    import contextlib
    with contextlib.ExitStack() as _stk:
        def PP(nm, shp):
            return _stk.enter_context(nc.psum_tensor(nm, shp, F32)).ap()

        ps_tr = [PP("aqtr0", [128, 128]), PP("aqtr1", [128, 128])]
        ps_mm = [PP(f"aqmm{i}", [128, 512]) for i in range(4)]
        ps_sv = [PP("aqsv0", [1, 512]), PP("aqsv1", [1, 512])]
        block = _stk.enter_context(nc.Block())
        decos, finish = barrier_wrap(B, block, None)
        wq3 = wq_sb.rearrange("p (k o) -> p k o", k=8)
        wk3 = wk_sb.rearrange("p (k o) -> p k o", k=8)
        wv3 = wv_sb.rearrange("p (k o) -> p k o", k=8)

        @decos["sync"]
        def _(sync):
            sync.dma_start(out=wq3, in_=inp["wq"].rearrange(
                "(k p) o -> p k o", p=128)).then_inc(ld.h, 16)
            sync.dma_start(out=wk3, in_=inp["wk"].rearrange(
                "(k p) o -> p k o", p=128)).then_inc(ld.h, 16)
            sync.dma_start(out=wv3, in_=inp["wv"].rearrange(
                "(k p) o -> p k o", p=128)).then_inc(ld.h, 16)
            ld.v += 48
            for u in range(4 * NGR):
                if u >= 2:
                    sync.wait_ge(vv.h, v_apply[u - 2])
                r0 = u * 128
                sync.dma_start(out=xg[:, (u % 2) * D:(u % 2 + 1) * D],
                               in_=B.dram["xres"][r0:r0 + 128, :]
                               ).then_inc(ld.h, 16)
                ld.v += 16

        @decos["scalar"]
        def _(scalar):
            def emit_ev(scalar, u):
                g, sb = u // 4, u % 4
                for k in range(8):
                    scalar.wait_ge(tv.h, t_tr[(u, k)])
                    if k == 0 and sb == 0 and g >= 1:
                        scalar.wait_ge(tv.h, t_unit[(g - 1, 23)])
                    inst = scalar.activation(
                        out=lnT[:, k * 512 + sb * 128:k * 512 + (sb + 1) * 128],
                        in_=ps_tr[k % 2], func=ACT.Copy)
                    B.inc(inst, sv, 1)

            for g in range(NGR):
                for sb in range(4):
                    u = g * 4 + sb
                    scalar.wait_ge(ld.h, xw(u))
                    if u >= 1:
                        scalar.wait_ge(vv.h, v_stats[u - 1])
                    inst = scalar.activation(
                        out=qtmp, in_=xg[:, (u % 2) * D:(u % 2 + 1) * D],
                        func=ACT.Square)
                    B.inc(inst, sv, 1)
                    scalar.wait_ge(vv.h, v_stats[u])
                    scalar.activation(out=lnsc[:, 4:5], in_=lnsc[:, 3:4],
                                      func=ACT.Ln, bias=EPS)
                    scalar.drain()
                    scalar.activation(out=lnsc[:, 4:5], in_=lnsc[:, 4:5],
                                      func=ACT.Exp, scale=-0.5)
                    scalar.drain()
                    scalar.sem_inc(sv.h, 1)
                    sv.v += 1
                    if sb >= 1:
                        emit_ev(scalar, u - 1)
                emit_ev(scalar, g * 4 + 3)
                for j in range(24):
                    scalar.wait_ge(tv.h, t_unit[(g, j)])
                    if g >= 1:
                        if j == 0:
                            scalar.wait_ge(st.h, st_q[g - 1])
                        elif j == 8:
                            scalar.wait_ge(st.h, st_k[g - 1])
                        elif j >= 16 and (j - 16) % 2 == 0:
                            scalar.wait_ge(st.h, st_v[(g - 1, (j - 16) // 2)])
                    if j < 8:
                        dstc = qs[:, j * 512:(j + 1) * 512]
                    elif j < 16:
                        dstc = ks[:, (j - 8) * 512:(j - 7) * 512]
                    else:
                        ts, hf = (j - 16) // 2, (j - 16) % 2
                        dstc = vs[:, ts * 1024 + hf * 512:
                                  ts * 1024 + (hf + 1) * 512]
                    inst = scalar.activation(out=dstc, in_=ps_mm[j % 4],
                                             func=ACT.Copy)
                    B.inc(inst, sv, 1)
            scalar.wait_ge(tv.h, t_sv[(NGR - 1, 7)])
            inst = scalar.activation(out=svrow[0:1, 0:512], in_=ps_sv[0],
                                     func=ACT.Copy)
            B.inc(inst, sv, 1)
            inst = scalar.activation(out=svrow[0:1, 512:1024], in_=ps_sv[1],
                                     func=ACT.Copy)
            B.inc(inst, sv, 1)

        @decos["vector"]
        def _(vector):
            for u in range(4 * NGR):
                xs_ = xg[:, (u % 2) * D:(u % 2 + 1) * D]
                vector.wait_ge(ld.h, xw(u))
                vector.wait_ge(sv.h, s_sq[u])
                B.ln_tile(vector, None, xs_, None, qtmp, lnsc)
                B.raw_inc(vector, vv, 1)
                vector.wait_ge(sv.h, s_inv[u])
                if u >= 2:
                    vector.wait_ge(tv.h, t_trd[u - 2])
                B.ln_apply(vector, xs_, lnx[:, (u % 2) * D:(u % 2 + 1) * D],
                           lnsc[:, 2:3], lnsc[:, 4:5], lnsc[:, 5:6])
                B.raw_inc(vector, vv, 1)

        @decos["tensor"]
        def _(tensor):
            tensor.wait_ge(ld.h, ld0 + 80)
            for g in range(NGR):
                for sb in range(4):
                    u = g * 4 + sb
                    tensor.wait_ge(vv.h, v_apply[u])
                    for k in range(8):
                        if k >= 2:
                            tensor.wait_ge(sv.h, s_ev[(u, k - 2)])
                        elif u >= 1:
                            tensor.wait_ge(sv.h, s_ev[(u - 1, 6 + k)])
                        inst = tensor.transpose(
                            ps_tr[k % 2],
                            lnx[:, (u % 2) * D + k * 128:
                                (u % 2) * D + (k + 1) * 128], B.ids)
                        B.inc(inst, tv, 1)
                tensor.wait_ge(sv.h, s_ev[(g * 4 + 3, 7)])
                for j in range(24):
                    gp = (g, j - 4)
                    if j < 4:
                        gp = (g - 1, 20 + j) if g >= 1 else None
                    if gp is not None:
                        tensor.wait_ge(sv.h, s_unit[gp])
                    for k in range(8):
                        if j < 8:
                            lhs = wq3[:, k, j * 128:(j + 1) * 128]
                            rhs = lnT[:, k * 512:(k + 1) * 512]
                        elif j < 16:
                            lhs = wk3[:, k, (j - 8) * 128:(j - 7) * 128]
                            rhs = lnT[:, k * 512:(k + 1) * 512]
                        else:
                            ts, hf = (j - 16) // 2, (j - 16) % 2
                            lhs = lnT[:, k * 512 + ts * 128:
                                      k * 512 + (ts + 1) * 128]
                            rhs = wv3[:, k, hf * 512:(hf + 1) * 512]
                        inst = tensor.matmul(ps_mm[j % 4], lhs, rhs,
                                             start=(k == 0), stop=(k == 7))
                        B.inc(inst, tv, 1)
                for idx in range(8):
                    ts, hf = idx // 2, idx % 2
                    tensor.wait_ge(sv.h, s_unit[(g, 16 + idx)])
                    inst = tensor.matmul(
                        ps_sv[hf], B.onesc,
                        vs[:, ts * 1024 + hf * 512:ts * 1024 + (hf + 1) * 512],
                        start=(g == 0 and ts == 0), stop=(g == NGR - 1 and
                                                          ts == 3))
                    B.inc(inst, tv, 1)

        @decos["gpsimd"]
        def _(gpsimd):
            for g in range(NGR):
                t0c = g * 512
                gpsimd.wait_ge(sv.h, s_unit[(g, 7)])
                dstq = B.dram["q_fm"].rearrange("(c p) t -> p c t", p=128)
                inst = gpsimd.dma_start(
                    out=dstq[:, :, t0c:t0c + 512],
                    in_=qs.rearrange("p (c t) -> p c t", c=8))
                B.inc(inst, st, 16)
                gpsimd.wait_ge(st.h, st.v)
                gpsimd.wait_ge(sv.h, s_unit[(g, 15)])
                dstk = B.dram["k_fm"].rearrange("(c p) t -> p c t", p=128)
                inst = gpsimd.dma_start(
                    out=dstk[:, :, t0c:t0c + 512],
                    in_=ks.rearrange("p (c t) -> p c t", c=8))
                B.inc(inst, st, 16)
                gpsimd.wait_ge(st.h, st.v)
                for ts in range(4):
                    gpsimd.wait_ge(sv.h, s_unit[(g, 16 + ts * 2 + 1)])
                    r0 = t0c + ts * 128
                    inst = gpsimd.dma_start(
                        out=B.dram["v_tok"][r0:r0 + 128, :],
                        in_=vs[:, ts * 1024:(ts + 1) * 1024])
                    B.inc(inst, st, 16)
                    gpsimd.wait_ge(st.h, st.v)

        finish()
    B.phase_idx += 1


def _attn_core(B, tl):
    nc = B.nc
    ld, st, tv, vv, sv = B.ld, B.st, B.tv, B.vv, B.sv
    inp = B.inp
    ld0, sv0, vv0, tv0, st0 = ld.v, sv.v, vv.v, tv.v, st.v
    qh_sb, kh_sb, vh_sb, svrow, svrep, msk, e_sb, et_sb, e0T, o_blk, mt = (
        tl["qh"], tl["kh"], tl["vh"], tl["svrow"], tl["svrep"], tl["msk"],
        tl["e_sb"], tl["et_sb"], tl["e0T"], tl["o_blk"], tl["mt"])
    NH, NQ = 16, 32

    def b0_of(qi):
        return min(max((qi - 1) * 128, 0), T - 384)

    def mvar(qi):
        return 0 if qi == 0 else (2 if qi == NQ - 1 else 1)

    def prev_it(h, qi):
        if qi >= 1:
            return (h, qi - 1)
        return (h - 1, NQ - 1) if h >= 1 else None

    t_m, t_band, t_tr, t_av = {}, {}, {}, {}
    tp = tv0 + 2
    for h in range(NH):
        for qi in range(NQ):
            for ch in range(8):
                tp += 1
                t_m[(h, qi, ch)] = tp
            tp += 1
            t_band[(h, qi)] = tp
            for c in range(3):
                tp += 1
                t_tr[(h, qi, c)] = tp
            tp += 3
            t_av[(h, qi)] = tp
    s_e, s_ze, s_trev = {}, {}, {}
    sp = sv0 + 2
    for h in range(NH):
        for qi in range(NQ):
            sp += 1
            s_e[(h, qi)] = sp
            sp += 1
            s_ze[(h, qi)] = sp
            for c in range(3):
                sp += 1
                s_trev[(h, qi, c)] = sp
    v_red, v_mneg, v_m2, v_e0, v_comb = {}, {}, {}, {}, {}
    vp = vv0
    for h in range(NH):
        for qi in range(NQ):
            for ch in range(8):
                vp += 1
                v_red[(h, qi, ch)] = vp
            vp += 1
            v_mneg[(h, qi)] = vp
            vp += 1
            v_m2[(h, qi)] = vp
            vp += 1
            v_e0[(h, qi)] = vp
            vp += 1
            v_comb[(h, qi)] = vp
    st_o = {}
    stp = st0
    for h in range(NH):
        for qi in range(NQ):
            stp += 16
            st_o[(h, qi)] = stp

    def ld_h(h):
        return ld0 + 112 if h < 2 else ld0 + 16 + 48 * (h + 1)

    import contextlib
    with contextlib.ExitStack() as _stk:
        def PP(nm, shp):
            return _stk.enter_context(nc.psum_tensor(nm, shp, F32)).ap()

        ps_m = [PP("acm0", [128, 512]), PP("acm1", [128, 512])]
        ps_b = PP("acb", [128, 384])
        ps_tr = [PP("actr0", [128, 128]), PP("actr1", [128, 128])]
        ps_av = PP("acav", [128, 64])
        ps_rep = PP("acrep", [128, 512])
        block = _stk.enter_context(nc.Block())
        decos, finish = barrier_wrap(B, block, None)
        vh3 = vh_sb.rearrange("p (s c f) -> p s c f", s=2, c=32)

        @decos["sync"]
        def _(sync):
            sync.dma_start(out=msk.rearrange("p (v c) -> p v c", v=3),
                           in_=inp["win_mask"].rearrange("v p c -> p v c")
                           ).then_inc(ld.h, 16)
            ld.v += 16
            for h in range(NH):
                if h >= 2:
                    sync.wait_ge(tv.h, t_av[(h - 2, NQ - 1)])
                sl = h % 2
                sync.dma_start(out=qh_sb[:, sl * T:(sl + 1) * T],
                               in_=B.dram["q_fm"][h * 64:(h + 1) * 64, :]
                               ).then_inc(ld.h, 16)
                sync.dma_start(out=kh_sb[:, sl * T:(sl + 1) * T],
                               in_=B.dram["k_fm"][h * 64:(h + 1) * 64, :]
                               ).then_inc(ld.h, 16)
                sync.dma_start(out=vh3[:, sl],
                               in_=B.dram["v_tok"][:, h * 64:(h + 1) * 64]
                               .rearrange("(c p) f -> p c f", p=128)
                               ).then_inc(ld.h, 16)
                ld.v += 48

        @decos["tensor"]
        def _(tensor):
            inst = tensor.matmul(ps_rep, B.onesr, svrow[0:1, 0:512],
                                 start=True, stop=True)
            B.inc(inst, tv, 1)
            tensor.wait_ge(sv.h, sv0 + 1)
            inst = tensor.matmul(ps_rep, B.onesr, svrow[0:1, 512:1024],
                                 start=True, stop=True)
            B.inc(inst, tv, 1)
            for h in range(NH):
                tensor.wait_ge(ld.h, ld_h(h))
                sl = h % 2
                for qi in range(NQ):
                    pit = prev_it(h, qi)
                    qcol = qh_sb[:, sl * T + qi * 128:sl * T + (qi + 1) * 128]
                    for ch in range(8):
                        if ch >= 2:
                            tensor.wait_ge(vv.h, v_red[(h, qi, ch - 2)])
                        elif pit is not None:
                            tensor.wait_ge(vv.h, v_red[(pit[0], pit[1],
                                                        6 + ch)])
                        inst = tensor.matmul(
                            ps_m[ch % 2], qcol,
                            kh_sb[:, sl * T + ch * 512:sl * T + (ch + 1) * 512],
                            start=True, stop=True)
                        B.inc(inst, tv, 1)
                    if pit is not None:
                        tensor.wait_ge(sv.h, s_e[pit])
                    b0 = b0_of(qi)
                    inst = tensor.matmul(ps_b, qcol,
                                         kh_sb[:, sl * T + b0:sl * T + b0 + 384],
                                         start=True, stop=True)
                    B.inc(inst, tv, 1)
                    for c in range(3):
                        if c == 0:
                            tensor.wait_ge(vv.h, v_e0[(h, qi)])
                        if c >= 2:
                            tensor.wait_ge(sv.h, s_trev[(h, qi, c - 2)])
                        elif pit is not None:
                            tensor.wait_ge(sv.h, s_trev[(pit[0], pit[1],
                                                         c + 1)])
                        inst = tensor.transpose(
                            ps_tr[c % 2], e_sb[:, c * 128:(c + 1) * 128],
                            B.ids)
                        B.inc(inst, tv, 1)
                    for c in range(3):
                        tensor.wait_ge(sv.h, s_trev[(h, qi, c)])
                        if c == 0 and pit is not None:
                            tensor.wait_ge(vv.h, v_comb[pit])
                        bc = b0 // 128 + c
                        inst = tensor.matmul(ps_av,
                                             e0T[:, c * 128:(c + 1) * 128],
                                             vh3[:, sl, bc, :],
                                             start=(c == 0), stop=(c == 2))
                        B.inc(inst, tv, 1)

        @decos["scalar"]
        def _(scalar):
            scalar.wait_ge(tv.h, tv0 + 1)
            inst = scalar.activation(out=svrep[:, 0:512], in_=ps_rep,
                                     func=ACT.Copy)
            B.inc(inst, sv, 1)
            scalar.wait_ge(tv.h, tv0 + 2)
            inst = scalar.activation(out=svrep[:, 512:1024], in_=ps_rep,
                                     func=ACT.Copy)
            B.inc(inst, sv, 1)
            mneg = mt[:, 9:10]
            m2neg = mt[:, 11:12]
            for h in range(NH):
                for qi in range(NQ):
                    scalar.wait_ge(tv.h, t_band[(h, qi)])
                    scalar.wait_ge(vv.h, v_mneg[(h, qi)])
                    inst = scalar.activation(out=e_sb, in_=ps_b, func=ACT.Exp,
                                             scale=0.125, bias=mneg)
                    B.inc(inst, sv, 1)
                    scalar.wait_ge(vv.h, v_m2[(h, qi)])
                    scalar.activation(out=mt[:, 12:13], in_=m2neg,
                                      func=ACT.Exp)
                    scalar.drain()
                    inst = scalar.activation(out=et_sb, in_=e_sb, func=ACT.Exp,
                                             bias=m2neg)
                    B.inc(inst, sv, 1)
                    for c in range(3):
                        scalar.wait_ge(tv.h, t_tr[(h, qi, c)])
                        inst = scalar.activation(
                            out=e0T[:, c * 128:(c + 1) * 128],
                            in_=ps_tr[c % 2], func=ACT.Copy)
                        B.inc(inst, sv, 1)

        @decos["vector"]
        def _(vector):
            vector.wait_ge(ld.h, ld0 + 112)
            mtmp = mt[:, 0:8]
            mraw = mt[:, 8:9]
            mneg = mt[:, 9:10]
            m2 = mt[:, 10:11]
            m2neg = mt[:, 11:12]
            z = mt[:, 12:13]
            den = mt[:, 13:14]
            den2 = mt[:, 14:15]
            rcp = mt[:, 15:16]
            for h in range(NH):
                for qi in range(NQ):
                    for ch in range(8):
                        vector.wait_ge(tv.h, t_m[(h, qi, ch)])
                        B.red(vector, mtmp[:, ch:ch + 1], ps_m[ch % 2],
                              ALU.max)
                        B.raw_inc(vector, vv, 1)
                    B.red(vector, mraw, mtmp, ALU.max)
                    vector.tensor_single_scalar(out=mneg, in_=mraw,
                                                scalar=-0.125, op=ALU.mult)
                    B.raw_inc(vector, vv, 1)
                    vector.wait_ge(sv.h, s_e[(h, qi)])
                    mv = mvar(qi)
                    vector.tensor_tensor(out=e_sb, in0=e_sb,
                                         in1=msk[:, mv * 384:(mv + 1) * 384],
                                         op=ALU.mult)
                    B.red(vector, m2, e_sb, ALU.max)
                    vector.tensor_single_scalar(out=m2neg, in_=m2,
                                                scalar=-1.0, op=ALU.mult)
                    B.raw_inc(vector, vv, 1)
                    vector.wait_ge(sv.h, s_ze[(h, qi)])
                    vector.tensor_scalar(out=e_sb, in0=et_sb, scalar1=z,
                                         scalar2=None, op0=ALU.subtract)
                    B.red(vector, den, e_sb, ALU.add)
                    vector.scalar_tensor_tensor(out=den2, in0=z,
                                                scalar=float(T), in1=den,
                                                op0=ALU.mult, op1=ALU.add)
                    B.recip(vector, rcp, den2)
                    B.raw_inc(vector, vv, 1)
                    vector.wait_ge(tv.h, t_av[(h, qi)])
                    pit2 = (h, qi - 2)
                    if qi < 2:
                        pit2 = (h - 1, NQ - 2 + qi) if h >= 1 else None
                    if pit2 is not None:
                        vector.wait_ge(st.h, st_o[pit2])
                    ob = o_blk[:, (qi % 2) * 64:(qi % 2 + 1) * 64]
                    vector.scalar_tensor_tensor(
                        out=ob, in0=svrep[:, h * 64:(h + 1) * 64], scalar=z,
                        in1=ps_av, op0=ALU.mult, op1=ALU.add)
                    vector.drain()
                    vector.tensor_scalar(out=ob, in0=ob, scalar1=rcp,
                                         scalar2=None, op0=ALU.mult)
                    B.raw_inc(vector, vv, 1)

        @decos["gpsimd"]
        def _(gpsimd):
            for h in range(NH):
                for qi in range(NQ):
                    gpsimd.wait_ge(vv.h, v_comb[(h, qi)])
                    inst = gpsimd.dma_start(
                        out=B.dram["o_tok"][qi * 128:(qi + 1) * 128,
                                            h * 64:(h + 1) * 64],
                        in_=o_blk[:, (qi % 2) * 64:(qi % 2 + 1) * 64])
                    B.inc(inst, st, 16)
                    gpsimd.wait_ge(st.h, st.v)

        finish()
    B.phase_idx += 1


def _attn_wo(B, tl):
    nc = B.nc
    ld, st, tv, vv, sv = B.ld, B.st, B.tv, B.vv, B.sv
    inp = B.inp
    ld0, sv0, vv0, tv0, st0 = ld.v, sv.v, vv.v, tv.v, st.v
    wo_sb, og, xg, oT, outf = (tl["wo"], tl["og"], tl["xg2"], tl["oT"],
                               tl["outf"])
    NQ = 32

    t_tr, t_half = {}, {}
    tp = tv0
    for qi in range(NQ):
        for k in range(8):
            tp += 1
            t_tr[(qi, k)] = tp
        for hf in range(2):
            tp += 8
            t_half[(qi, hf)] = tp
    s_ev = {}
    sp = sv0
    for qi in range(NQ):
        for k in range(8):
            sp += 1
            s_ev[(qi, k)] = sp
    v_fin = {}
    vp = vv0
    for qi in range(NQ):
        vp += 1
        v_fin[qi] = vp
    st_c = {}
    stp = st0
    for qi in range(NQ):
        stp += 16
        st_c[qi] = stp

    def ld_qi(qi):
        return ld0 + 80 if qi < 2 else ld0 + 16 + 32 * (qi + 1)

    import contextlib
    with contextlib.ExitStack() as _stk:
        def PP(nm, shp):
            return _stk.enter_context(nc.psum_tensor(nm, shp, F32)).ap()

        ps_tr = [PP("awtr0", [128, 128]), PP("awtr1", [128, 128])]
        ps_wo = [PP("awm0", [128, 512]), PP("awm1", [128, 512])]
        block = _stk.enter_context(nc.Block())
        decos, finish = barrier_wrap(B, block, None)
        wo3 = wo_sb.rearrange("p (k o) -> p k o", k=8)

        @decos["sync"]
        def _(sync):
            sync.dma_start(out=wo3, in_=inp["wo"].rearrange(
                "(k p) o -> p k o", p=128)).then_inc(ld.h, 16)
            ld.v += 16
            for qi in range(NQ):
                if qi >= 2:
                    sync.wait_ge(vv.h, v_fin[qi - 2])
                r0 = qi * 128
                sync.dma_start(out=og[:, (qi % 2) * D:(qi % 2 + 1) * D],
                               in_=B.dram["o_tok"][r0:r0 + 128, :]
                               ).then_inc(ld.h, 16)
                sync.dma_start(out=xg[:, (qi % 2) * D:(qi % 2 + 1) * D],
                               in_=B.dram["xres"][r0:r0 + 128, :]
                               ).then_inc(ld.h, 16)
                ld.v += 32

        @decos["tensor"]
        def _(tensor):
            for qi in range(NQ):
                tensor.wait_ge(ld.h, ld_qi(qi))
                for k in range(8):
                    if k >= 2:
                        tensor.wait_ge(sv.h, s_ev[(qi, k - 2)])
                    elif qi >= 1:
                        tensor.wait_ge(sv.h, s_ev[(qi - 1, 6 + k)])
                    inst = tensor.transpose(
                        ps_tr[k % 2],
                        og[:, (qi % 2) * D + k * 128:(qi % 2) * D +
                           (k + 1) * 128], B.ids)
                    B.inc(inst, tv, 1)
                for hf in range(2):
                    if hf == 0 and qi >= 1:
                        tensor.wait_ge(vv.h, v_fin[qi - 1])
                    for k in range(8):
                        if hf == 0:
                            tensor.wait_ge(sv.h, s_ev[(qi, k)])
                        inst = tensor.matmul(
                            ps_wo[hf], oT[:, k * 128:(k + 1) * 128],
                            wo3[:, k, hf * 512:(hf + 1) * 512],
                            start=(k == 0), stop=(k == 7))
                        B.inc(inst, tv, 1)

        @decos["scalar"]
        def _(scalar):
            for qi in range(NQ):
                for k in range(8):
                    scalar.wait_ge(tv.h, t_tr[(qi, k)])
                    inst = scalar.activation(out=oT[:, k * 128:(k + 1) * 128],
                                             in_=ps_tr[k % 2], func=ACT.Copy)
                    B.inc(inst, sv, 1)

        @decos["vector"]
        def _(vector):
            for qi in range(NQ):
                vector.wait_ge(ld.h, ld_qi(qi))
                if qi >= 2:
                    vector.wait_ge(st.h, st_c[qi - 2])
                xs_ = xg[:, (qi % 2) * D:(qi % 2 + 1) * D]
                of = outf[:, (qi % 2) * D:(qi % 2 + 1) * D]
                vector.wait_ge(tv.h, t_half[(qi, 0)])
                vector.tensor_tensor(out=of[:, 0:512], in0=ps_wo[0],
                                     in1=xs_[:, 0:512], op=ALU.add)
                vector.wait_ge(tv.h, t_half[(qi, 1)])
                vector.tensor_tensor(out=of[:, 512:1024], in0=ps_wo[1],
                                     in1=xs_[:, 512:1024], op=ALU.add)
                B.raw_inc(vector, vv, 1)

        @decos["gpsimd"]
        def _(gpsimd):
            for qi in range(NQ):
                gpsimd.wait_ge(vv.h, v_fin[qi])
                r0 = qi * 128
                inst = gpsimd.dma_start(
                    out=B.dram["xres"][r0:r0 + 128, :],
                    in_=outf[:, (qi % 2) * D:(qi % 2 + 1) * D])
                B.inc(inst, st, 16)
                gpsimd.wait_ge(st.h, st.v)

        finish()
    B.phase_idx += 1


def phase_attn(B):
    nc = B.nc
    import contextlib
    svrow = nc.alloc_sbuf_tensor("at_svrow", [1, D], F32).ap()
    with contextlib.ExitStack() as _stk:
        def A(nm, shp, pt=128):
            return _stk.enter_context(
                nc.sbuf_tensor(nm, [pt] + shp, F32)).ap()

        tl = {
            "wq": A("at_wq", [8 * D]),
            "wk": A("at_wk", [8 * D]),
            "wv": A("at_wv", [8 * D]),
            "lnT": A("at_lnT", [8 * 512]),
            "qs": A("at_qs", [8 * 512]),
            "ks": A("at_ks", [8 * 512]),
            "vs": A("at_vs", [4 * D]),
            "xg": A("at_x", [2 * D]),
            "lnx": A("at_ln", [2 * D]),
            "qtmp": A("at_sq", [D]),
            "lnsc": A("at_st", [8]),
            "svrow": svrow,
        }
        _attn_qkv(B, tl)
    with contextlib.ExitStack() as _stk:
        def A(nm, shp, pt=128):
            return _stk.enter_context(
                nc.sbuf_tensor(nm, [pt] + shp, F32)).ap()

        tl2 = {
            "qh": A("at_qh", [2 * T], pt=64),
            "kh": A("at_kh", [2 * T], pt=64),
            "vh": A("at_vh", [2 * 32 * 64]),
            "svrow": svrow,
            "svrep": A("at_svrep", [D]),
            "msk": A("at_msk", [3 * 384]),
            "e_sb": A("at_e", [384]),
            "et_sb": A("at_et", [384]),
            "e0T": A("at_e0T", [384]),
            "o_blk": A("at_ob", [2 * 64]),
            "mt": A("at_mt", [16]),
        }
        _attn_core(B, tl2)
    with contextlib.ExitStack() as _stk:
        def A(nm, shp, pt=128):
            return _stk.enter_context(
                nc.sbuf_tensor(nm, [pt] + shp, F32)).ap()

        tl3 = {
            "wo": A("at_wo", [8 * D]),
            "og": A("at_og", [2 * D]),
            "xg2": A("at_x2", [2 * D]),
            "oT": A("at_oT", [D]),
            "outf": A("at_of", [2 * D]),
        }
        _attn_wo(B, tl3)


def phase_copy_out(B):
    nc = B.nc
    ld, st, vv = B.ld, B.st, B.vv
    ld0, st0 = B.ld.v, B.st.v
    NT = T // 128
    with (
        nc.sbuf_tensor("pc_t", [128, 2 * D], F32) as tile,
        nc.Block() as block,
    ):
        tile = tile.ap()
        decos, finish = barrier_wrap(B, block, None)

        @decos["sync"]
        def _(sync):
            for i in range(NT):
                if i >= 2:
                    sync.wait_ge(st.h, st0 + 16 * (i - 1))
                inst = sync.dma_start(out=tile[:, (i % 2) * D:(i % 2 + 1) * D],
                                      in_=B.dram["xres"][i * 128:(i + 1) * 128, :])
                B.inc(inst, ld, 16)
                sync.wait_ge(ld.h, ld.v)

        @decos["gpsimd"]
        def _(gpsimd):
            for i in range(NT):
                gpsimd.wait_ge(ld.h, ld0 + 16 * (i + 1))
                inst = gpsimd.dma_start(out=B.dram["out_x"][i * 128:(i + 1) * 128, :],
                                        in_=tile[:, (i % 2) * D:(i % 2 + 1) * D])
                B.inc(inst, st, 16)
                gpsimd.wait_ge(st.h, st.v)

        finish()
    B.phase_idx += 1


def host_layout(inputs):
    L = {}
    L["w_in_t"] = np.ascontiguousarray(inputs["L0_attn_W_in"].T)
    L["conv_w_l"] = np.ascontiguousarray(
        inputs["L0_attn_conv_w"][:, 0, :].reshape(NS, 128, 4).transpose(1, 0, 2))
    L["conv_b_l"] = np.ascontiguousarray(inputs["L0_attn_conv_b"].reshape(NS, 128).T)
    L["w_x_t"] = np.ascontiguousarray(inputs["L0_attn_W_x"].T)
    L["wdt_l"] = np.ascontiguousarray(inputs["L0_attn_W_dt"][:, 0].reshape(NS, 128).T)
    L["bdt_l"] = np.ascontiguousarray(inputs["L0_attn_b_dt"].reshape(NS, 128).T)
    L["alog_c"] = np.ascontiguousarray(np.tile(inputs["L0_attn_A_log"], 8)[:, None])
    L["d_col"] = np.ascontiguousarray(inputs["L0_attn_D_param"].reshape(NS, 128).T)
    L["lng_col"] = np.ascontiguousarray(inputs["L0_attn_ln_g"].reshape(NS, 128).T)
    L["lnb_col"] = np.ascontiguousarray(inputs["L0_attn_ln_b"].reshape(NS, 128).T)
    L["w_out_t"] = np.ascontiguousarray(inputs["L0_attn_W_out"].T)
    L["ffn_w"] = np.ascontiguousarray(inputs["L0_ffn_weight"])
    L["ffn_b_col"] = np.ascontiguousarray(inputs["L0_ffn_bias"].reshape(8, 128).T)
    L["wq"] = np.ascontiguousarray(inputs["L1_attn_Wq"])
    L["wk"] = np.ascontiguousarray(inputs["L1_attn_Wk"])
    L["wv"] = np.ascontiguousarray(inputs["L1_attn_Wv"])
    L["wo"] = np.ascontiguousarray(inputs["L1_attn_Wo"])
    L["rw_t"] = np.ascontiguousarray(inputs["L1_ffn_router_w"].T)
    L["cb_rep"] = np.ascontiguousarray(
        np.tile(inputs["L1_ffn_correction_bias"][None, :], (128, 1)))
    L["gate_w"] = np.ascontiguousarray(inputs["L1_ffn_gate_w"])
    L["up_w"] = np.ascontiguousarray(inputs["L1_ffn_up_w"])
    L["down_w"] = np.ascontiguousarray(inputs["L1_ffn_down_w"])
    L["sgw_t"] = np.ascontiguousarray(inputs["L1_ffn_shared_gate_w"].T)
    L["suw_t"] = np.ascontiguousarray(inputs["L1_ffn_shared_up_w"].T)
    L["sdw_t"] = np.ascontiguousarray(inputs["L1_ffn_shared_down_w"].T)
    L["ident"] = np.eye(128, dtype=np.float32)
    L["ones_row"] = np.ones((1, 128), dtype=np.float32)
    L["ones_col"] = np.ones((128, 1), dtype=np.float32)
    first = np.zeros((128, 384), np.float32)
    for p in range(128):
        first[p, 0:p + 128] = 1.0
    mid = np.zeros((128, 384), np.float32)
    for p in range(128):
        mid[p, p:p + 256] = 1.0
    last = np.zeros((128, 384), np.float32)
    for p in range(128):
        last[p, p + 128:384] = 1.0
    L["win_mask"] = np.ascontiguousarray(np.stack([first, mid, last]))
    return L


_N_BUILD = [6]
_SKIP_QUANT = [False]
_SKIP_LN = [False]


def kernel(**inputs):
    L = host_layout(inputs)
    x = np.ascontiguousarray(inputs["L0_attn_x"], dtype=np.float32)
    nc, _ = build_program(_N_BUILD[0])
    in_maps = []
    for b in range(8):
        m = dict(L)
        m["x_in"] = np.ascontiguousarray(x[b])
        in_maps.append(m)
    res = run_bass_kernel_spmd(nc, in_maps, core_ids=list(range(8)))
    return np.stack([r["out_x"] for r in res.results]).astype(np.float32)
